# version 06
# baseline (speedup 1.0000x reference)
import numpy as np
from contextlib import ExitStack, contextmanager

import concourse.bass as bass
import concourse.mybir as mybir
from concourse.bass_utils import run_bass_kernel_spmd

F32 = mybir.dt.float32
I32 = mybir.dt.int32
U32 = mybir.dt.uint32
BF16 = mybir.dt.bfloat16
AF = mybir.ActivationFunctionType
ALU = mybir.AluOpType

ENGINES = ("sync", "tensor", "vector", "scalar", "gpsimd")

NTOK = 4096
DM = 1024


class Sem:
    __slots__ = ("h", "n", "name")

    def __init__(self, kb, name):
        self.h = kb.es.enter_context(kb.nc.semaphore(name))
        self.n = 0
        self.name = name


class TT:
    __slots__ = ("ap", "w", "rs", "name")

    def __init__(self, ap, name=""):
        self.ap = ap
        self.w = None
        self.rs = []
        self.name = name

    def __getitem__(self, idx):
        return self.ap[idx]


class KB:
    def __init__(self, nc):
        self.nc = nc
        self.es = ExitStack()
        self.prog = {e: [] for e in ENGINES}
        self.esem = {e: Sem(self, f"es_{e}") for e in ENGINES}
        NDS = 8
        self.dsem = {e: [Sem(self, f"ds_{e}{i}") for i in range(NDS)]
                     for e in ("sync", "scalar", "gpsimd", "vector")}
        self.drr = {e: 0 for e in ("sync", "scalar", "gpsimd", "vector")}
        self.waited = {e: {} for e in ENGINES}
        self._name_i = 0
        self.cst_ones = self.sb([128, 128], name="cst_ones")
        self.op("vector", lambda e: e.memset(self.cst_ones.ap[:, :], 1.0),
                outs=(self.cst_ones,))
        self.ident = self.sb([128, 128], name="ident128")
        self.op("gpsimd", lambda e: e.affine_select(
            out=self.ident.ap[:, :], in_=self.cst_ones.ap[:, :], pattern=[[-1, 128]],
            compare_op=ALU.is_equal, fill=0.0, base=0, channel_multiplier=1),
            ins=(self.cst_ones,), outs=(self.ident,))
        self.barrier()

    def name(self, p="t"):
        self._name_i += 1
        return f"{p}{self._name_i}"

    def sb(self, shape, dt=F32, name=None):
        h = self.es.enter_context(self.nc.sbuf_tensor(name or self.name("sb"), shape, dt))
        return TT(h, name or "sb")

    def dram(self, shape, dt=F32, name=None, kind="Internal"):
        h = self.nc.dram_tensor(name or self.name("dr"), shape, dt, kind=kind)
        return TT(h.ap(), name or "dr")

    @contextmanager
    def scope(self):
        es = ExitStack()
        old, self.es = self.es, es
        try:
            yield
        finally:
            self.barrier()
            self.es = old
            es.close()

    def psum(self, shape, dt=F32, name=None):
        h = self.es.enter_context(self.nc.psum_tensor(name or self.name("ps"), shape, dt))
        return TT(h, name or "ps")

    def _wait(self, eng, tok):
        sem, val = tok
        if self.waited[eng].get(sem, 0) >= val:
            return
        self.waited[eng][sem] = val
        self.prog[eng].append(lambda e, s=sem.h, v=val: e.wait_ge(s, v))

    def op(self, eng, fn, ins=(), outs=(), dma=False, extra=()):
        deps = []
        for t in ins:
            if t is not None and t.w is not None:
                deps.append(t.w)
        for t in outs:
            if t is not None:
                if t.w is not None:
                    deps.append(t.w)
                deps.extend(t.rs)
        deps.extend(extra)
        if dma:
            pool = self.dsem[eng]
            sem = pool[self.drr[eng] % len(pool)]
            self.drr[eng] += 1
            if sem.n:
                deps.append((sem, sem.n))
            inc = 16
        else:
            sem = self.esem[eng]
            inc = 1
        for d in deps:
            self._wait(eng, d)
        sem.n += inc
        tok = (sem, sem.n)
        self.prog[eng].append(lambda e, f=fn, s=sem.h, i=inc: f(e).then_inc(s, i))
        for t in outs:
            if t is not None:
                t.w = tok
                t.rs = []
        for t in ins:
            if t is not None:
                t.rs.append(tok)
        return tok

    def mm_group(self, fns, ins=(), outs=(), extra=()):
        deps = []
        for t in ins:
            if t is not None and t.w is not None:
                deps.append(t.w)
        for t in outs:
            if t is not None:
                if t.w is not None:
                    deps.append(t.w)
                deps.extend(t.rs)
        deps.extend(extra)
        for d in deps:
            self._wait("tensor", d)
        sem = self.esem["tensor"]
        for j, fn in enumerate(fns):
            sem.n += 1
            if j == len(fns) - 1:
                self.prog["tensor"].append(
                    lambda e, f=fn, s=sem.h, v=1: f(e).then_inc(s, v))
            else:
                self.prog["tensor"].append(
                    lambda e, f=fn, s=sem.h, v=1: f(e).then_inc(s, v))
        tok = (sem, sem.n)
        for t in outs:
            if t is not None:
                t.w = tok
                t.rs = []
        for t in ins:
            if t is not None:
                t.rs.append(tok)
        return tok

    def barrier(self):
        toks = [(s, s.n) for s in self.esem.values()]
        for pool in self.dsem.values():
            toks += [(s, s.n) for s in pool]
        for eng in ENGINES:
            for t in toks:
                self._wait(eng, t)

    def finalize(self):
        allsems = [s for pool in self.dsem.values() for s in pool]
        allsems += [self.esem[e] for e in ENGINES if e != "sync"]
        for s in allsems:
            if s.n:
                self._wait("sync", (s, s.n))
        with self.nc.Block() as block:
            block.sync(lambda e: [f(e) for f in self.prog["sync"]])
            block.tensor(lambda e: [f(e) for f in self.prog["tensor"]])
            block.vector(lambda e: [f(e) for f in self.prog["vector"]])
            block.scalar(lambda e: [f(e) for f in self.prog["scalar"]])
            block.gpsimd(lambda e: [f(e) for f in self.prog["gpsimd"]])
        self.es.close()

    def dma(self, eng, dst, dst_idx, src, src_idx, ins=(), outs=()):
        return self.op(eng,
                       lambda e: e.dma_start(out=dst.ap[dst_idx], in_=src.ap[src_idx]),
                       ins=(src,) + tuple(ins), outs=(dst,) + tuple(outs), dma=True)

    def dma_ap(self, eng, dst_tt, dst_ap, src_tt, src_ap):
        return self.op(eng, lambda e: e.dma_start(out=dst_ap, in_=src_ap),
                       ins=(src_tt,), outs=(dst_tt,), dma=True)


def ceil_div(a, b):
    return (a + b - 1) // b


CST_COLS = 8


def host_consts():
    c = np.zeros((128, CST_COLS), np.float32)
    c[:, 0] = 1.0
    return c



def emit_quant_w(kb, wT_dram, n_in, n_out, stash_dram, nel):
    nr = n_in // 128
    with kb.scope():
        wt = [kb.sb([128, n_out]) for _ in range(2)]
        acc = kb.sb([128, 1])
        part = kb.sb([128, 1])
        tot_ps = kb.psum([1, 1])
        tot = kb.sb([1, 1])
        thr = kb.sb([128, 1])
        ones = kb.cst_ones
        sgn = kb.sb([128, n_out])
        msk = kb.sb([128, n_out])
        kb.op("vector", lambda e: e.memset(acc.ap[:, :], 0.0), outs=(acc,))
        for r in range(nr):
            w = wt[r % 2]
            kb.dma("sync", w, np.s_[:, :], wT_dram, np.s_[r * 128:(r + 1) * 128, :])
            kb.op("vector",
                  lambda e, w=w: e.reduce_sum(part.ap[:, :], w.ap[:, :],
                                              axis=mybir.AxisListType.X,
                                              apply_absolute_value=True),
                  ins=(w,), outs=(part,))
            kb.op("vector", lambda e: e.tensor_add(acc.ap[:, :], acc.ap[:, :], part.ap[:, :]),
                  ins=(part, acc), outs=(acc,))
        kb.op("tensor", lambda e: e.matmul(tot_ps.ap[:, :], acc.ap[:, :], ones.ap[:, 0:1],
                                           start=True, stop=True),
              ins=(acc, ones), outs=(tot_ps,))
        kb.op("vector", lambda e: e.tensor_scalar(
            tot.ap[:, :], tot_ps.ap[:, :], 1.0 / nel, 1e-6, op0=ALU.mult, op1=ALU.add),
            ins=(tot_ps,), outs=(tot,))
        kb.op("vector", lambda e: e.tensor_scalar_mul(tot.ap[:, :], tot.ap[:, :], 0.5),
              ins=(tot,), outs=(tot,))
        thr_ps = kb.psum([128, 1])
        kb.op("tensor", lambda e: e.matmul(thr_ps.ap[:, :], ones.ap[0:1, 0:128], tot.ap[:, :],
                                           start=True, stop=True),
              ins=(tot, ones), outs=(thr_ps,))
        kb.op("vector", lambda e: e.tensor_copy(thr.ap[:, :], thr_ps.ap[:, :]),
              ins=(thr_ps,), outs=(thr,))
        for r in range(nr):
            w = wt[r % 2]
            kb.dma("sync", w, np.s_[:, :], wT_dram, np.s_[r * 128:(r + 1) * 128, :])
            kb.op("scalar", lambda e, w=w: e.activation(out=sgn.ap[:, :], in_=w.ap[:, :],
                                                        func=AF.Sign),
                  ins=(w,), outs=(sgn,))
            kb.op("scalar", lambda e, w=w: e.activation(out=msk.ap[:, :], in_=w.ap[:, :],
                                                        func=AF.Abs),
                  ins=(w,), outs=(msk,))
            kb.op("vector", lambda e: e.tensor_single_scalar(
                out=msk.ap[:, :], in_=msk.ap[:, :], scalar=thr.ap[:, :], op=ALU.is_gt),
                ins=(msk, thr), outs=(msk,))
            kb.op("vector", lambda e, w=w: e.tensor_mul(w.ap[:, :], sgn.ap[:, :], msk.ap[:, :]),
                  ins=(sgn, msk), outs=(w,))
            kb.dma("sync", stash_dram, np.s_[r * 128:(r + 1) * 128, :], w, np.s_[:, :])


def emit_ln_fm(kb, xg, xhat, ntok, dim, pbank, gam_col=None, bet_col=None, eps=1e-5):
    nch = len(xg)
    ones = kb.cst_ones
    sx_ps, sxx_ps, bc_ps = pbank["sx"], pbank["sxx"], pbank["bc"]
    sq = pbank["sq"]
    rows = pbank["rows"]
    sx, sxx, arow, brow = rows
    kb.mm_group(
        [(lambda e, c=c: e.matmul(sx_ps.ap[:, :], ones.ap[:, 0:1], xg[c].ap[:, :],
                                  start=(c == 0), stop=(c == nch - 1))) for c in range(nch)],
        ins=tuple(xg) + (ones,), outs=(sx_ps,))
    for c in range(nch):
        s = sq[c % 2]
        kb.op("scalar", lambda e, c=c, s=s: e.activation(out=s.ap[:, :], in_=xg[c].ap[:, :],
                                                         func=AF.Square),
              ins=(xg[c],), outs=(s,))
        kb.mm_group(
            [lambda e, c=c, s=s: e.matmul(sxx_ps.ap[:, :], ones.ap[:, 0:1], s.ap[:, :],
                                          start=(c == 0), stop=(c == nch - 1))],
            ins=(s, ones), outs=(sxx_ps,) if c == nch - 1 else (sxx_ps,))
    kb.op("vector", lambda e: e.tensor_scalar_mul(sx.ap[:, :], sx_ps.ap[:, :], 1.0 / dim),
          ins=(sx_ps,), outs=(sx,))
    kb.op("vector", lambda e: e.tensor_scalar_mul(sxx.ap[:, :], sxx_ps.ap[:, :], 1.0 / dim),
          ins=(sxx_ps,), outs=(sxx,))
    kb.op("vector", lambda e: e.scalar_tensor_tensor(
        out=arow.ap[:, :], in0=sx.ap[:, :], scalar=-1.0, in1=sx.ap[:, :],
        op0=ALU.mult, op1=ALU.mult),
        ins=(sx,), outs=(arow,))
    kb.op("vector", lambda e: e.tensor_add(arow.ap[:, :], arow.ap[:, :], sxx.ap[:, :]),
          ins=(arow, sxx), outs=(arow,))
    kb.op("vector", lambda e: e.tensor_scalar_add(arow.ap[:, :], arow.ap[:, :], eps),
          ins=(arow,), outs=(arow,))
    kb.op("scalar", lambda e: e.activation(out=arow.ap[:, :], in_=arow.ap[:, :], func=AF.Sqrt),
          ins=(arow,), outs=(arow,))
    kb.op("vector", lambda e: e.reciprocal(arow.ap[:, :], arow.ap[:, :]),
          ins=(arow,), outs=(arow,))
    kb.op("vector", lambda e: e.scalar_tensor_tensor(
        out=brow.ap[:, :], in0=sx.ap[:, :], scalar=-1.0, in1=arow.ap[:, :],
        op0=ALU.mult, op1=ALU.mult),
        ins=(sx, arow), outs=(brow,))
    abc = pbank["abc"]
    for row, dstt in ((arow, abc[0]), (brow, abc[1])):
        kb.op("tensor", lambda e, row=row: e.matmul(
            bc_ps.ap[:, :], ones.ap[0:1, 0:128], row.ap[:, :], start=True, stop=True),
            ins=(row, ones), outs=(bc_ps,))
        kb.op("vector", lambda e, d=dstt: e.tensor_copy(d.ap[:, :], bc_ps.ap[:, :]),
              ins=(bc_ps,), outs=(dstt,))
    for c in range(nch):
        kb.op("vector", lambda e, c=c: e.tensor_mul(
            xhat[c].ap[:, :], xg[c].ap[:, :], abc[0].ap[:, :]),
            ins=(xg[c], abc[0]), outs=(xhat[c],))
        kb.op("vector", lambda e, c=c: e.tensor_add(
            xhat[c].ap[:, :], xhat[c].ap[:, :], abc[1].ap[:, :]),
            ins=(xhat[c], abc[1]), outs=(xhat[c],))
        if gam_col is not None:
            g, gc = gam_col
            b, bc_ = bet_col
            kb.op("vector", lambda e, c=c, g=g, gc=gc: e.tensor_single_scalar(
                out=xhat[c].ap[:, :], in_=xhat[c].ap[:, :],
                scalar=g.ap[:, gc + c:gc + c + 1], op=ALU.mult),
                ins=(xhat[c], g), outs=(xhat[c],))
            kb.op("vector", lambda e, c=c, b=b, bc_=bc_: e.tensor_single_scalar(
                out=xhat[c].ap[:, :], in_=xhat[c].ap[:, :],
                scalar=b.ap[:, bc_ + c:bc_ + c + 1], op=ALU.add),
                ins=(xhat[c], b), outs=(xhat[c],))


def emit_ffn0(kb, x_in, x_out, wfcq, wprojq, lnwb, ngroups=8, gtok=512):
    NH = 4096
    with kb.scope():
        xg = [kb.sb([128, gtok]) for _ in range(8)]
        xh = [kb.sb([128, gtok]) for _ in range(8)]
        hh = [kb.sb([128, gtok]) for _ in range(32)]
        sq = [kb.sb([128, gtok]) for _ in range(2)]
        rows = [kb.sb([1, gtok]) for _ in range(4)]
        abc = [kb.sb([128, gtok]) for _ in range(2)]
        wl = [kb.sb([128, 1024]) for _ in range(2)]
        wl2 = [kb.sb([128, 4096]) for _ in range(2)]
        ob = kb.sb([128, gtok])
        sx_ps = kb.psum([1, gtok])
        sxx_ps = kb.psum([1, gtok])
        bc_ps = kb.psum([128, gtok])
        h_ps = kb.psum([128, gtok])
        o_ps = kb.psum([128, gtok])
        pbank = {"sx": sx_ps, "sxx": sxx_ps, "bc": bc_ps, "sq": sq, "rows": rows, "abc": abc}
        for g in range(ngroups):
            t0 = g * gtok
            for c in range(8):
                kb.dma("sync", xg[c], np.s_[:, :], x_in,
                       np.s_[c * 128:(c + 1) * 128, t0:t0 + gtok])
            emit_ln_fm(kb, xg, xh, gtok, DM, pbank)
            for hc in range(32):
                w = wl[hc % 2]
                src = wfcq.ap[:, hc * 128:(hc + 1) * 128].rearrange(
                    "(a p) h -> p a h", p=128)
                dst = w.ap[:, :].rearrange("p (a h) -> p a h", a=8)
                kb.dma_ap("sync", w, dst, wfcq, src)
                kb.mm_group(
                    [(lambda e, dc=dc, w=w: e.matmul(
                        h_ps.ap[:, :], w.ap[:, dc * 128:(dc + 1) * 128], xh[dc].ap[:, :],
                        start=(dc == 0), stop=(dc == 7))) for dc in range(8)],
                    ins=tuple(xh) + (w,), outs=(h_ps,))
                kb.op("scalar", lambda e, hc=hc: e.activation(
                    out=hh[hc].ap[:, :], in_=h_ps.ap[:, :], func=AF.Gelu),
                    ins=(h_ps,), outs=(hh[hc],))
            emit_ln_fm(kb, hh, hh, gtok, NH, pbank, gam_col=(lnwb, 0), bet_col=(lnwb, 32))
            for oc in range(8):
                w = wl2[oc % 2]
                src = wprojq.ap[:, oc * 128:(oc + 1) * 128].rearrange(
                    "(a p) h -> p a h", p=128)
                dst = w.ap[:, :].rearrange("p (a h) -> p a h", a=32)
                kb.dma_ap("sync", w, dst, wprojq, src)
                kb.mm_group(
                    [(lambda e, hc=hc, w=w: e.matmul(
                        o_ps.ap[:, :], w.ap[:, hc * 128:(hc + 1) * 128], hh[hc].ap[:, :],
                        start=(hc == 0), stop=(hc == 31))) for hc in range(32)],
                    ins=tuple(hh) + (w,), outs=(o_ps,))
                kb.op("vector", lambda e, oc=oc: e.scalar_tensor_tensor(
                    out=ob.ap[:, :], in0=o_ps.ap[:, :], scalar=0.5, in1=xg[oc].ap[:, :],
                    op0=ALU.mult, op1=ALU.add),
                    ins=(o_ps, xg[oc]), outs=(ob,))
                kb.dma("sync", x_out, np.s_[oc * 128:(oc + 1) * 128, t0:t0 + gtok],
                       ob, np.s_[:, :])



def emit_attn0_heads(kb, xfull, yout, wqkvT, abcols, T, s_in, s_out, ch0, nch,
                     skip_ln=False):
    L = 128
    with kb.scope():
        Mlo = kb.sb([128, 128]); Mup = kb.sb([128, 128])
        Ione = kb.sb([128, 128]); Itwo = kb.sb([128, 128])
        ones = kb.cst_ones
        kb.op("gpsimd", lambda e: e.affine_select(
            out=Mlo.ap[:, :], in_=ones.ap[:, :], pattern=[[-1, 128]],
            compare_op=ALU.is_gt, fill=0.0, base=0, channel_multiplier=1),
            ins=(ones,), outs=(Mlo,))
        kb.op("gpsimd", lambda e: e.affine_select(
            out=Mup.ap[:, :], in_=ones.ap[:, :], pattern=[[1, 128]],
            compare_op=ALU.is_gt, fill=0.0, base=0, channel_multiplier=-1),
            ins=(ones,), outs=(Mup,))
        kb.op("gpsimd", lambda e: e.affine_select(
            out=Ione.ap[:, :], in_=ones.ap[:, :], pattern=[[-1, 128]],
            compare_op=ALU.is_equal, fill=0.0, base=0, channel_multiplier=1),
            ins=(ones,), outs=(Ione,))
        kb.op("vector", lambda e: e.tensor_scalar_mul(Itwo.ap[:, :], Ione.ap[:, :], 2.0),
              ins=(Ione,), outs=(Itwo,))
        S = [kb.sb([64, 64], name=f"S{h}") for h in range(4)]
        for h in range(4):
            kb.dma("sync", S[h], np.s_[:, :], s_in, np.s_[h * 64:(h + 1) * 64, :])
        wq = kb.sb([128, 8 * 256]); wk = kb.sb([128, 8 * 256]); wv = kb.sb([128, 8 * 256])
        for w, c0 in ((wq, 0), (wk, 256), (wv, 512)):
            src = wqkvT.ap[:, c0:c0 + 256].rearrange("(a p) h -> p a h", p=128)
            dst = w.ap[:, :].rearrange("p (a h) -> p a h", a=8)
            kb.dma_ap("sync", w, dst, wqkvT, src)
        xg = [kb.sb([128, L]) for _ in range(8)]
        xh = [kb.sb([128, L]) for _ in range(8)]
        sq = [kb.sb([128, L]) for _ in range(2)]
        rows = [kb.sb([1, L]) for _ in range(4)]
        abc = [kb.sb([128, L]) for _ in range(2)]
        qfm = [kb.sb([128, L]) for _ in range(2)]
        kfm = [kb.sb([128, L]) for _ in range(2)]
        krm = kb.sb([128, 256])
        vrm = kb.sb([128, 256])
        nrm = kb.sb([1, 512])
        rcol = kb.sb([128, 4])
        Xb = [kb.sb([128, 128]) for _ in range(4)]
        Xtb = [kb.sb([128, 128]) for _ in range(4)]
        Msb = [kb.sb([128, 128]) for _ in range(4)]
        Mtb = [kb.sb([128, 128]) for _ in range(4)]
        Ps = kb.sb([128, 128])
        kf0 = kb.sb([64, 128])
        qf0 = kb.sb([64, 128])
        R2 = kb.sb([128, 64])
        Zs = kb.sb([128, 64])
        Us = kb.sb([128, 128])
        yev = kb.sb([128, 128])
        p_sx = kb.psum([1, L]); p_sxx = kb.psum([1, L]); p_bc = kb.psum([128, L])
        p_a = kb.psum([128, 256])
        p_b = kb.psum([128, 128])
        p_y = kb.psum([128, 128])
        p_s = kb.psum([64, 64])
        p_n = kb.psum([1, 512])
        pbank = {"sx": p_sx, "sxx": p_sxx, "bc": p_bc, "sq": sq, "rows": rows, "abc": abc}
        for ci in range(nch):
            ch = ch0 + ci
            t0 = ch * L
            for c in range(8):
                kb.dma("sync", xg[c], np.s_[:, :], xfull,
                       np.s_[c * 128:(c + 1) * 128, t0:t0 + L])
            if skip_ln:
                for c in range(8):
                    kb.op("vector", lambda e, c=c: e.tensor_copy(xh[c].ap[:, :], xg[c].ap[:, :]),
                          ins=(xg[c],), outs=(xh[c],))
            else:
                emit_ln_fm(kb, xg, xh, L, DM, pbank)
            for half in range(2):
                kb.mm_group(
                    [(lambda e, dc=dc, half=half: e.matmul(
                        p_a.ap[:, 0:L], wq.ap[:, dc * 256 + half * 128:dc * 256 + (half + 1) * 128],
                        xh[dc].ap[:, :], start=(dc == 0), stop=(dc == 7))) for dc in range(8)],
                    ins=tuple(xh) + (wq,), outs=(p_a,))
                kb.op("vector", lambda e, half=half: e.tensor_copy(qfm[half].ap[:, :], p_a.ap[:, 0:L]),
                      ins=(p_a,), outs=(qfm[half],))
                kb.mm_group(
                    [(lambda e, dc=dc, half=half: e.matmul(
                        p_a.ap[:, 0:L], wk.ap[:, dc * 256 + half * 128:dc * 256 + (half + 1) * 128],
                        xh[dc].ap[:, :], start=(dc == 0), stop=(dc == 7))) for dc in range(8)],
                    ins=tuple(xh) + (wk,), outs=(p_a,))
                kb.op("vector", lambda e, half=half: e.tensor_copy(kfm[half].ap[:, :], p_a.ap[:, 0:L]),
                      ins=(p_a,), outs=(kfm[half],))
            kb.mm_group(
                [(lambda e, dc=dc: e.matmul(
                    p_a.ap[:, 0:256],
                    xh[dc].ap[:, :],
                    wv.ap[:, dc * 256:(dc + 1) * 256],
                    start=(dc == 0), stop=(dc == 7))) for dc in range(8)],
                ins=tuple(xh) + (wv,), outs=(p_a,))
            kb.op("vector", lambda e: e.tensor_copy(vrm.ap[:, :], p_a.ap[:, 0:256]),
                  ins=(p_a,), outs=(vrm,))
            kb.mm_group(
                [(lambda e, dc=dc: e.matmul(
                    p_a.ap[:, 0:256],
                    xh[dc].ap[:, :],
                    wk.ap[:, dc * 256:(dc + 1) * 256],
                    start=(dc == 0), stop=(dc == 7))) for dc in range(8)],
                ins=tuple(xh) + (wk,), outs=(p_a,))
            kb.op("vector", lambda e: e.tensor_copy(krm.ap[:, :], p_a.ap[:, 0:256]),
                  ins=(p_a,), outs=(krm,))
            for half in range(2):
                s = sq[0]
                kb.op("scalar", lambda e, half=half, s=s: e.activation(
                    out=s.ap[:, :], in_=kfm[half].ap[:, :], func=AF.Square),
                    ins=(kfm[half],), outs=(s,))
                for sub in range(2):
                    h = half * 2 + sub
                    kb.op("tensor", lambda e, s=s, sub=sub, h=h: e.matmul(
                        p_n.ap[:, h * 128:(h + 1) * 128],
                        ones.ap[sub * 64:(sub + 1) * 64, 0:1],
                        s.ap[sub * 64:(sub + 1) * 64, :],
                        start=True, stop=True),
                        ins=(s, ones), outs=(p_n,))
            kb.op("scalar", lambda e: e.activation(out=nrm.ap[:, :], in_=p_n.ap[:, :],
                                                   func=AF.Sqrt),
                  ins=(p_n,), outs=(nrm,))
            kb.op("vector", lambda e: e.tensor_scalar_max(nrm.ap[:, :], nrm.ap[:, :], 1e-12),
                  ins=(nrm,), outs=(nrm,))
            kb.op("vector", lambda e: e.reciprocal(nrm.ap[:, :], nrm.ap[:, :]),
                  ins=(nrm,), outs=(nrm,))
            for half in range(2):
                for sub in range(2):
                    h = half * 2 + sub
                    kb.op("tensor", lambda e, h=h, sub=sub: e.matmul(
                        p_bc.ap[sub * 64:(sub + 1) * 64, :],
                        ones.ap[0:1, 0:64],
                        nrm.ap[:, h * 128:(h + 1) * 128],
                        start=True, stop=True),
                        ins=(nrm, ones), outs=(p_bc,))
                kb.op("vector", lambda e, half=half: e.tensor_mul(
                    kfm[half].ap[:, :], kfm[half].ap[:, :], p_bc.ap[:, :]),
                    ins=(kfm[half], p_bc), outs=(kfm[half],))
            for h in range(4):
                kb.op("tensor", lambda e, h=h: e.transpose(
                    p_b.ap[0:128, 0:1],
                    nrm.ap[:, h * 128:(h + 1) * 128], Ione.ap[0:1, 0:1]),
                    ins=(nrm, Ione), outs=(p_b,))
                kb.op("vector", lambda e, h=h: e.tensor_copy(
                    rcol.ap[:, h:h + 1], p_b.ap[:, 0:1]),
                    ins=(p_b,), outs=(rcol,))
            for h in range(4):
                kb.op("vector", lambda e, h=h: e.tensor_single_scalar(
                    out=krm.ap[:, h * 64:(h + 1) * 64], in_=krm.ap[:, h * 64:(h + 1) * 64],
                    scalar=rcol.ap[:, h:h + 1], op=ALU.mult),
                    ins=(krm, rcol), outs=(krm,))
            for h in range(4):
                half, sub = h // 2, h % 2
                kf = kfm[half].ap[sub * 64:(sub + 1) * 64, :]
                qf = qfm[half].ap[sub * 64:(sub + 1) * 64, :]
                kr = krm.ap[:, h * 64:(h + 1) * 64]
                vr = vrm.ap[:, h * 64:(h + 1) * 64]
                kfT, qfT, krT, vrT = kfm[half], qfm[half], krm, vrm
                acol = abcols.ap[:, h:h + 1]
                bcol = abcols.ap[:, 4 + h:5 + h]
                kb.op("vector", lambda e, kf=kf: e.tensor_copy(kf0.ap[:, :], kf),
                      ins=(kfT,), outs=(kf0,))
                kb.op("vector", lambda e, qf=qf: e.tensor_copy(qf0.ap[:, :], qf),
                      ins=(qfT,), outs=(qf0,))
                kb.op("tensor", lambda e, kf=kf: e.matmul(
                    p_b.ap[:, :], kf, kf, start=True, stop=True),
                    ins=(kfT,), outs=(p_b,))
                kb.op("vector", lambda e, h=h: e.tensor_mul(Xb[h].ap[:, :], p_b.ap[:, :], Mlo.ap[:, :]),
                      ins=(p_b, Mlo), outs=(Xb[h],))
                kb.op("vector", lambda e, h=h: e.tensor_mul(Xtb[h].ap[:, :], p_b.ap[:, :], Mup.ap[:, :]),
                      ins=(p_b, Mup), outs=(Xtb[h],))
                kb.op("vector", lambda e, h=h, acol=acol: e.tensor_single_scalar(
                    out=Xb[h].ap[:, :], in_=Xb[h].ap[:, :], scalar=acol, op=ALU.mult),
                    ins=(Xb[h], abcols), outs=(Xb[h],))
                kb.op("vector", lambda e, h=h: e.tensor_sub(Xb[h].ap[:, :], Ione.ap[:, :], Xb[h].ap[:, :]),
                      ins=(Ione, Xb[h]), outs=(Xb[h],))
                kb.op("vector", lambda e, h=h, acol=acol: e.tensor_single_scalar(
                    out=Xtb[h].ap[:, :], in_=Xtb[h].ap[:, :], scalar=acol, op=ALU.mult),
                    ins=(Xtb[h], abcols), outs=(Xtb[h],))
                kb.op("vector", lambda e, h=h: e.tensor_sub(Xtb[h].ap[:, :], Ione.ap[:, :], Xtb[h].ap[:, :]),
                      ins=(Ione, Xtb[h]), outs=(Xtb[h],))
                kb.op("vector", lambda e, h=h: e.tensor_sub(Msb[h].ap[:, :], Itwo.ap[:, :], Xb[h].ap[:, :]),
                      ins=(Itwo, Xb[h]), outs=(Msb[h],))
                kb.op("vector", lambda e, h=h: e.tensor_sub(Mtb[h].ap[:, :], Itwo.ap[:, :], Xtb[h].ap[:, :]),
                      ins=(Itwo, Xtb[h]), outs=(Mtb[h],))
                for it in range(6):
                    kb.op("tensor", lambda e, h=h: e.matmul(
                        p_b.ap[:, :], Mtb[h].ap[:, :], Xb[h].ap[:, :], start=True, stop=True),
                        ins=(Mtb[h], Xb[h]), outs=(p_b,))
                    kb.op("vector", lambda e: e.tensor_copy(Ps.ap[:, :], p_b.ap[:, :]),
                          ins=(p_b,), outs=(Ps,))
                    kb.op("tensor", lambda e, h=h: e.matmul(
                        p_b.ap[:, :], Xtb[h].ap[:, :], Ps.ap[:, :], start=True, stop=True),
                        ins=(Xtb[h], Ps), outs=(p_b,))
                    kb.op("vector", lambda e, h=h: e.scalar_tensor_tensor(
                        out=Xb[h].ap[:, :], in0=Xb[h].ap[:, :], scalar=2.0, in1=p_b.ap[:, :],
                        op0=ALU.mult, op1=ALU.subtract),
                        ins=(Xb[h], p_b), outs=(Xb[h],))
                    kb.op("tensor", lambda e, h=h: e.matmul(
                        p_b.ap[:, :], Ps.ap[:, :], Xtb[h].ap[:, :], start=True, stop=True),
                        ins=(Ps, Xtb[h]), outs=(p_b,))
                    kb.op("vector", lambda e, h=h: e.scalar_tensor_tensor(
                        out=Xtb[h].ap[:, :], in0=Xtb[h].ap[:, :], scalar=2.0, in1=p_b.ap[:, :],
                        op0=ALU.mult, op1=ALU.subtract),
                        ins=(Xtb[h], p_b), outs=(Xtb[h],))
                kb.op("tensor", lambda e, h=h: e.matmul(
                    p_b.ap[:, 0:64], kf0.ap[:, :], S[h].ap[:, :], start=True, stop=True),
                    ins=(kf0, S[h]), outs=(p_b,))
                kb.op("vector", lambda e, h=h, acol=acol: e.tensor_single_scalar(
                    out=R2.ap[:, :], in_=p_b.ap[:, 0:64], scalar=acol, op=ALU.mult),
                    ins=(p_b, abcols), outs=(R2,))
                kb.op("vector", lambda e, h=h, bcol=bcol, vr=vr: e.scalar_tensor_tensor(
                    out=R2.ap[:, :], in0=vr, scalar=bcol, in1=R2.ap[:, :],
                    op0=ALU.mult, op1=ALU.subtract),
                    ins=(vrT, abcols, R2), outs=(R2,))
                kb.op("tensor", lambda e, h=h: e.matmul(
                    p_b.ap[:, 0:64], Xtb[h].ap[:, :], R2.ap[:, :], start=True, stop=True),
                    ins=(Xtb[h], R2), outs=(p_b,))
                kb.op("vector", lambda e: e.tensor_copy(Zs.ap[:, :], p_b.ap[:, 0:64]),
                      ins=(p_b,), outs=(Zs,))
                kb.op("tensor", lambda e, kf=kf, qf=qf: e.matmul(
                    p_b.ap[:, :], kf, qf, start=True, stop=True),
                    ins=(kfT, qfT), outs=(p_b,))
                kb.op("vector", lambda e: e.tensor_mul(Us.ap[:, :], p_b.ap[:, :], Mup.ap[:, :]),
                      ins=(p_b, Mup), outs=(Us,))
                kb.mm_group(
                    [(lambda e, h=h, sub=sub: e.matmul(
                        p_y.ap[sub * 64:(sub + 1) * 64, :], S[h].ap[:, :], qf0.ap[:, :],
                        start=True, stop=False)),
                     (lambda e, h=h, sub=sub: e.matmul(
                         p_y.ap[sub * 64:(sub + 1) * 64, :], Zs.ap[:, :], Us.ap[:, :],
                         start=False, stop=True))],
                    ins=(S[h], qf0, Zs, Us), outs=(p_y,))
                kb.op("tensor", lambda e, kr=kr: e.matmul(
                    p_s.ap[:, :], kr, Zs.ap[:, :], start=True, stop=True),
                    ins=(krT, Zs), outs=(p_s,))
                kb.op("vector", lambda e, h=h: e.tensor_add(S[h].ap[:, :], S[h].ap[:, :], p_s.ap[:, :]),
                      ins=(S[h], p_s), outs=(S[h],))
                if h % 2 == 1:
                    kb.op("vector", lambda e: e.tensor_copy(yev.ap[:, :], p_y.ap[:, :]),
                          ins=(p_y,), outs=(yev,))
                    kb.dma("sync", yout, np.s_[half * 128:(half + 1) * 128, t0:t0 + L],
                           yev, np.s_[:, :])
        for h in range(4):
            kb.dma("sync", s_out, np.s_[h * 64:(h + 1) * 64, :], S[h], np.s_[:, :])


def emit_ab(kb, ab_in, abcols, head0):
    with kb.scope():
        abr = kb.sb([1, 32])
        p_c = kb.psum([128, 1])
        ones = kb.cst_ones
        kb.dma("sync", abr, np.s_[:, 0:16], ab_in, np.s_[0:1, :])
        kb.dma("sync", abr, np.s_[:, 16:32], ab_in, np.s_[1:2, :])
        kb.op("scalar", lambda e: e.activation(out=abr.ap[:, :], in_=abr.ap[:, :],
                                               func=AF.Sigmoid),
              ins=(abr,), outs=(abr,))
        kb.op("vector", lambda e: e.tensor_scalar_mul(abr.ap[:, :], abr.ap[:, :], 0.5),
              ins=(abr,), outs=(abr,))
        for j in range(8):
            c = (j // 4) * 16 + head0 + (j % 4)
            kb.op("tensor", lambda e, c=c: e.matmul(
                p_c.ap[:, :], ones.ap[0:1, 0:128], abr.ap[0:1, c:c + 1],
                start=True, stop=True),
                ins=(abr, ones), outs=(p_c,))
            kb.op("vector", lambda e, j=j: e.tensor_copy(abcols.ap[:, j:j + 1], p_c.ap[:, :]),
                  ins=(p_c,), outs=(abcols,))



def emit_moe_route(kb, x_in, gwT, xhat_tm, selw_out, selp_out, rows_out, cnt_out, ntok, C, srow_d):
    gsz = min(512, ntok)
    ng = ntok // gsz
    ones = kb.cst_ones
    with kb.scope():
        xg = [kb.sb([128, gsz]) for _ in range(8)]
        xh = [kb.sb([128, gsz]) for _ in range(8)]
        sq = [kb.sb([128, gsz]) for _ in range(2)]
        rows = [kb.sb([1, gsz]) for _ in range(4)]
        abc = [kb.sb([128, gsz]) for _ in range(2)]
        gw = kb.sb([128, 8 * 16])
        prob = kb.sb([128, 16])
        t8 = kb.sb([128, 8])
        i8 = kb.sb([128, 8], U32)
        i8f = kb.sb([128, 8])
        wcols = kb.sb([128, 2])
        sw = kb.sb([128, 4])
        xtm = kb.sb([128, 1024])
        p_sx = kb.psum([1, gsz]); p_sxx = kb.psum([1, gsz]); p_bc = kb.psum([128, gsz])
        p_l = kb.psum([128, 16])
        p_t = kb.psum([128, 128])
        pbank = {"sx": p_sx, "sxx": p_sxx, "bc": p_bc, "sq": sq, "rows": rows, "abc": abc}
        src = gwT.ap[:, :].rearrange("(a p) h -> p a h", p=128)
        kb.dma_ap("sync", gw, gw.ap[:, :].rearrange("p (a h) -> p a h", a=8), gwT, src)
        for g in range(ng):
            t0 = g * gsz
            for c in range(8):
                kb.dma("sync", xg[c], np.s_[:, :], x_in,
                       np.s_[c * 128:(c + 1) * 128, t0:t0 + gsz])
            emit_ln_fm(kb, xg, xh, gsz, DM, pbank)
            for st in range(gsz // 128):
                u0 = st * 128
                kb.mm_group(
                    [(lambda e, dc=dc, u0=u0: e.matmul(
                        p_l.ap[:, :], xh[dc].ap[:, u0:u0 + 128],
                        gw.ap[:, dc * 16:(dc + 1) * 16],
                        start=(dc == 0), stop=(dc == 7))) for dc in range(8)],
                    ins=tuple(xh) + (gw,), outs=(p_l,))
                kb.op("scalar", lambda e: e.activation(out=prob.ap[:, :], in_=p_l.ap[:, :],
                                                       func=AF.Sigmoid),
                      ins=(p_l,), outs=(prob,))
                kb.op("vector", lambda e: e.max(out=t8.ap[:, :], in_=prob.ap[:, :]),
                      ins=(prob,), outs=(t8,))
                kb.op("vector", lambda e: e.max_index(out=i8.ap[:, :], in_max=t8.ap[:, :],
                                                      in_values=prob.ap[:, :]),
                      ins=(t8, prob), outs=(i8,))
                kb.op("vector", lambda e: e.tensor_copy(i8f.ap[:, :], i8.ap[:, :]),
                      ins=(i8,), outs=(i8f,))
                kb.op("vector", lambda e: e.tensor_add(wcols.ap[:, 0:1], t8.ap[:, 0:1],
                                                       t8.ap[:, 1:2]),
                      ins=(t8,), outs=(wcols,))
                kb.op("vector", lambda e: e.tensor_scalar_add(wcols.ap[:, 0:1],
                                                              wcols.ap[:, 0:1], 1e-6),
                      ins=(wcols,), outs=(wcols,))
                kb.op("vector", lambda e: e.reciprocal(wcols.ap[:, 1:2], wcols.ap[:, 0:1]),
                      ins=(wcols,), outs=(wcols,))
                kb.op("vector", lambda e: e.tensor_mul(sw.ap[:, 2:3], t8.ap[:, 0:1],
                                                       wcols.ap[:, 1:2]),
                      ins=(t8, wcols), outs=(sw,))
                kb.op("vector", lambda e: e.tensor_mul(sw.ap[:, 3:4], t8.ap[:, 1:2],
                                                       wcols.ap[:, 1:2]),
                      ins=(t8, wcols), outs=(sw,))
                kb.op("vector", lambda e: e.tensor_copy(sw.ap[:, 0:2], i8f.ap[:, 0:2]),
                      ins=(i8f,), outs=(sw,))
                kb.dma("sync", selw_out, np.s_[t0 + u0:t0 + u0 + 128, :], sw, np.s_[:, :])
                kb.dma("sync", selp_out, np.s_[t0 + u0:t0 + u0 + 128, :], sw, np.s_[:, 0:2])
                for dc in range(8):
                    kb.op("tensor", lambda e, dc=dc, u0=u0: e.transpose(
                        p_t.ap[:, :], xh[dc].ap[:, u0:u0 + 128], Icst(kb).ap[:, :]),
                        ins=(xh[dc], Icst(kb)), outs=(p_t,))
                    kb.op("vector", lambda e, dc=dc: e.tensor_copy(
                        xtm.ap[:, dc * 128:(dc + 1) * 128], p_t.ap[:, :]),
                        ins=(p_t,), outs=(xtm,))
                kb.dma("sync", xhat_tm, np.s_[t0 + u0:t0 + u0 + 128, :], xtm, np.s_[:, :])
    N2 = 2 * ntok
    RB = min(2048, ntok)
    with kb.scope():
        selc = kb.sb([1, RB])
        selb = kb.sb([16, RB])
        scan = kb.sb([16, RB])
        rl = kb.sb([1, RB])
        carry = kb.sb([16, 1])
        ecol = kb.sb([16, 1], I32)
        ecolf = kb.sb([16, 1])
        cnt = kb.sb([16, 2])
        QC = min(512, RB)
        p_r = kb.psum([1, QC])
        p_bb = kb.psum([16, QC])
        ones = kb.cst_ones
        kb.op("gpsimd", lambda e: e.iota(ecol.ap[:, :], pattern=[[0, 1]], base=0,
                                         channel_multiplier=1), outs=(ecol,))
        kb.op("vector", lambda e: e.tensor_copy(ecolf.ap[:, :], ecol.ap[:, :]),
              ins=(ecol,), outs=(ecolf,))
        kb.op("vector", lambda e: e.memset(carry.ap[:, :], 0.0), outs=(carry,))
        selp_flat = selp_out.ap[:, :].rearrange("t k -> (t k)").unsqueeze(0)
        for j in range(N2 // RB):
            kb.dma_ap("sync", selc, selc.ap[0:1, :], selp_out,
                      selp_flat[:, j * RB:(j + 1) * RB])
            kb.dma_ap("sync", rows_out, rows_out.ap[0:1, j * RB:(j + 1) * RB],
                      selc, selc.ap[0:1, :])
            for q in range(RB // QC):
                kb.op("tensor", lambda e, q=q: e.matmul(
                    p_bb.ap[:, :], ones.ap[0:1, 0:16], selc.ap[:, q * QC:(q + 1) * QC],
                    start=True, stop=True),
                    ins=(selc, ones), outs=(p_bb,))
                kb.op("vector", lambda e, q=q: e.tensor_copy(
                    selb.ap[:, q * QC:(q + 1) * QC], p_bb.ap[:, :]),
                    ins=(p_bb,), outs=(selb,))
            kb.op("vector", lambda e: e.tensor_single_scalar(
                out=selb.ap[:, :], in_=selb.ap[:, :], scalar=ecolf.ap[:, :],
                op=ALU.is_equal),
                ins=(selb, ecolf), outs=(selb,))
            kb.op("vector", lambda e: e.tensor_tensor_scan(
                out=scan.ap[:, :], data0=selb.ap[:, :], data1=selb.ap[:, :],
                initial=carry.ap[:, :], op0=ALU.add, op1=ALU.bypass),
                ins=(selb, carry), outs=(scan,))
            kb.op("vector", lambda e: e.tensor_copy(carry.ap[:, :],
                                                    scan.ap[:, RB - 1:RB]),
                  ins=(scan,), outs=(carry,))
            if (j + 1) * RB == ntok:
                kb.op("vector", lambda e: e.tensor_copy(cnt.ap[:, 0:1], carry.ap[:, :]),
                      ins=(carry,), outs=(cnt,))
            kb.op("vector", lambda e: e.tensor_scalar_add(scan.ap[:, :], scan.ap[:, :],
                                                          -1.0),
                  ins=(scan,), outs=(scan,))
            kb.op("vector", lambda e: e.tensor_mul(selb.ap[:, :], selb.ap[:, :],
                                                   scan.ap[:, :]),
                  ins=(selb, scan), outs=(selb,))
            for q in range(RB // QC):
                kb.op("tensor", lambda e, q=q: e.matmul(
                    p_r.ap[:, :], ones.ap[0:16, 0:1], selb.ap[:, q * QC:(q + 1) * QC],
                    start=True, stop=True),
                    ins=(selb, ones), outs=(p_r,))
                kb.op("vector", lambda e, q=q: e.tensor_copy(
                    rl.ap[:, q * QC:(q + 1) * QC], p_r.ap[:, :]),
                    ins=(p_r,), outs=(rl,))
            kb.dma("sync", rows_out, np.s_[1:2, j * RB:(j + 1) * RB], rl, np.s_[0:1, :])
            OOB = float(16 * C + 100)
            kb.op("vector", lambda e: e.tensor_single_scalar(
                out=scan.ap[0:1, :], in_=rl.ap[:, :], scalar=float(C), op=ALU.is_lt),
                ins=(rl,), outs=(scan,))
            kb.op("vector", lambda e: e.scalar_tensor_tensor(
                out=rl.ap[:, :], in0=selc.ap[:, :], scalar=float(C), in1=rl.ap[:, :],
                op0=ALU.mult, op1=ALU.add),
                ins=(selc, rl), outs=(rl,))
            kb.op("vector", lambda e: e.tensor_mul(rl.ap[:, :], rl.ap[:, :],
                                                   scan.ap[0:1, :]),
                  ins=(rl, scan), outs=(rl,))
            kb.op("vector", lambda e: e.scalar_tensor_tensor(
                out=rl.ap[:, :], in0=scan.ap[0:1, :], scalar=-OOB, in1=rl.ap[:, :],
                op0=ALU.mult, op1=ALU.add),
                ins=(scan, rl), outs=(rl,))
            kb.op("vector", lambda e: e.tensor_scalar_add(rl.ap[:, :], rl.ap[:, :], OOB),
                  ins=(rl,), outs=(rl,))
            kb.dma("sync", srow_d, np.s_[0:1, j * RB:(j + 1) * RB], rl, np.s_[0:1, :])
        kb.op("vector", lambda e: e.tensor_copy(cnt.ap[:, 1:2], carry.ap[:, :]),
              ins=(carry,), outs=(cnt,))
        kb.dma("sync", cnt_out, np.s_[:, :], cnt, np.s_[:, :])

def Icst(kb):
    return kb.ident


_MOEB_NEXP = 16
_MOEB_STAGES = {'z', 'o', 't', 'r', 'e'}


def emit_moe_experts(kb, rows_in, cnts3, w1_in, w2_in,
                     ntok, C, CAP_G, ebuf, obuf, vrow_d):
    N2 = 2 * ntok
    RB = min(2048, ntok)
    nr = ntok // 128
    ner = C // 128
    ones = kb.cst_ones
    OOB = float(16 * C + 100)
    with kb.scope():
        cnts = kb.sb([96, 16])
        offc = kb.sb([16, 2])
        QC = min(512, RB)
        p_o = kb.psum([1, max(QC, 16)])
        p_bb = kb.psum([16, QC])
        p_t = kb.psum([128, 128])
        kb.dma("sync", cnts, np.s_[:, :], cnts3, np.s_[:, :])
        zt = kb.sb([128, 1024])
        kb.op("vector", lambda e: e.memset(zt.ap[:, :], 0.0), outs=(zt,))
        nbrows = obuf.ap.shape[0]
        if 'z' in _MOEB_STAGES:
            for r0 in range(0, nbrows - 127, 128):
                kb.dma("sync", obuf, np.s_[r0:r0 + 128, :], zt, np.s_[:, :])
            if nbrows % 128:
                r0 = nbrows - (nbrows % 128)
                kb.dma("sync", obuf, np.s_[r0:nbrows, :], zt, np.s_[0:nbrows - r0, :])
        if 'o' not in _MOEB_STAGES:
            kb.op("vector", lambda e: e.memset(offc.ap[:, :], 0.0), outs=(offc,))
        colsel = kb.sb([96, 2])
        s216 = kb.sb([2, 16])
        if 'o' in _MOEB_STAGES:
            kb.op("vector", lambda e: e.memset(colsel.ap[:, :], 0.0), outs=(colsel,))
            kb.op("vector", lambda e: e.memset(colsel.ap[0:32, 0:1], 1.0), outs=(colsel,))
            kb.op("vector", lambda e: e.memset(colsel.ap[32:64, 1:2], 1.0), outs=(colsel,))
            kb.op("vector", lambda e: e.memset(colsel.ap[64:96, 1:2], 1.0), outs=(colsel,))
            kb.op("tensor", lambda e: e.matmul(p_bb.ap[0:2, 0:16], colsel.ap[:, :],
                                               cnts.ap[:, :], start=True, stop=True),
                  ins=(colsel, cnts), outs=(p_bb,))
            kb.op("vector", lambda e: e.tensor_copy(s216.ap[:, :], p_bb.ap[0:2, 0:16]),
                  ins=(p_bb,), outs=(s216,))
        if 't' in _MOEB_STAGES:
            kb.op("tensor", lambda e: e.transpose(
                p_t.ap[0:16, 0:2], s216.ap[:, :], kb.ident.ap[0:2, 0:2]),
                ins=(s216,), outs=(p_t,))
            kb.op("vector", lambda e: e.tensor_copy(offc.ap[:, :], p_t.ap[0:16, 0:2]),
                  ins=(p_t,), outs=(offc,))
        selc = kb.sb([1, RB]); rlc = kb.sb([1, RB])
        selb = kb.sb([16, RB]); tmp16 = kb.sb([16, RB])
        rgc = kb.sb([1, RB]); vlc = kb.sb([1, RB]); slc = kb.sb([1, RB])
        ecol = kb.sb([16, 1], I32); ecolf = kb.sb([16, 1])
        kb.op("gpsimd", lambda e: e.iota(ecol.ap[:, :], pattern=[[0, 1]], base=0,
                                         channel_multiplier=1), outs=(ecol,))
        kb.op("vector", lambda e: e.tensor_copy(ecolf.ap[:, :], ecol.ap[:, :]),
              ins=(ecol,), outs=(ecolf,))
        for j in (range(N2 // RB) if 'r' in _MOEB_STAGES else []):
            half = 0 if (j * RB) < ntok else 1
            kb.dma("sync", selc, np.s_[:, :], rows_in, np.s_[0:1, j * RB:(j + 1) * RB])
            kb.dma("sync", rlc, np.s_[:, :], rows_in, np.s_[1:2, j * RB:(j + 1) * RB])
            for q in range(RB // QC):
                kb.op("tensor", lambda e, q=q: e.matmul(
                    p_bb.ap[:, :], ones.ap[0:1, 0:16], selc.ap[:, q * QC:(q + 1) * QC],
                    start=True, stop=True),
                    ins=(selc, ones), outs=(p_bb,))
                kb.op("vector", lambda e, q=q: e.tensor_copy(
                    selb.ap[:, q * QC:(q + 1) * QC], p_bb.ap[:, :]),
                    ins=(p_bb,), outs=(selb,))
            kb.op("vector", lambda e: e.tensor_single_scalar(
                out=selb.ap[:, :], in_=selb.ap[:, :], scalar=ecolf.ap[:, :],
                op=ALU.is_equal),
                ins=(selb, ecolf), outs=(selb,))
            kb.op("vector", lambda e, half=half: e.tensor_single_scalar(
                out=tmp16.ap[:, :], in_=selb.ap[:, :],
                scalar=offc.ap[:, half:half + 1], op=ALU.mult),
                ins=(selb, offc), outs=(tmp16,))
            for q in range(RB // QC):
                kb.op("tensor", lambda e, q=q: e.matmul(
                    p_o.ap[:, :], ones.ap[0:16, 0:1], tmp16.ap[:, q * QC:(q + 1) * QC],
                    start=True, stop=True),
                    ins=(tmp16, ones), outs=(p_o,))
                kb.op("vector", lambda e, q=q: e.tensor_add(
                    rgc.ap[:, q * QC:(q + 1) * QC], rlc.ap[:, q * QC:(q + 1) * QC],
                    p_o.ap[:, :]),
                    ins=(rlc, p_o), outs=(rgc,))
            kb.op("vector", lambda e: e.tensor_single_scalar(
                out=vlc.ap[:, :], in_=rgc.ap[:, :], scalar=float(CAP_G), op=ALU.is_lt),
                ins=(rgc,), outs=(vlc,))
            kb.op("vector", lambda e: e.tensor_single_scalar(
                out=slc.ap[:, :], in_=rlc.ap[:, :], scalar=float(C), op=ALU.is_lt),
                ins=(rlc,), outs=(slc,))
            kb.op("vector", lambda e: e.tensor_mul(vlc.ap[:, :], vlc.ap[:, :],
                                                   slc.ap[:, :]),
                  ins=(vlc, slc), outs=(vlc,))
            kb.dma("sync", vrow_d, np.s_[0:1, j * RB:(j + 1) * RB], vlc, np.s_[:, :])
        w1s = [kb.sb([128, 8 * 128]) for _ in range(2)]
        w2h = [kb.sb([128, 512]) for _ in range(16)]
        xe = [kb.sb([128, C]) for _ in range(8)]
        he = [kb.sb([128, C]) for _ in range(16)]
        erow = kb.sb([128, 1024])
        p_h = kb.psum([128, 512])
        p_h2 = kb.psum([128, 256])
        for e_ in range(_MOEB_NEXP):
            for r in range(ner):
                kb.dma("sync", erow, np.s_[:, :], ebuf,
                       np.s_[e_ * C + r * 128:e_ * C + (r + 1) * 128, :])
                for dc in range(8):
                    kb.op("tensor", lambda e, dc=dc: e.transpose(
                        p_t.ap[:, :], erow.ap[:, dc * 128:(dc + 1) * 128],
                        kb.ident.ap[:, :]),
                        ins=(erow,), outs=(p_t,))
                    kb.op("vector", lambda e, dc=dc, r=r: e.tensor_copy(
                        xe[dc].ap[:, r * 128:(r + 1) * 128], p_t.ap[:, :]),
                        ins=(p_t,), outs=(xe[dc],))
            for fc in range(16):
                w = w1s[fc % 2]
                src = w1_in.ap[e_][:, fc * 128:(fc + 1) * 128].rearrange(
                    "(a p) h -> p a h", p=128)
                kb.dma_ap("sync", w, w.ap[:, :].rearrange("p (a h) -> p a h", a=8),
                          w1_in, src)
                nfree = (C + 511) // 512
                for fg in range(nfree):
                    f0 = fg * 512
                    f1 = min(C, f0 + 512)
                    pp = p_h if (f1 - f0) == 512 else p_h2
                    kb.mm_group(
                        [(lambda e, dc=dc, w=w, f0=f0, f1=f1, pp=pp: e.matmul(
                            pp.ap[:, 0:f1 - f0], w.ap[:, dc * 128:(dc + 1) * 128],
                            xe[dc].ap[:, f0:f1],
                            start=(dc == 0), stop=(dc == 7))) for dc in range(8)],
                        ins=tuple(xe) + (w,), outs=(pp,))
                    kb.op("scalar", lambda e, fc=fc, f0=f0, f1=f1, pp=pp: e.activation(
                        out=he[fc].ap[:, f0:f1], in_=pp.ap[:, 0:f1 - f0], func=AF.Gelu),
                        ins=(pp,), outs=(he[fc],))
            for colh in range(2):
                for fc in range(16):
                    kb.dma("sync", w2h[fc], np.s_[:, :], w2_in,
                           np.s_[e_, fc * 128:(fc + 1) * 128, colh * 512:(colh + 1) * 512])
                for r in range(ner):
                    kb.mm_group(
                        [(lambda e, fc=fc, r=r: e.matmul(
                            p_h.ap[:, :], he[fc].ap[:, r * 128:(r + 1) * 128],
                            w2h[fc].ap[:, :],
                            start=(fc == 0), stop=(fc == 15))) for fc in range(16)],
                        ins=tuple(he) + tuple(w2h), outs=(p_h,))
                    kb.op("vector", lambda e, colh=colh: e.tensor_copy(
                        erow.ap[:, colh * 512:(colh + 1) * 512], p_h.ap[:, :]),
                        ins=(p_h,), outs=(erow,))
                    kb.dma("sync", obuf,
                           np.s_[e_ * C + r * 128:e_ * C + (r + 1) * 128,
                                 colh * 512:(colh + 1) * 512],
                           erow, np.s_[:, colh * 512:(colh + 1) * 512])

def _norm_ktR(kb, ktR, ones, p_n1, p_bc5, nrow, sqk):
    for p in range(24):
        kb.op("scalar", lambda e, p=p: e.activation(out=sqk.ap[:, :], in_=ktR[p].ap[:, :],
                                                    func=AF.Square),
              ins=(ktR[p],), outs=(sqk,))
        kb.op("tensor", lambda e: e.matmul(p_n1.ap[:, :], ones.ap[:, 0:1], sqk.ap[:, :],
                                           start=True, stop=True),
              ins=(sqk, ones), outs=(p_n1,))
        kb.op("scalar", lambda e: e.activation(out=nrow.ap[:, :], in_=p_n1.ap[:, :],
                                               func=AF.Sqrt, scale=0.25),
              ins=(p_n1,), outs=(nrow,))
        kb.op("vector", lambda e: e.reciprocal(nrow.ap[:, :], nrow.ap[:, :]),
              ins=(nrow,), outs=(nrow,))
        kb.op("tensor", lambda e: e.matmul(p_bc5.ap[:, :], ones.ap[0:1, 0:128],
                                           nrow.ap[:, :], start=True, stop=True),
              ins=(nrow, ones), outs=(p_bc5,))
        kb.op("vector", lambda e, p=p: e.tensor_mul(ktR[p].ap[:, :], ktR[p].ap[:, :],
                                                    p_bc5.ap[:, :]),
              ins=(ktR[p], p_bc5), outs=(ktR[p],))


_ATT1_STAGES = {'m', 'p', 'o'}
_ATT1_MSUB = 5


def emit_attn1(kb, x_in, x_out, W, ntok, r0, nr_pairs):
    half = ntok // 2
    ones = kb.cst_ones
    with kb.scope():
        wq = [kb.sb([128, 1024]) for _ in range(6)]
        for c in range(6):
            src = W['wqT'].ap[:, c * 128:(c + 1) * 128].rearrange("(a p) h -> p a h", p=128)
            kb.dma_ap("sync", wq[c], wq[c].ap[:, :].rearrange("p (a h) -> p a h", a=8),
                      W['wqT'], src)
        wo = [kb.sb([128, 1024]) for _ in range(2)]
        ktR = [kb.sb([128, 512]) for _ in range(24)]
        for p in range(24):
            kb.dma("sync", ktR[p], np.s_[:, :], W['ktR'], np.s_[p, :, :])
        vtab = kb.sb([128, 96 * 32])
        src = W['vtab'].ap[:, :].rearrange("(n p) v -> p n v", p=128)
        kb.dma_ap("sync", vtab, vtab.ap[:, :].rearrange("p (n v) -> p n v", n=96),
                  W['vtab'], src)
        wres8 = kb.sb([128, 64]); wmemT = kb.sb([32, 64])
        bres_c = kb.sb([32, 1]); bmem_r = kb.sb([1, 64])
        bqc = kb.sb([128, 6]); gqc = kb.sb([128, 1]); bqlnc = kb.sb([128, 1])
        E8 = kb.sb([128, 8]); E8b = kb.sb([8, 128])
        goc = kb.sb([128, 8]); boc = kb.sb([128, 8])
        for t, k_ in ((wres8, 'wres8'), (wmemT, 'wmemT'), (bres_c, 'bres_col'),
                      (bmem_r, 'bmem_row'), (bqc, 'bq_cols'), (gqc, 'gq_col'),
                      (bqlnc, 'bqln_col'), (E8, 'E8'), (E8b, 'E8b'),
                      (goc, 'gout_cols'), (boc, 'bout_cols')):
            kb.dma("sync", t, np.s_[:, :], W[k_], np.s_[:, :])
        with kb.scope():
            p_n1 = kb.psum([1, 512])
            p_bc5 = kb.psum([128, 512])
            nrow = kb.sb([1, 512]); sqk = kb.sb([128, 512])
            _norm_ktR(kb, ktR, ones, p_n1, p_bc5, nrow, sqk)
        xg = [[kb.sb([128, 128]) for _ in range(8)] for _ in range(2)]
        xh = [[kb.sb([128, 128]) for _ in range(8)] for _ in range(2)]
        sq2 = [kb.sb([128, 128]) for _ in range(2)]
        rows4 = [kb.sb([1, 128]) for _ in range(4)]
        abc2 = [kb.sb([128, 128]) for _ in range(2)]
        qf = [[kb.sb([128, 128]) for _ in range(6)] for _ in range(2)]
        mu8 = kb.sb([8, 128]); v8 = kb.sb([8, 128]); m8s = kb.sb([8, 128])
        t8 = [kb.sb([128, 48 * 8]) for _ in range(2)]
        u8 = kb.sb([128, 8])
        rzc = kb.sb([128, 1]); rzr = kb.sb([1, 128])
        sc2 = kb.sb([128, 512]); m2 = kb.sb([128, 512])
        sck = [kb.sb([128, 128]) for _ in range(4)]
        o2s = kb.sb([32, 128])
        out3 = [kb.sb([128, 48 * 64]) for _ in range(2)]
        lo = [kb.sb([128, 1024]) for _ in range(2)]
        dta = kb.sb([128, 1024])
        dred = kb.sb([128, 16 * 8])
        lofm = [[kb.sb([128, 128]) for _ in range(8)] for _ in range(2)]
        tg = [[kb.sb([128, 128]) for _ in range(8)] for _ in range(2)]
        p_at = kb.psum([128, 128])
        p_q = kb.psum([128, 128])
        p_s32 = kb.psum([32, 128])
        p_sc = kb.psum([128, 512])
        p_t128 = kb.psum([128, 192])
        p_stats = kb.psum([1, 256])
        class _V:
            def __init__(self, base, ap):
                self.base = base; self.ap = ap
            @property
            def w(self): return self.base.w
            @w.setter
            def w(self, v): self.base.w = v
            @property
            def rs(self): return self.base.rs
            @rs.setter
            def rs(self, v): self.base.rs = v
        p_s8 = _V(p_s32, p_s32.ap[0:8, :])
        p_o2 = p_s32
        p_t = _V(p_t128, p_t128.ap[:, 0:128])
        p_o3 = _V(p_t128, p_t128.ap[:, 128:192])
        pbank = {"sx": _V(p_stats, p_stats.ap[:, 0:128]),
                 "sxx": _V(p_stats, p_stats.ap[:, 128:256]),
                 "bc": p_q, "sq": sq2, "rows": rows4, "abc": abc2}
        for rp in range(r0, r0 + nr_pairs):
            for side in range(2):
                t0 = rp * 128 + side * half
                for c in range(8):
                    kb.dma("sync", xg[side][c], np.s_[:, :], x_in,
                           np.s_[c * 128:(c + 1) * 128, t0:t0 + 128])
                emit_ln_fm(kb, xg[side], xh[side], 128, DM, pbank)
                for c in range(6):
                    kb.mm_group(
                        [(lambda e, dc=dc, c=c, side=side: e.matmul(
                            p_q.ap[:, :], wq[c].ap[:, dc * 128:(dc + 1) * 128],
                            xh[side][dc].ap[:, :],
                            start=(dc == 0), stop=(dc == 7))) for dc in range(8)],
                        ins=tuple(xh[side]) + (wq[c],), outs=(p_q,))
                    q = qf[side][c]
                    kb.op("scalar", lambda e, c=c, q=q: e.activation(
                        out=q.ap[:, :], in_=p_q.ap[:, :], func=AF.Identity,
                        bias=bqc.ap[:, c:c + 1]),
                        ins=(p_q, bqc), outs=(q,))
                    kb.op("tensor", lambda e, q=q: e.matmul(
                        p_s8.ap[:, :], E8.ap[:, :], q.ap[:, :], start=True, stop=True),
                        ins=(q, E8), outs=(p_s8,))
                    kb.op("vector", lambda e: e.tensor_scalar_mul(
                        mu8.ap[:, :], p_s8.ap[:, :], 1.0 / 16), ins=(p_s8,), outs=(mu8,))
                    kb.op("scalar", lambda e, q=q: e.activation(
                        out=sq2[0].ap[:, :], in_=q.ap[:, :], func=AF.Square),
                        ins=(q,), outs=(sq2[0],))
                    kb.op("tensor", lambda e: e.matmul(
                        p_s8.ap[:, :], E8.ap[:, :], sq2[0].ap[:, :], start=True, stop=True),
                        ins=(sq2[0], E8), outs=(p_s8,))
                    kb.op("vector", lambda e: e.tensor_scalar_mul(
                        v8.ap[:, :], p_s8.ap[:, :], 1.0 / 16), ins=(p_s8,), outs=(v8,))
                    kb.op("vector", lambda e: e.tensor_mul(m8s.ap[:, :], mu8.ap[:, :],
                                                           mu8.ap[:, :]),
                          ins=(mu8,), outs=(m8s,))
                    kb.op("vector", lambda e: e.tensor_sub(v8.ap[:, :], v8.ap[:, :],
                                                           m8s.ap[:, :]),
                          ins=(v8, m8s), outs=(v8,))
                    kb.op("vector", lambda e: e.tensor_scalar_add(v8.ap[:, :], v8.ap[:, :],
                                                                  1e-5),
                          ins=(v8,), outs=(v8,))
                    kb.op("scalar", lambda e: e.activation(out=v8.ap[:, :], in_=v8.ap[:, :],
                                                           func=AF.Sqrt),
                          ins=(v8,), outs=(v8,))
                    kb.op("vector", lambda e: e.reciprocal(v8.ap[:, :], v8.ap[:, :]),
                          ins=(v8,), outs=(v8,))
                    kb.op("tensor", lambda e: e.matmul(p_q.ap[:, :], E8b.ap[:, :],
                                                       mu8.ap[:, :], start=True, stop=True),
                          ins=(mu8, E8b), outs=(p_q,))
                    kb.op("vector", lambda e, q=q: e.tensor_sub(q.ap[:, :], q.ap[:, :],
                                                                p_q.ap[:, :]),
                          ins=(q, p_q), outs=(q,))
                    kb.op("tensor", lambda e: e.matmul(p_q.ap[:, :], E8b.ap[:, :],
                                                       v8.ap[:, :], start=True, stop=True),
                          ins=(v8, E8b), outs=(p_q,))
                    kb.op("vector", lambda e, q=q: e.tensor_mul(q.ap[:, :], q.ap[:, :],
                                                                p_q.ap[:, :]),
                          ins=(q, p_q), outs=(q,))
                    kb.op("vector", lambda e, q=q: e.tensor_single_scalar(
                        out=q.ap[:, :], in_=q.ap[:, :], scalar=gqc.ap[:, 0:1], op=ALU.mult),
                        ins=(q, gqc), outs=(q,))
                    kb.op("vector", lambda e, q=q: e.tensor_single_scalar(
                        out=q.ap[:, :], in_=q.ap[:, :], scalar=bqlnc.ap[:, 0:1], op=ALU.add),
                        ins=(q, bqlnc), outs=(q,))
                for sp in (range(24) if 'm' in _ATT1_STAGES else []):
                    c, g = sp // 4, sp % 4
                    q = qf[side][c]
                    tp = (96, 0) if g == 3 else None
                    kb.op("tensor", lambda e, q=q, g=g, sp=sp, tp=tp: e.matmul(
                        p_sc.ap[:, :], q.ap[g * 32:(g + 1) * 32, :],
                        ktR[sp].ap[g * 32:(g + 1) * 32, :],
                        start=True, stop=True, tile_position=tp),
                        ins=(q, ktR[sp]), outs=(p_sc,))
                    for sl in range(2):
                        kb.op("vector", lambda e, sp=sp, sl=sl, side=side: e.max(
                            out=t8[side].ap[:, (2 * sp + sl) * 8:(2 * sp + sl + 1) * 8],
                            in_=p_sc.ap[:, sl * 256:(sl + 1) * 256]),
                            ins=(p_sc,), outs=(t8[side],))
                    if _ATT1_MSUB < 2:
                        continue
                    kb.op("vector", lambda e: e.tensor_copy(sc2.ap[:, :], p_sc.ap[:, :]),
                          ins=(p_sc,), outs=(sc2,))
                    kb.op("scalar", lambda e: e.activation(out=sc2.ap[:, :],
                                                           in_=sc2.ap[:, :], func=AF.Exp),
                          ins=(sc2,), outs=(sc2,))
                    for sl_ in range(2):
                        kb.op("vector", lambda e, sp=sp, side=side, sl_=sl_: e.tensor_tensor(
                            out=m2.ap[:, sl_ * 256:(sl_ + 1) * 256],
                            in0=p_sc.ap[:, sl_ * 256:(sl_ + 1) * 256],
                            in1=t8[side].ap[:, (2 * sp + sl_) * 8 + 7:(2 * sp + sl_) * 8 + 8]
                                .to_broadcast([128, 256]),
                            op=ALU.is_ge),
                            ins=(p_sc, t8[side]), outs=(m2,))
                    kb.op("vector", lambda e: e.tensor_mul(sc2.ap[:, :], sc2.ap[:, :],
                                                           m2.ap[:, :]),
                          ins=(sc2, m2), outs=(sc2,))
                    if _ATT1_MSUB < 3:
                        continue
                    for q4 in range(4):
                        kb.op("tensor", lambda e, q4=q4: e.transpose(
                            p_t.ap[:, :], sc2.ap[:, q4 * 128:(q4 + 1) * 128],
                            kb.ident.ap[:, :]),
                            ins=(sc2,), outs=(p_t,))
                        kb.op("vector", lambda e, q4=q4: e.tensor_copy(
                            sck[q4].ap[:, :], p_t.ap[:, :]),
                            ins=(p_t,), outs=(sck[q4],))
                    if _ATT1_MSUB < 4:
                        continue
                    for sl in range(2):
                        s_ = 2 * sp + sl
                        kb.op("scalar", lambda e, s_=s_, side=side: e.activation(
                            out=u8.ap[:, :], in_=t8[side].ap[:, s_ * 8:(s_ + 1) * 8],
                            func=AF.Exp),
                            ins=(t8[side],), outs=(u8,))
                        kb.op("vector", lambda e: e.tensor_reduce(
                            out=rzc.ap[:, :], in_=u8.ap[:, :],
                            axis=mybir.AxisListType.X, op=ALU.add),
                            ins=(u8,), outs=(rzc,))
                        kb.op("vector", lambda e: e.reciprocal(rzc.ap[:, :], rzc.ap[:, :]),
                              ins=(rzc,), outs=(rzc,))
                        kb.op("tensor", lambda e: e.transpose(
                            p_t.ap[0:1, 0:128], rzc.ap[:, :], kb.ident.ap[:, :]),
                            ins=(rzc,), outs=(p_t,))
                        kb.op("vector", lambda e: e.tensor_copy(rzr.ap[:, :],
                                                                p_t.ap[0:1, 0:128]),
                              ins=(p_t,), outs=(rzr,))
                        kb.op("tensor", lambda e: e.matmul(
                            p_q.ap[0:32, :], ones.ap[0:1, 0:32], rzr.ap[:, :],
                            start=True, stop=True),
                            ins=(rzr, ones), outs=(p_q,))
                        kb.mm_group(
                            [(lambda e, s_=s_, sl=sl, cc=cc: e.matmul(
                                p_o2.ap[:, :],
                                vtab.ap[:, (2 * s_ + cc) * 32:(2 * s_ + cc + 1) * 32],
                                sck[2 * sl + cc].ap[:, :],
                                start=(cc == 0), stop=(cc == 1))) for cc in range(2)],
                            ins=(vtab, sck[2 * sl], sck[2 * sl + 1]), outs=(p_o2,))
                        kb.op("scalar", lambda e: e.activation(
                            out=o2s.ap[:, :], in_=p_o2.ap[:, :], func=AF.Identity),
                            ins=(p_o2,), outs=(o2s,))
                        kb.op("vector", lambda e: e.tensor_mul(o2s.ap[:, :], o2s.ap[:, :],
                                                               p_q.ap[0:32, :]),
                              ins=(o2s, p_q), outs=(o2s,))
                        kb.op("tensor", lambda e, g=g, q=q, sl=sl, tp=tp: e.matmul(
                            p_o2.ap[:, :],
                            wres8.ap[g * 32:(g + 1) * 32, sl * 32:(sl + 1) * 32],
                            q.ap[g * 32:(g + 1) * 32, :],
                            start=True, stop=True, tile_position=tp),
                            ins=(q, wres8), outs=(p_o2,))
                        kb.op("vector", lambda e: e.tensor_add(o2s.ap[:, :], o2s.ap[:, :],
                                                               p_o2.ap[:, :]),
                              ins=(o2s, p_o2), outs=(o2s,))
                        kb.op("vector", lambda e: e.tensor_single_scalar(
                            out=o2s.ap[:, :], in_=o2s.ap[:, :],
                            scalar=bres_c.ap[:, 0:1], op=ALU.add),
                            ins=(o2s, bres_c), outs=(o2s,))
                        kb.mm_group(
                            [(lambda e: e.matmul(p_o3.ap[:, :], o2s.ap[:, :],
                                                 wmemT.ap[:, :], start=True, stop=False)),
                             (lambda e: e.matmul(p_o3.ap[:, :], ones.ap[0:1, 0:128],
                                                 bmem_r.ap[:, :], start=False, stop=True))],
                            ins=(o2s, wmemT, bmem_r, ones), outs=(p_o3,))
                        kb.op("scalar", lambda e, s_=s_, side=side: e.activation(
                            out=out3[side].ap[:, s_ * 64:(s_ + 1) * 64], in_=p_o3.ap[:, :],
                            func=AF.Identity),
                            ins=(p_o3,), outs=(out3[side],))
            if 'm' not in _ATT1_STAGES or _ATT1_MSUB < 5:
                for side in range(2):
                    kb.op("vector", lambda e, side=side: e.memset(out3[side].ap[:, :], 0.01),
                          outs=(out3[side],))
            for side in (range(2) if 'p' in _ATT1_STAGES else []):
                for c0 in (0, 2048):
                    seg = out3[side].ap[:, c0:c0 + 1024]
                    kb.op("vector", lambda e, seg=seg: e.tensor_single_scalar(
                        out=dta.ap[:, :], in_=seg, scalar=0.0, op=ALU.min),
                        ins=(out3[side],), outs=(dta,))
                    kb.op("scalar", lambda e: e.activation(out=dta.ap[:, :],
                                                           in_=dta.ap[:, :], func=AF.Exp),
                          ins=(dta,), outs=(dta,))
                    kb.op("vector", lambda e, seg=seg, side=side: e.scalar_tensor_tensor(
                        out=seg, in0=seg, scalar=1.0, in1=dta.ap[:, :],
                        op0=ALU.add, op1=ALU.max),
                        ins=(out3[side], dta), outs=(out3[side],))
            if 'p' not in _ATT1_STAGES:
                for side in range(2):
                    kb.op("vector", lambda e, side=side: e.memset(lo[side].ap[:, :], 0.01),
                          outs=(lo[side],))
            q0 = out3[0].ap[:, 2048:3072]; k0 = out3[0].ap[:, 0:1024]
            q1 = out3[1].ap[:, 2048:3072]; k1 = out3[1].ap[:, 0:1024]
            for (col, aa, bb) in (((0, q0, k0), (1, q1, k0), (2, q1, k1)) if 'p' in _ATT1_STAGES else []):
                kb.op("vector", lambda e, aa=aa, bb=bb: e.tensor_tensor(
                    out=dta.ap[:, :], in0=aa, in1=bb, op=ALU.mult),
                    ins=(out3[0], out3[1]), outs=(dta,))
                kb.op("vector", lambda e, col=col: e.tensor_reduce(
                    out=dred.ap[:, col * 16:(col + 1) * 16],
                    in_=dta.ap[:, :].rearrange("p (h d) -> p h d", h=16),
                    axis=mybir.AxisListType.X, op=ALU.add),
                    ins=(dta,), outs=(dred,))
            if 'p' in _ATT1_STAGES:
                kb.op("vector", lambda e: e.tensor_add(dred.ap[:, 48:64], dred.ap[:, 16:32],
                                                       dred.ap[:, 32:48]),
                      ins=(dred,), outs=(dred,))
                for (src, dst) in ((0, 4), (3, 5)):
                    kb.op("vector", lambda e, src=src, dst=dst: e.tensor_scalar(
                        out=dred.ap[:, dst * 16:(dst + 1) * 16],
                        in0=dred.ap[:, src * 16:(src + 1) * 16],
                        scalar1=1e-5, scalar2=8.0, op0=ALU.add, op1=ALU.mult),
                        ins=(dred,), outs=(dred,))
                    kb.op("vector", lambda e, dst=dst: e.reciprocal(
                        dred.ap[:, dst * 16:(dst + 1) * 16], dred.ap[:, dst * 16:(dst + 1) * 16]),
                        ins=(dred,), outs=(dred,))
                kb.op("vector", lambda e: e.tensor_mul(dred.ap[:, 96:112], dred.ap[:, 0:16],
                                                       dred.ap[:, 64:80]),
                      ins=(dred,), outs=(dred,))
                kb.op("vector", lambda e: e.tensor_mul(dred.ap[:, 16:32], dred.ap[:, 16:32],
                                                       dred.ap[:, 80:96]),
                      ins=(dred,), outs=(dred,))
                kb.op("vector", lambda e: e.tensor_mul(dred.ap[:, 32:48], dred.ap[:, 32:48],
                                                       dred.ap[:, 80:96]),
                      ins=(dred,), outs=(dred,))
                v0 = out3[0].ap[:, 1024:2048].rearrange("p (h d) -> p h d", h=16)
                v1 = out3[1].ap[:, 1024:2048].rearrange("p (h d) -> p h d", h=16)
                kb.op("vector", lambda e, v0=v0: e.tensor_tensor(
                    out=lo[0].ap[:, :].rearrange("p (h d) -> p h d", h=16), in0=v0,
                    in1=dred.ap[:, 96:112].rearrange("p (h o) -> p h o", o=1)
                        .to_broadcast([128, 16, 64]),
                    op=ALU.mult),
                    ins=(out3[0], dred), outs=(lo[0],))
                kb.op("vector", lambda e, v0=v0: e.tensor_tensor(
                    out=lo[1].ap[:, :].rearrange("p (h d) -> p h d", h=16), in0=v0,
                    in1=dred.ap[:, 16:32].rearrange("p (h o) -> p h o", o=1)
                        .to_broadcast([128, 16, 64]),
                    op=ALU.mult),
                    ins=(out3[0], dred), outs=(lo[1],))
                kb.op("vector", lambda e, v1=v1: e.tensor_tensor(
                    out=dta.ap[:, :].rearrange("p (h d) -> p h d", h=16), in0=v1,
                    in1=dred.ap[:, 32:48].rearrange("p (h o) -> p h o", o=1)
                        .to_broadcast([128, 16, 64]),
                    op=ALU.mult),
                    ins=(out3[1], dred), outs=(dta,))
                kb.op("vector", lambda e: e.tensor_add(lo[1].ap[:, :], lo[1].ap[:, :],
                                                       dta.ap[:, :]),
                      ins=(lo[1], dta), outs=(lo[1],))
            for side in range(2):
                t0 = rp * 128 + side * half
                for dc in range(8):
                    kb.op("tensor", lambda e, dc=dc, side=side: e.transpose(
                        p_t.ap[:, :], lo[side].ap[:, dc * 128:(dc + 1) * 128],
                        kb.ident.ap[:, :]),
                        ins=(lo[side],), outs=(p_t,))
                    kb.op("vector", lambda e, dc=dc, side=side: e.tensor_copy(
                        lofm[side][dc].ap[:, :], p_t.ap[:, :]),
                        ins=(p_t,), outs=(lofm[side][dc],))
                for oc in range(8):
                    wt = wo[oc % 2]
                    src = W['woT'].ap[:, oc * 128:(oc + 1) * 128].rearrange(
                        "(a p) h -> p a h", p=128)
                    kb.dma_ap("sync", wt, wt.ap[:, :].rearrange("p (a h) -> p a h", a=8),
                              W['woT'], src)
                    kb.mm_group(
                        [(lambda e, hc=hc, wt=wt, side=side: e.matmul(
                            p_at.ap[:, :], wt.ap[:, hc * 128:(hc + 1) * 128],
                            lofm[side][hc].ap[:, :],
                            start=(hc == 0), stop=(hc == 7))) for hc in range(8)],
                        ins=tuple(lofm[side]) + (wt,), outs=(p_at,))
                    kb.op("vector", lambda e, oc=oc, side=side: e.tensor_add(
                        tg[side][oc].ap[:, :], xh[side][oc].ap[:, :], p_at.ap[:, :]),
                        ins=(xh[side][oc], p_at), outs=(tg[side][oc],))
                emit_ln_fm(kb, tg[side], tg[side], 128, DM, pbank,
                           gam_col=(goc, 0), bet_col=(boc, 0))
                for oc in range(8):
                    kb.op("vector", lambda e, oc=oc, side=side: e.tensor_add(
                        tg[side][oc].ap[:, :], tg[side][oc].ap[:, :], xg[side][oc].ap[:, :]),
                        ins=(tg[side][oc], xg[side][oc]), outs=(tg[side][oc],))
                    kb.dma("sync", x_out, np.s_[oc * 128:(oc + 1) * 128, t0:t0 + 128],
                           tg[side][oc], np.s_[:, :])



def emit_outproj(kb, x_in, y_in, x_out, wpT, ntok, gtok=512):
    ng = ntok // gtok
    with kb.scope():
        yg = [kb.sb([128, gtok]) for _ in range(8)]
        xc = kb.sb([128, gtok])
        wl = [kb.sb([128, 1024]) for _ in range(2)]
        p_o = kb.psum([128, gtok])
        for g in range(ng):
            t0 = g * gtok
            for c in range(8):
                kb.dma("sync", yg[c], np.s_[:, :], y_in,
                       np.s_[c * 128:(c + 1) * 128, t0:t0 + gtok])
            for oc in range(8):
                w = wl[oc % 2]
                src = wpT.ap[:, oc * 128:(oc + 1) * 128].rearrange("(a p) h -> p a h", p=128)
                kb.dma_ap("sync", w, w.ap[:, :].rearrange("p (a h) -> p a h", a=8), wpT, src)
                kb.mm_group(
                    [(lambda e, dc=dc, w=w: e.matmul(
                        p_o.ap[:, :], w.ap[:, dc * 128:(dc + 1) * 128], yg[dc].ap[:, :],
                        start=(dc == 0), stop=(dc == 7))) for dc in range(8)],
                    ins=tuple(yg) + (w,), outs=(p_o,))
                kb.dma("sync", xc, np.s_[:, :], x_in,
                       np.s_[oc * 128:(oc + 1) * 128, t0:t0 + gtok])
                kb.op("vector", lambda e: e.tensor_add(xc.ap[:, :], xc.ap[:, :], p_o.ap[:, :]),
                      ins=(xc, p_o), outs=(xc,))
                kb.dma("sync", x_out, np.s_[oc * 128:(oc + 1) * 128, t0:t0 + gtok],
                       xc, np.s_[:, :])



def _mk_nc():
    return bass.Bass('TRN2', target_bir_lowering=False, debug=False)


def _dram_in(nc, name, shape):
    return TT(nc.dram_tensor(name, shape, F32, kind='ExternalInput').ap())


def _dram_out(nc, name, shape):
    return TT(nc.dram_tensor(name, shape, F32, kind='ExternalOutput').ap())


def build_ffn(ntok, ngroups, gtok):
    nc = _mk_nc()
    x_in = _dram_in(nc, 'x_in', [1024, ntok])
    wfcT = _dram_in(nc, 'wfcT', [1024, 4096])
    wprojT = _dram_in(nc, 'wprojT', [4096, 1024])
    lnwb_in = _dram_in(nc, 'lnwb', [128, 64])
    x_out = _dram_out(nc, 'x_out', [1024, ntok])
    kb = KB(nc)
    wfcq = kb.dram([1024, 4096], name='wfcq')
    wprojq = kb.dram([4096, 1024], name='wprojq')
    lnwb = kb.sb([128, 64], name='lnwb_sb')
    kb.dma("sync", lnwb, np.s_[:, :], lnwb_in, np.s_[:, :])
    emit_quant_w(kb, wfcT, 1024, 4096, wfcq, 4096 * 1024)
    emit_quant_w(kb, wprojT, 4096, 1024, wprojq, 4096 * 1024)
    kb.barrier()
    emit_ffn0(kb, x_in, x_out, wfcq, wprojq, lnwb, ngroups=ngroups, gtok=gtok)
    kb.finalize()
    return nc


def build_att0(Twin, nch):
    nc = _mk_nc()
    x_in = _dram_in(nc, 'x_in', [1024, Twin])
    wqkvT = _dram_in(nc, 'wqkvT', [1024, 768])
    ab = _dram_in(nc, 'ab', [2, 16])
    sin = _dram_in(nc, 'sin', [256, 64])
    yout = _dram_out(nc, 'yout', [256, Twin])
    sout = _dram_out(nc, 'sout', [256, 64])
    kb = KB(nc)
    abcols = kb.sb([128, 8], name='abcols')
    emit_ab(kb, ab, abcols, head0=0)
    emit_attn0_heads(kb, x_in, yout, wqkvT, abcols, Twin, sin, sout, ch0=0, nch=nch)
    kb.finalize()
    return nc


def build_outproj(ntok, gtok):
    nc = _mk_nc()
    x_in = _dram_in(nc, 'x_in', [1024, ntok])
    y_in = _dram_in(nc, 'y_in', [1024, ntok])
    wpT = _dram_in(nc, 'wpT', [1024, 1024])
    x_out = _dram_out(nc, 'x_out', [1024, ntok])
    kb = KB(nc)
    emit_outproj(kb, x_in, y_in, x_out, wpT, ntok, gtok=gtok)
    kb.finalize()
    return nc


def build_moeA(ntok, C):
    nc = _mk_nc()
    x_in = _dram_in(nc, 'x_in', [1024, ntok])
    gwT = _dram_in(nc, 'gwT', [1024, 16])
    xhat_tm = _dram_out(nc, 'xhat_tm', [ntok, 1024])
    selw = _dram_out(nc, 'selw', [ntok, 4])
    selp = _dram_out(nc, 'selp', [ntok, 2])
    rows = _dram_out(nc, 'rows', [2, 2 * ntok])
    cnt = _dram_out(nc, 'cnt', [16, 2])
    srow = _dram_out(nc, 'srow', [1, 2 * ntok])
    kb = KB(nc)
    emit_moe_route(kb, x_in, gwT, xhat_tm, selw, selp, rows, cnt, ntok, C, srow)
    kb.finalize()
    return nc


def build_moeS(ntok_win, C):
    nc = _mk_nc()
    xhat_win = _dram_in(nc, 'xhat_win', [ntok_win, 1024])
    srow_win = _dram_in(nc, 'srow_win', [1, 2 * ntok_win])
    ebuf_in = _dram_in(nc, 'ebuf_in', [16 * C + 128, 1024])
    ebuf = _dram_out(nc, 'ebuf', [16 * C + 128, 1024])
    kb = KB(nc)
    emit_moe_scat(kb, xhat_win, srow_win, ebuf_in, ebuf, C, ntok_win)
    kb.finalize()
    return nc


def build_moeB(ntok, C, CAP_G):
    nc = _mk_nc()
    ebuf = _dram_in(nc, 'ebuf', [16 * C + 128, 1024])
    rows = _dram_in(nc, 'rows', [2, 2 * ntok])
    cnts3 = _dram_in(nc, 'cnts3', [96, 16])
    w1 = _dram_in(nc, 'w1', [16, 1024, 2048])
    w2 = _dram_in(nc, 'w2', [16, 2048, 1024])
    obuf = _dram_out(nc, 'obuf', [16 * C + 128, 1024])
    vrow = _dram_out(nc, 'vrow', [1, 2 * ntok])
    kb = KB(nc)
    emit_moe_experts(kb, rows, cnts3, w1, w2, ntok, C, CAP_G, ebuf, obuf, vrow)
    kb.finalize()
    return nc


def build_moeG(ntok_win, C):
    nc = _mk_nc()
    x_in_win = _dram_in(nc, 'x_in_win', [1024, ntok_win])
    obuf = _dram_in(nc, 'obuf', [16 * C + 128, 1024])
    srow_win = _dram_in(nc, 'srow_win', [1, 2 * ntok_win])
    vrow_win = _dram_in(nc, 'vrow_win', [1, 2 * ntok_win])
    selw_win = _dram_in(nc, 'selw_win', [ntok_win, 4])
    x_out_win = _dram_out(nc, 'x_out_win', [1024, ntok_win])
    kb = KB(nc)
    emit_moe_gath(kb, x_in_win, x_out_win, obuf, srow_win, vrow_win, selw_win,
                  C, ntok_win)
    kb.finalize()
    return nc


def build_att1(ntok, r0, nr_pairs):
    nc = _mk_nc()
    x_in = _dram_in(nc, 'x_in', [1024, ntok])
    W = {}
    for k_, s_ in [('wqT', [1024, 768]), ('bq_cols', [128, 6]), ('gq_col', [128, 1]),
                   ('bqln_col', [128, 1]), ('ktR', [24, 128, 512]),
                   ('vtab', [96 * 128, 32]), ('wres8', [128, 64]), ('bres_col', [32, 1]),
                   ('wmemT', [32, 64]), ('bmem_row', [1, 64]), ('woT', [1024, 1024]),
                   ('gout_cols', [128, 8]), ('bout_cols', [128, 8]),
                   ('E8', [128, 8]), ('E8b', [8, 128])]:
        W[k_] = _dram_in(nc, k_, s_)
    x_out = _dram_out(nc, 'x_out', [1024, ntok])
    kb = KB(nc)
    emit_attn1(kb, x_in, x_out, W, ntok, r0, nr_pairs)
    kb.finalize()
    return nc


def _run(nc, in_maps):
    res = run_bass_kernel_spmd(nc, in_maps, core_ids=list(range(len(in_maps))))
    return res.results


def kernel(L0_attn_x, L0_attn_Wq, L0_attn_Wk, L0_attn_Wv, L0_attn_Wproj,
           L0_attn_alpha_raw, L0_attn_beta_raw, L0_ffn_W_fc, L0_ffn_ln_w,
           L0_ffn_ln_b, L0_ffn_W_proj, L1_attn_Wq, L1_attn_bq, L1_attn_gq,
           L1_attn_bq_ln, L1_attn_keys, L1_attn_values, L1_attn_Wres,
           L1_attn_bres, L1_attn_Wmem, L1_attn_bmem, L1_attn_Wo,
           L1_attn_g_out, L1_attn_b_out, L1_ffn_gate_w, L1_ffn_w1, L1_ffn_w2,
           _ncores=8, _nwin_att0=8, _nwin_att1=8):
    f32 = np.float32
    asn = lambda a: np.ascontiguousarray(np.asarray(a), dtype=f32)
    x0 = asn(L0_attn_x)
    B, T, _ = x0.shape
    ncores = _ncores
    pos = T // ncores
    ntok = 2 * pos
    CAP_G = int(1.25 * (B * T) * 2 / 16)
    C = min(768, max(128, ((ntok * 2 // 16) * 3 // 2 + 127) // 128 * 128))
    gtok = min(512, ntok)
    ngroups = ntok // gtok
    Twin = T // _nwin_att0
    nch_win = Twin // 128
    npair = pos // 128
    nr_pairs_win = npair // _nwin_att1

    def slabs(xf):
        out = []
        for c in range(ncores):
            xc = np.concatenate([xf[b, c * pos:(c + 1) * pos] for b in range(B)], 0)
            out.append(asn(xc.T))
        return out

    def unslabs(xs):
        full = np.empty((B, T, 1024), f32)
        for c in range(ncores):
            xc = np.asarray(xs[c]).T
            for b in range(B):
                full[b, c * pos:(c + 1) * pos] = xc[b * pos:(b + 1) * pos]
        return full

    nc_ffn = build_ffn(ntok, ngroups, gtok)
    nc_att0 = build_att0(Twin, nch_win)
    nc_op = build_outproj(ntok, gtok)
    nwin_moe = max(1, ntok // 2048)
    ntok_moew = ntok // nwin_moe
    nc_moeA = build_moeA(ntok, C)
    nc_moeS = build_moeS(ntok_moew, C)
    nc_moeB = build_moeB(ntok, C, CAP_G)
    nc_moeG = build_moeG(ntok_moew, C)
    ntok_w1 = 2 * nr_pairs_win * 128
    nc_att1 = build_att1(ntok_w1, 0, nr_pairs_win)

    lnwb = np.zeros((128, 64), f32)
    lnwb[:, 0:32] = asn(L0_ffn_ln_w).reshape(32, 128).T
    lnwb[:, 32:64] = asn(L0_ffn_ln_b).reshape(32, 128).T
    ffn_base = {'wfcT': asn(np.asarray(L0_ffn_W_fc).T),
                'wprojT': asn(np.asarray(L0_ffn_W_proj).T), 'lnwb': lnwb}

    def run_ffn(xslabs):
        r = _run(nc_ffn, [dict(ffn_base, x_in=xslabs[c]) for c in range(ncores)])
        return [r[c]['x_out'] for c in range(ncores)]

    xs = slabs(x0)
    xs = run_ffn(xs)
    x1 = unslabs(xs)
    WqT, WkT, WvT = asn(np.asarray(L0_attn_Wq).T), asn(np.asarray(L0_attn_Wk).T), asn(np.asarray(L0_attn_Wv).T)
    ab_full = np.stack([asn(L0_attn_alpha_raw), asn(L0_attn_beta_raw)])
    x1T = [asn(x1[b].T) for b in range(B)]
    y_acc = [np.empty((1024, T), f32) for _ in range(B)]
    s_cur = [np.zeros((256, 64), f32) for _ in range(8)]
    core_bh = [(c // 4, c % 4) for c in range(8)]
    wqkv = []
    for (b, hg) in core_bh:
        hs = np.s_[:, hg * 256:(hg + 1) * 256]
        wqkv.append(asn(np.concatenate([WqT[hs], WkT[hs], WvT[hs]], 1)))
    ab_core = []
    for (b, hg) in core_bh:
        a = np.zeros((2, 16), f32)
        a[:, 0:4] = ab_full[:, 4 * hg:4 * hg + 4]
        ab_core.append(a)
    for w in range(_nwin_att0):
        t0 = w * Twin
        ims = []
        for c, (b, hg) in enumerate(core_bh):
            ims.append({'x_in': asn(x1T[b][:, t0:t0 + Twin]), 'wqkvT': wqkv[c],
                        'ab': ab_core[c], 'sin': s_cur[c]})
        r = _run(nc_att0, ims)
        for c, (b, hg) in enumerate(core_bh):
            y_acc[b][hg * 256:(hg + 1) * 256, t0:t0 + Twin] = r[c]['yout']
            s_cur[c] = asn(r[c]['sout'])
    yslabs = []
    for c in range(ncores):
        yc = np.concatenate([y_acc[b][:, c * pos:(c + 1) * pos] for b in range(B)], 1)
        yslabs.append(asn(yc))
    wpT0 = asn(np.asarray(L0_attn_Wproj).T)
    r = _run(nc_op, [{'x_in': xs[c], 'y_in': yslabs[c], 'wpT': wpT0}
                     for c in range(ncores)])
    xs = [r[c]['x_out'] for c in range(ncores)]
    xs = run_ffn(xs)

    gwT = asn(np.asarray(L1_ffn_gate_w).T)
    w1 = asn(L1_ffn_w1); w2 = asn(L1_ffn_w2)

    ebz = np.zeros((16 * C + 128, 1024), f32)

    def run_moe(xslabs):
        rA = _run(nc_moeA, [{'x_in': xslabs[c], 'gwT': gwT} for c in range(ncores)])
        cnt = [np.asarray(rA[c]['cnt']) for c in range(ncores)]
        eb = [ebz for _ in range(ncores)]
        for w in range(nwin_moe):
            t0 = w * ntok_moew
            ims = []
            for c in range(ncores):
                ims.append({'xhat_win': np.asarray(rA[c]['xhat_tm'])[t0:t0 + ntok_moew],
                            'srow_win': np.asarray(rA[c]['srow'])[:, 2 * t0:2 * (t0 + ntok_moew)],
                            'ebuf_in': eb[c]})
            rS = _run(nc_moeS, ims)
            eb = [rS[c]['ebuf'] for c in range(ncores)]
        ims = []
        for c in range(ncores):
            c3 = np.zeros((96, 16), f32)
            for cc in range(c):
                c3[cc] = cnt[cc][:, 0]
            j = 0
            for cc in range(ncores):
                if cc != c:
                    c3[32 + j] = cnt[cc][:, 0]
                    j += 1
            for cc in range(c):
                c3[64 + cc] = cnt[cc][:, 1] - cnt[cc][:, 0]
            ims.append({'ebuf': eb[c], 'rows': rA[c]['rows'],
                        'cnts3': c3, 'w1': w1, 'w2': w2})
        rB = _run(nc_moeB, ims)
        xo = [np.array(np.asarray(x), copy=True) for x in xslabs]
        for w in range(nwin_moe):
            t0 = w * ntok_moew
            ims = []
            for c in range(ncores):
                ims.append({'x_in_win': np.ascontiguousarray(np.asarray(xslabs[c])[:, t0:t0 + ntok_moew]),
                            'obuf': rB[c]['obuf'],
                            'srow_win': np.asarray(rA[c]['srow'])[:, 2 * t0:2 * (t0 + ntok_moew)],
                            'vrow_win': np.asarray(rB[c]['vrow'])[:, 2 * t0:2 * (t0 + ntok_moew)],
                            'selw_win': np.asarray(rA[c]['selw'])[t0:t0 + ntok_moew]})
            rG = _run(nc_moeG, ims)
            for c in range(ncores):
                xo[c][:, t0:t0 + ntok_moew] = rG[c]['x_out_win']
        return [asn(x) for x in xo]

    xs = run_moe(xs)
    keyscat = np.concatenate([asn(L1_attn_keys), asn(L1_attn_keys)[:16]], 0)
    valuescat = np.concatenate([asn(L1_attn_values), asn(L1_attn_values)[:16]], 0)
    ktR = np.zeros((24, 128, 512), f32)
    for p in range(24):
        for g in range(4):
            for sl in range(2):
                ktR[p, 32 * g + sl * 16:32 * g + sl * 16 + 16, sl * 256:(sl + 1) * 256] = \
                    keyscat[2 * p + sl].T
    vtab = np.zeros((96 * 128, 32), f32)
    for s_ in range(48):
        for cc in range(2):
            vtab[(2 * s_ + cc) * 128:(2 * s_ + cc + 1) * 128] = \
                valuescat[s_][cc * 128:(cc + 1) * 128]
    wres8 = np.zeros((128, 64), f32)
    WresT = asn(np.asarray(L1_attn_Wres).T)
    for g in range(4):
        wres8[32 * g:32 * g + 16, 0:32] = WresT
        wres8[32 * g + 16:32 * g + 32, 32:64] = WresT
    E8 = np.zeros((128, 8), f32)
    for p in range(128):
        E8[p, p // 16] = 1
    att1_base = {
        'wqT': asn(np.asarray(L1_attn_Wq).T),
        'bq_cols': asn(asn(L1_attn_bq).reshape(6, 128).T),
        'gq_col': asn(np.tile(asn(L1_attn_gq), 8)[:, None]),
        'bqln_col': asn(np.tile(asn(L1_attn_bq_ln), 8)[:, None]),
        'ktR': ktR, 'vtab': vtab, 'wres8': wres8,
        'bres_col': asn(asn(L1_attn_bres)[:, None]),
        'wmemT': asn(np.asarray(L1_attn_Wmem).T),
        'bmem_row': asn(asn(L1_attn_bmem)[None, :]),
        'woT': asn(np.asarray(L1_attn_Wo).T),
        'gout_cols': asn(asn(L1_attn_g_out).reshape(8, 128).T),
        'bout_cols': asn(asn(L1_attn_b_out).reshape(8, 128).T),
        'E8': E8, 'E8b': asn(E8.T),
    }
    hw = nr_pairs_win * 128
    xnew = [np.array(x, copy=True) for x in xs]
    for w in range(_nwin_att1):
        t0 = w * hw
        ims = []
        for c in range(ncores):
            sub = np.concatenate([np.asarray(xs[c])[:, t0:t0 + hw],
                                  np.asarray(xs[c])[:, pos + t0:pos + t0 + hw]], 1)
            ims.append(dict(att1_base, x_in=asn(sub)))
        r = _run(nc_att1, ims)
        for c in range(ncores):
            xo = np.asarray(r[c]['x_out'])
            xnew[c][:, t0:t0 + hw] = xo[:, 0:hw]
            xnew[c][:, pos + t0:pos + t0 + hw] = xo[:, hw:2 * hw]
    xs = [asn(x) for x in xnew]
    xs = run_moe(xs)
    out = unslabs(xs)
    return out


def emit_moe_scat(kb, xhat_win, srow_win, ebuf_in, ebuf, C, ntok_win):
    nr = ntok_win // 128
    nbound = 16 * C
    with kb.scope():
        zt = kb.sb([128, 1024])
        nbrows = ebuf.ap.shape[0]
        for r0 in range(0, nbrows, 128):
            r1 = min(nbrows, r0 + 128)
            kb.dma("sync", zt, np.s_[0:r1 - r0, :], ebuf_in, np.s_[r0:r1, :])
            kb.dma("sync", ebuf, np.s_[r0:r1, :], zt, np.s_[0:r1 - r0, :])
        xtm = kb.sb([128, 1024])
        span = kb.sb([1, 256])
        rowt = kb.sb([1, 128])
        scol = kb.sb([128, 2], I32)
        scolf = kb.sb([128, 2])
        p_t = kb.psum([128, 128])
        for r in range(nr):
            kb.dma("sync", xtm, np.s_[:, :], xhat_win, np.s_[r * 128:(r + 1) * 128, :])
            kb.dma("sync", span, np.s_[:, :], srow_win, np.s_[0:1, r * 256:(r + 1) * 256])
            for k in range(2):
                kb.op("vector", lambda e, k=k: e.tensor_copy(
                    rowt.ap[:, :],
                    span.ap[0:1, :].rearrange("o (t k) -> o t k", k=2)[:, :, k]),
                    ins=(span,), outs=(rowt,))
                kb.op("tensor", lambda e: e.transpose(
                    p_t.ap[:, 0:1], rowt.ap[:, :], kb.ident.ap[0:1, 0:1]),
                    ins=(rowt,), outs=(p_t,))
                kb.op("vector", lambda e, k=k: e.tensor_copy(scolf.ap[:, k:k + 1],
                                                             p_t.ap[:, 0:1]),
                      ins=(p_t,), outs=(scolf,))
            kb.op("vector", lambda e: e.tensor_copy(scol.ap[:, :], scolf.ap[:, :]),
                  ins=(scolf,), outs=(scol,))
            for k in range(2):
                kb.op("gpsimd", lambda e, k=k: e.indirect_dma_start(
                    out=ebuf.ap[:, :],
                    out_offset=bass.IndirectOffsetOnAxis(ap=scol.ap[:, k:k + 1], axis=0),
                    in_=xtm.ap[:, :], in_offset=None,
                    bounds_check=nbound, oob_is_err=False),
                    ins=(xtm, scol), outs=(ebuf,), dma=True)


def emit_moe_gath(kb, x_in_win, x_out_win, obuf, srow_win, vrow_win, selw_win,
                  C, ntok_win):
    nr = ntok_win // 128
    nbound = 16 * C
    with kb.scope():
        span = kb.sb([1, 256]); vspan = kb.sb([1, 256])
        rowt = kb.sb([1, 128])
        scol = kb.sb([128, 2], I32); scolf = kb.sb([128, 2])
        wv = kb.sb([128, 2]); vcol = kb.sb([128, 2])
        g0 = kb.sb([128, 1024]); g1 = kb.sb([128, 1024])
        ytm = kb.sb([128, 1024])
        xchunk = kb.sb([128, 128])
        p_t = kb.psum([128, 128])
        for r in range(nr):
            kb.dma("sync", span, np.s_[:, :], srow_win, np.s_[0:1, r * 256:(r + 1) * 256])
            kb.dma("sync", vspan, np.s_[:, :], vrow_win, np.s_[0:1, r * 256:(r + 1) * 256])
            for k in range(2):
                kb.op("vector", lambda e, k=k: e.tensor_copy(
                    rowt.ap[:, :],
                    span.ap[0:1, :].rearrange("o (t k) -> o t k", k=2)[:, :, k]),
                    ins=(span,), outs=(rowt,))
                kb.op("tensor", lambda e: e.transpose(
                    p_t.ap[:, 0:1], rowt.ap[:, :], kb.ident.ap[0:1, 0:1]),
                    ins=(rowt,), outs=(p_t,))
                kb.op("vector", lambda e, k=k: e.tensor_copy(scolf.ap[:, k:k + 1],
                                                             p_t.ap[:, 0:1]),
                      ins=(p_t,), outs=(scolf,))
                kb.op("vector", lambda e, k=k: e.tensor_copy(
                    rowt.ap[:, :],
                    vspan.ap[0:1, :].rearrange("o (t k) -> o t k", k=2)[:, :, k]),
                    ins=(vspan,), outs=(rowt,))
                kb.op("tensor", lambda e: e.transpose(
                    p_t.ap[:, 0:1], rowt.ap[:, :], kb.ident.ap[0:1, 0:1]),
                    ins=(rowt,), outs=(p_t,))
                kb.op("vector", lambda e, k=k: e.tensor_copy(vcol.ap[:, k:k + 1],
                                                             p_t.ap[:, 0:1]),
                      ins=(p_t,), outs=(vcol,))
            kb.op("vector", lambda e: e.tensor_copy(scol.ap[:, :], scolf.ap[:, :]),
                  ins=(scolf,), outs=(scol,))
            kb.dma("sync", wv, np.s_[:, :], selw_win, np.s_[r * 128:(r + 1) * 128, 2:4])
            kb.op("vector", lambda e: e.tensor_mul(wv.ap[:, :], wv.ap[:, :], vcol.ap[:, :]),
                  ins=(wv, vcol), outs=(wv,))
            kb.op("vector", lambda e: e.memset(g0.ap[:, :], 0.0), outs=(g0,))
            kb.op("vector", lambda e: e.memset(g1.ap[:, :], 0.0), outs=(g1,))
            kb.op("gpsimd", lambda e: e.indirect_dma_start(
                out=g0.ap[:, :], out_offset=None, in_=obuf.ap[:, :],
                in_offset=bass.IndirectOffsetOnAxis(ap=scol.ap[:, 0:1], axis=0),
                bounds_check=nbound, oob_is_err=False),
                ins=(obuf, scol), outs=(g0,), dma=True)
            kb.op("gpsimd", lambda e: e.indirect_dma_start(
                out=g1.ap[:, :], out_offset=None, in_=obuf.ap[:, :],
                in_offset=bass.IndirectOffsetOnAxis(ap=scol.ap[:, 1:2], axis=0),
                bounds_check=nbound, oob_is_err=False),
                ins=(obuf, scol), outs=(g1,), dma=True)
            kb.op("vector", lambda e: e.tensor_single_scalar(
                out=ytm.ap[:, :], in_=g0.ap[:, :], scalar=wv.ap[:, 0:1], op=ALU.mult),
                ins=(g0, wv), outs=(ytm,))
            kb.op("vector", lambda e: e.tensor_single_scalar(
                out=g1.ap[:, :], in_=g1.ap[:, :], scalar=wv.ap[:, 1:2], op=ALU.mult),
                ins=(g1, wv), outs=(g1,))
            kb.op("vector", lambda e: e.tensor_add(ytm.ap[:, :], ytm.ap[:, :], g1.ap[:, :]),
                  ins=(ytm, g1), outs=(ytm,))
            for dc in range(8):
                kb.op("tensor", lambda e, dc=dc: e.transpose(
                    p_t.ap[:, :], ytm.ap[:, dc * 128:(dc + 1) * 128], kb.ident.ap[:, :]),
                    ins=(ytm,), outs=(p_t,))
                kb.dma("sync", xchunk, np.s_[:, :], x_in_win,
                       np.s_[dc * 128:(dc + 1) * 128, r * 128:(r + 1) * 128])
                kb.op("vector", lambda e: e.scalar_tensor_tensor(
                    out=xchunk.ap[:, :], in0=p_t.ap[:, :], scalar=0.5,
                    in1=xchunk.ap[:, :], op0=ALU.mult, op1=ALU.add),
                    ins=(p_t, xchunk), outs=(xchunk,))
                kb.dma("sync", x_out_win,
                       np.s_[dc * 128:(dc + 1) * 128, r * 128:(r + 1) * 128],
                       xchunk, np.s_[:, :])


# version 07
# speedup vs baseline: 1.1801x; 1.1801x over previous
import numpy as np
from contextlib import ExitStack, contextmanager

import concourse.bass as bass
import concourse.mybir as mybir
from concourse.bass_utils import run_bass_kernel_spmd

F32 = mybir.dt.float32
I32 = mybir.dt.int32
U32 = mybir.dt.uint32
BF16 = mybir.dt.bfloat16
AF = mybir.ActivationFunctionType
ALU = mybir.AluOpType

ENGINES = ("sync", "tensor", "vector", "scalar", "gpsimd")

NTOK = 4096
DM = 1024


class Sem:
    __slots__ = ("h", "n", "name")

    def __init__(self, kb, name):
        self.h = kb.es.enter_context(kb.nc.semaphore(name))
        self.n = 0
        self.name = name


class TT:
    __slots__ = ("ap", "w", "rs", "name")

    def __init__(self, ap, name=""):
        self.ap = ap
        self.w = None
        self.rs = []
        self.name = name

    def __getitem__(self, idx):
        return self.ap[idx]


class KB:
    def __init__(self, nc):
        self.nc = nc
        self.es = ExitStack()
        self.prog = {e: [] for e in ENGINES}
        self.esem = {e: Sem(self, f"es_{e}") for e in ENGINES}
        NDS = 8
        self.dsem = {e: [Sem(self, f"ds_{e}{i}") for i in range(NDS)]
                     for e in ("sync", "scalar", "gpsimd", "vector")}
        self.drr = {e: 0 for e in ("sync", "scalar", "gpsimd", "vector")}
        self.waited = {e: {} for e in ENGINES}
        self._name_i = 0
        self.cst_ones = self.sb([128, 128], name="cst_ones")
        self.op("vector", lambda e: e.memset(self.cst_ones.ap[:, :], 1.0),
                outs=(self.cst_ones,))
        self.ident = self.sb([128, 128], name="ident128")
        self.op("gpsimd", lambda e: e.affine_select(
            out=self.ident.ap[:, :], in_=self.cst_ones.ap[:, :], pattern=[[-1, 128]],
            compare_op=ALU.is_equal, fill=0.0, base=0, channel_multiplier=1),
            ins=(self.cst_ones,), outs=(self.ident,))
        self.barrier()

    def name(self, p="t"):
        self._name_i += 1
        return f"{p}{self._name_i}"

    def sb(self, shape, dt=F32, name=None):
        h = self.es.enter_context(self.nc.sbuf_tensor(name or self.name("sb"), shape, dt))
        return TT(h, name or "sb")

    def dram(self, shape, dt=F32, name=None, kind="Internal"):
        h = self.nc.dram_tensor(name or self.name("dr"), shape, dt, kind=kind)
        return TT(h.ap(), name or "dr")

    @contextmanager
    def scope(self):
        es = ExitStack()
        old, self.es = self.es, es
        try:
            yield
        finally:
            self.barrier()
            self.es = old
            es.close()

    def psum(self, shape, dt=F32, name=None):
        h = self.es.enter_context(self.nc.psum_tensor(name or self.name("ps"), shape, dt))
        return TT(h, name or "ps")

    def _wait(self, eng, tok):
        sem, val = tok
        if self.waited[eng].get(sem, 0) >= val:
            return
        self.waited[eng][sem] = val
        self.prog[eng].append(lambda e, s=sem.h, v=val: e.wait_ge(s, v))

    def op(self, eng, fn, ins=(), outs=(), dma=False, extra=()):
        deps = []
        for t in ins:
            if t is not None and t.w is not None:
                deps.append(t.w)
        for t in outs:
            if t is not None:
                if t.w is not None:
                    deps.append(t.w)
                deps.extend(t.rs)
        deps.extend(extra)
        if dma:
            pool = self.dsem[eng]
            sem = pool[self.drr[eng] % len(pool)]
            self.drr[eng] += 1
            if sem.n:
                deps.append((sem, sem.n))
            inc = 16
        else:
            sem = self.esem[eng]
            inc = 1
        for d in deps:
            self._wait(eng, d)
        sem.n += inc
        tok = (sem, sem.n)
        self.prog[eng].append(lambda e, f=fn, s=sem.h, i=inc: f(e).then_inc(s, i))
        for t in outs:
            if t is not None:
                t.w = tok
                t.rs = []
        for t in ins:
            if t is not None:
                t.rs.append(tok)
        return tok

    def mm_group(self, fns, ins=(), outs=(), extra=()):
        deps = []
        for t in ins:
            if t is not None and t.w is not None:
                deps.append(t.w)
        for t in outs:
            if t is not None:
                if t.w is not None:
                    deps.append(t.w)
                deps.extend(t.rs)
        deps.extend(extra)
        for d in deps:
            self._wait("tensor", d)
        sem = self.esem["tensor"]
        for j, fn in enumerate(fns):
            sem.n += 1
            if j == len(fns) - 1:
                self.prog["tensor"].append(
                    lambda e, f=fn, s=sem.h, v=1: f(e).then_inc(s, v))
            else:
                self.prog["tensor"].append(
                    lambda e, f=fn, s=sem.h, v=1: f(e).then_inc(s, v))
        tok = (sem, sem.n)
        for t in outs:
            if t is not None:
                t.w = tok
                t.rs = []
        for t in ins:
            if t is not None:
                t.rs.append(tok)
        return tok

    def barrier(self):
        toks = [(s, s.n) for s in self.esem.values()]
        for pool in self.dsem.values():
            toks += [(s, s.n) for s in pool]
        for eng in ENGINES:
            for t in toks:
                self._wait(eng, t)

    def finalize(self):
        allsems = [s for pool in self.dsem.values() for s in pool]
        allsems += [self.esem[e] for e in ENGINES if e != "sync"]
        for s in allsems:
            if s.n:
                self._wait("sync", (s, s.n))
        with self.nc.Block() as block:
            block.sync(lambda e: [f(e) for f in self.prog["sync"]])
            block.tensor(lambda e: [f(e) for f in self.prog["tensor"]])
            block.vector(lambda e: [f(e) for f in self.prog["vector"]])
            block.scalar(lambda e: [f(e) for f in self.prog["scalar"]])
            block.gpsimd(lambda e: [f(e) for f in self.prog["gpsimd"]])
        self.es.close()

    def dma(self, eng, dst, dst_idx, src, src_idx, ins=(), outs=()):
        return self.op(eng,
                       lambda e: e.dma_start(out=dst.ap[dst_idx], in_=src.ap[src_idx]),
                       ins=(src,) + tuple(ins), outs=(dst,) + tuple(outs), dma=True)

    def dma_ap(self, eng, dst_tt, dst_ap, src_tt, src_ap):
        return self.op(eng, lambda e: e.dma_start(out=dst_ap, in_=src_ap),
                       ins=(src_tt,), outs=(dst_tt,), dma=True)


def ceil_div(a, b):
    return (a + b - 1) // b


CST_COLS = 8


def host_consts():
    c = np.zeros((128, CST_COLS), np.float32)
    c[:, 0] = 1.0
    return c



def emit_quant_w(kb, wT_dram, n_in, n_out, stash_dram, nel):
    nr = n_in // 128
    with kb.scope():
        wt = [kb.sb([128, n_out]) for _ in range(2)]
        acc = kb.sb([128, 1])
        part = kb.sb([128, 1])
        tot_ps = kb.psum([1, 1])
        tot = kb.sb([1, 1])
        thr = kb.sb([128, 1])
        ones = kb.cst_ones
        sgn = kb.sb([128, n_out])
        msk = kb.sb([128, n_out])
        kb.op("vector", lambda e: e.memset(acc.ap[:, :], 0.0), outs=(acc,))
        for r in range(nr):
            w = wt[r % 2]
            kb.dma("sync", w, np.s_[:, :], wT_dram, np.s_[r * 128:(r + 1) * 128, :])
            kb.op("vector",
                  lambda e, w=w: e.reduce_sum(part.ap[:, :], w.ap[:, :],
                                              axis=mybir.AxisListType.X,
                                              apply_absolute_value=True),
                  ins=(w,), outs=(part,))
            kb.op("vector", lambda e: e.tensor_add(acc.ap[:, :], acc.ap[:, :], part.ap[:, :]),
                  ins=(part, acc), outs=(acc,))
        kb.op("tensor", lambda e: e.matmul(tot_ps.ap[:, :], acc.ap[:, :], ones.ap[:, 0:1],
                                           start=True, stop=True),
              ins=(acc, ones), outs=(tot_ps,))
        kb.op("vector", lambda e: e.tensor_scalar(
            tot.ap[:, :], tot_ps.ap[:, :], 1.0 / nel, 1e-6, op0=ALU.mult, op1=ALU.add),
            ins=(tot_ps,), outs=(tot,))
        kb.op("vector", lambda e: e.tensor_scalar_mul(tot.ap[:, :], tot.ap[:, :], 0.5),
              ins=(tot,), outs=(tot,))
        thr_ps = kb.psum([128, 1])
        kb.op("tensor", lambda e: e.matmul(thr_ps.ap[:, :], ones.ap[0:1, 0:128], tot.ap[:, :],
                                           start=True, stop=True),
              ins=(tot, ones), outs=(thr_ps,))
        kb.op("vector", lambda e: e.tensor_copy(thr.ap[:, :], thr_ps.ap[:, :]),
              ins=(thr_ps,), outs=(thr,))
        wq_b = kb.sb([128, n_out], BF16)
        for r in range(nr):
            w = wt[r % 2]
            kb.dma("sync", w, np.s_[:, :], wT_dram, np.s_[r * 128:(r + 1) * 128, :])
            kb.op("scalar", lambda e, w=w: e.activation(out=sgn.ap[:, :], in_=w.ap[:, :],
                                                        func=AF.Sign),
                  ins=(w,), outs=(sgn,))
            kb.op("scalar", lambda e, w=w: e.activation(out=msk.ap[:, :], in_=w.ap[:, :],
                                                        func=AF.Abs),
                  ins=(w,), outs=(msk,))
            kb.op("vector", lambda e: e.tensor_single_scalar(
                out=msk.ap[:, :], in_=msk.ap[:, :], scalar=thr.ap[:, :], op=ALU.is_gt),
                ins=(msk, thr), outs=(msk,))
            kb.op("vector", lambda e: e.tensor_mul(wq_b.ap[:, :], sgn.ap[:, :], msk.ap[:, :]),
                  ins=(sgn, msk), outs=(wq_b,))
            kb.dma("sync", stash_dram, np.s_[r * 128:(r + 1) * 128, :], wq_b, np.s_[:, :])


def emit_ln_fm(kb, xg, xhat, ntok, dim, pbank, gam_col=None, bet_col=None, eps=1e-5):
    nch = len(xg)
    ones = kb.cst_ones
    sx_ps, sxx_ps, bc_ps = pbank["sx"], pbank["sxx"], pbank["bc"]
    sq = pbank["sq"]
    rows = pbank["rows"]
    sx, sxx, arow, brow = rows
    kb.mm_group(
        [(lambda e, c=c: e.matmul(sx_ps.ap[:, :], ones.ap[:, 0:1], xg[c].ap[:, :],
                                  start=(c == 0), stop=(c == nch - 1))) for c in range(nch)],
        ins=tuple(xg) + (ones,), outs=(sx_ps,))
    for c in range(nch):
        s = sq[c % 2]
        kb.op("scalar", lambda e, c=c, s=s: e.activation(out=s.ap[:, :], in_=xg[c].ap[:, :],
                                                         func=AF.Square),
              ins=(xg[c],), outs=(s,))
        kb.mm_group(
            [lambda e, c=c, s=s: e.matmul(sxx_ps.ap[:, :], ones.ap[:, 0:1], s.ap[:, :],
                                          start=(c == 0), stop=(c == nch - 1))],
            ins=(s, ones), outs=(sxx_ps,) if c == nch - 1 else (sxx_ps,))
    kb.op("vector", lambda e: e.tensor_scalar_mul(sx.ap[:, :], sx_ps.ap[:, :], 1.0 / dim),
          ins=(sx_ps,), outs=(sx,))
    kb.op("vector", lambda e: e.tensor_scalar_mul(sxx.ap[:, :], sxx_ps.ap[:, :], 1.0 / dim),
          ins=(sxx_ps,), outs=(sxx,))
    kb.op("vector", lambda e: e.scalar_tensor_tensor(
        out=arow.ap[:, :], in0=sx.ap[:, :], scalar=-1.0, in1=sx.ap[:, :],
        op0=ALU.mult, op1=ALU.mult),
        ins=(sx,), outs=(arow,))
    kb.op("vector", lambda e: e.tensor_add(arow.ap[:, :], arow.ap[:, :], sxx.ap[:, :]),
          ins=(arow, sxx), outs=(arow,))
    kb.op("vector", lambda e: e.tensor_scalar_add(arow.ap[:, :], arow.ap[:, :], eps),
          ins=(arow,), outs=(arow,))
    kb.op("scalar", lambda e: e.activation(out=arow.ap[:, :], in_=arow.ap[:, :], func=AF.Sqrt),
          ins=(arow,), outs=(arow,))
    kb.op("vector", lambda e: e.reciprocal(arow.ap[:, :], arow.ap[:, :]),
          ins=(arow,), outs=(arow,))
    kb.op("vector", lambda e: e.scalar_tensor_tensor(
        out=brow.ap[:, :], in0=sx.ap[:, :], scalar=-1.0, in1=arow.ap[:, :],
        op0=ALU.mult, op1=ALU.mult),
        ins=(sx, arow), outs=(brow,))
    abc = pbank["abc"]
    for row, dstt in ((arow, abc[0]), (brow, abc[1])):
        kb.op("tensor", lambda e, row=row: e.matmul(
            bc_ps.ap[:, :], ones.ap[0:1, 0:128], row.ap[:, :], start=True, stop=True),
            ins=(row, ones), outs=(bc_ps,))
        kb.op("vector", lambda e, d=dstt: e.tensor_copy(d.ap[:, :], bc_ps.ap[:, :]),
              ins=(bc_ps,), outs=(dstt,))
    for c in range(nch):
        kb.op("vector", lambda e, c=c: e.tensor_mul(
            xhat[c].ap[:, :], xg[c].ap[:, :], abc[0].ap[:, :]),
            ins=(xg[c], abc[0]), outs=(xhat[c],))
        kb.op("vector", lambda e, c=c: e.tensor_add(
            xhat[c].ap[:, :], xhat[c].ap[:, :], abc[1].ap[:, :]),
            ins=(xhat[c], abc[1]), outs=(xhat[c],))
        if gam_col is not None:
            g, gc = gam_col
            b, bc_ = bet_col
            kb.op("vector", lambda e, c=c, g=g, gc=gc: e.tensor_single_scalar(
                out=xhat[c].ap[:, :], in_=xhat[c].ap[:, :],
                scalar=g.ap[:, gc + c:gc + c + 1], op=ALU.mult),
                ins=(xhat[c], g), outs=(xhat[c],))
            kb.op("vector", lambda e, c=c, b=b, bc_=bc_: e.tensor_single_scalar(
                out=xhat[c].ap[:, :], in_=xhat[c].ap[:, :],
                scalar=b.ap[:, bc_ + c:bc_ + c + 1], op=ALU.add),
                ins=(xhat[c], b), outs=(xhat[c],))


def emit_ffn0(kb, x_in, x_out, wfcq, wprojq, lnwb, ngroups=8, gtok=512):
    NH = 4096
    with kb.scope():
        xg = [kb.sb([128, gtok]) for _ in range(8)]
        xh = [kb.sb([128, gtok], BF16) for _ in range(8)]
        hh = [kb.sb([128, gtok]) for _ in range(32)]
        hb = [kb.sb([128, gtok], BF16) for _ in range(32)]
        sq = [kb.sb([128, gtok]) for _ in range(2)]
        rows = [kb.sb([1, gtok]) for _ in range(4)]
        abc = [kb.sb([128, gtok]) for _ in range(2)]
        wl = [kb.sb([128, 1024], BF16) for _ in range(2)]
        wl2 = [kb.sb([128, 4096], BF16) for _ in range(2)]
        ob = kb.sb([128, gtok])
        sx_ps = kb.psum([1, gtok])
        sxx_ps = kb.psum([1, gtok])
        bc_ps = kb.psum([128, gtok])
        h_ps = kb.psum([128, gtok])
        o_ps = kb.psum([128, gtok])
        pbank = {"sx": sx_ps, "sxx": sxx_ps, "bc": bc_ps, "sq": sq, "rows": rows, "abc": abc}
        for g in range(ngroups):
            t0 = g * gtok
            for c in range(8):
                kb.dma("sync", xg[c], np.s_[:, :], x_in,
                       np.s_[c * 128:(c + 1) * 128, t0:t0 + gtok])
            emit_ln_fm(kb, xg, xh, gtok, DM, pbank)
            for hc in range(32):
                w = wl[hc % 2]
                src = wfcq.ap[:, hc * 128:(hc + 1) * 128].rearrange(
                    "(a p) h -> p a h", p=128)
                dst = w.ap[:, :].rearrange("p (a h) -> p a h", a=8)
                kb.dma_ap("sync", w, dst, wfcq, src)
                kb.mm_group(
                    [(lambda e, dc=dc, w=w: e.matmul(
                        h_ps.ap[:, :], w.ap[:, dc * 128:(dc + 1) * 128], xh[dc].ap[:, :],
                        start=(dc == 0), stop=(dc == 7))) for dc in range(8)],
                    ins=tuple(xh) + (w,), outs=(h_ps,))
                kb.op("scalar", lambda e, hc=hc: e.activation(
                    out=hh[hc].ap[:, :], in_=h_ps.ap[:, :], func=AF.Gelu),
                    ins=(h_ps,), outs=(hh[hc],))
            emit_ln_fm(kb, hh, hb, gtok, NH, pbank, gam_col=(lnwb, 0), bet_col=(lnwb, 32))
            for oc in range(8):
                w = wl2[oc % 2]
                src = wprojq.ap[:, oc * 128:(oc + 1) * 128].rearrange(
                    "(a p) h -> p a h", p=128)
                dst = w.ap[:, :].rearrange("p (a h) -> p a h", a=32)
                kb.dma_ap("sync", w, dst, wprojq, src)
                kb.mm_group(
                    [(lambda e, hc=hc, w=w: e.matmul(
                        o_ps.ap[:, :], w.ap[:, hc * 128:(hc + 1) * 128], hb[hc].ap[:, :],
                        start=(hc == 0), stop=(hc == 31))) for hc in range(32)],
                    ins=tuple(hb) + (w,), outs=(o_ps,))
                kb.op("vector", lambda e, oc=oc: e.scalar_tensor_tensor(
                    out=ob.ap[:, :], in0=o_ps.ap[:, :], scalar=0.5, in1=xg[oc].ap[:, :],
                    op0=ALU.mult, op1=ALU.add),
                    ins=(o_ps, xg[oc]), outs=(ob,))
                kb.dma("sync", x_out, np.s_[oc * 128:(oc + 1) * 128, t0:t0 + gtok],
                       ob, np.s_[:, :])



def emit_attn0_heads(kb, xfull, yout, wqkvT, abcols, T, s_in, s_out, ch0, nch,
                     skip_ln=False):
    L = 128
    with kb.scope():
        Mlo = kb.sb([128, 128]); Mup = kb.sb([128, 128])
        Ione = kb.sb([128, 128]); Itwo = kb.sb([128, 128])
        ones = kb.cst_ones
        kb.op("gpsimd", lambda e: e.affine_select(
            out=Mlo.ap[:, :], in_=ones.ap[:, :], pattern=[[-1, 128]],
            compare_op=ALU.is_gt, fill=0.0, base=0, channel_multiplier=1),
            ins=(ones,), outs=(Mlo,))
        kb.op("gpsimd", lambda e: e.affine_select(
            out=Mup.ap[:, :], in_=ones.ap[:, :], pattern=[[1, 128]],
            compare_op=ALU.is_gt, fill=0.0, base=0, channel_multiplier=-1),
            ins=(ones,), outs=(Mup,))
        kb.op("gpsimd", lambda e: e.affine_select(
            out=Ione.ap[:, :], in_=ones.ap[:, :], pattern=[[-1, 128]],
            compare_op=ALU.is_equal, fill=0.0, base=0, channel_multiplier=1),
            ins=(ones,), outs=(Ione,))
        kb.op("vector", lambda e: e.tensor_scalar_mul(Itwo.ap[:, :], Ione.ap[:, :], 2.0),
              ins=(Ione,), outs=(Itwo,))
        S = [kb.sb([64, 64], name=f"S{h}") for h in range(4)]
        for h in range(4):
            kb.dma("sync", S[h], np.s_[:, :], s_in, np.s_[h * 64:(h + 1) * 64, :])
        wq = kb.sb([128, 8 * 256]); wk = kb.sb([128, 8 * 256]); wv = kb.sb([128, 8 * 256])
        for w, c0 in ((wq, 0), (wk, 256), (wv, 512)):
            src = wqkvT.ap[:, c0:c0 + 256].rearrange("(a p) h -> p a h", p=128)
            dst = w.ap[:, :].rearrange("p (a h) -> p a h", a=8)
            kb.dma_ap("sync", w, dst, wqkvT, src)
        xg = [kb.sb([128, L]) for _ in range(8)]
        xh = [kb.sb([128, L]) for _ in range(8)]
        sq = [kb.sb([128, L]) for _ in range(2)]
        rows = [kb.sb([1, L]) for _ in range(4)]
        abc = [kb.sb([128, L]) for _ in range(2)]
        qfm = [kb.sb([128, L]) for _ in range(2)]
        kfm = [kb.sb([128, L]) for _ in range(2)]
        krm = kb.sb([128, 256])
        vrm = kb.sb([128, 256])
        nrm = kb.sb([1, 512])
        rcol = kb.sb([128, 4])
        Xb = [kb.sb([128, 128]) for _ in range(4)]
        Xtb = [kb.sb([128, 128]) for _ in range(4)]
        Msb = [kb.sb([128, 128]) for _ in range(4)]
        Mtb = [kb.sb([128, 128]) for _ in range(4)]
        Ps = kb.sb([128, 128])
        kf0 = kb.sb([64, 128])
        qf0 = kb.sb([64, 128])
        R2 = kb.sb([128, 64])
        Zs = kb.sb([128, 64])
        Us = kb.sb([128, 128])
        yev = kb.sb([128, 128])
        p_sx = kb.psum([1, L]); p_sxx = kb.psum([1, L]); p_bc = kb.psum([128, L])
        p_a = kb.psum([128, 256])
        p_b = kb.psum([128, 128])
        p_y = kb.psum([128, 128])
        p_s = kb.psum([64, 64])
        p_n = kb.psum([1, 512])
        pbank = {"sx": p_sx, "sxx": p_sxx, "bc": p_bc, "sq": sq, "rows": rows, "abc": abc}
        for ci in range(nch):
            ch = ch0 + ci
            t0 = ch * L
            for c in range(8):
                kb.dma("sync", xg[c], np.s_[:, :], xfull,
                       np.s_[c * 128:(c + 1) * 128, t0:t0 + L])
            if skip_ln:
                for c in range(8):
                    kb.op("vector", lambda e, c=c: e.tensor_copy(xh[c].ap[:, :], xg[c].ap[:, :]),
                          ins=(xg[c],), outs=(xh[c],))
            else:
                emit_ln_fm(kb, xg, xh, L, DM, pbank)
            for half in range(2):
                kb.mm_group(
                    [(lambda e, dc=dc, half=half: e.matmul(
                        p_a.ap[:, 0:L], wq.ap[:, dc * 256 + half * 128:dc * 256 + (half + 1) * 128],
                        xh[dc].ap[:, :], start=(dc == 0), stop=(dc == 7))) for dc in range(8)],
                    ins=tuple(xh) + (wq,), outs=(p_a,))
                kb.op("vector", lambda e, half=half: e.tensor_copy(qfm[half].ap[:, :], p_a.ap[:, 0:L]),
                      ins=(p_a,), outs=(qfm[half],))
                kb.mm_group(
                    [(lambda e, dc=dc, half=half: e.matmul(
                        p_a.ap[:, 0:L], wk.ap[:, dc * 256 + half * 128:dc * 256 + (half + 1) * 128],
                        xh[dc].ap[:, :], start=(dc == 0), stop=(dc == 7))) for dc in range(8)],
                    ins=tuple(xh) + (wk,), outs=(p_a,))
                kb.op("vector", lambda e, half=half: e.tensor_copy(kfm[half].ap[:, :], p_a.ap[:, 0:L]),
                      ins=(p_a,), outs=(kfm[half],))
            kb.mm_group(
                [(lambda e, dc=dc: e.matmul(
                    p_a.ap[:, 0:256],
                    xh[dc].ap[:, :],
                    wv.ap[:, dc * 256:(dc + 1) * 256],
                    start=(dc == 0), stop=(dc == 7))) for dc in range(8)],
                ins=tuple(xh) + (wv,), outs=(p_a,))
            kb.op("vector", lambda e: e.tensor_copy(vrm.ap[:, :], p_a.ap[:, 0:256]),
                  ins=(p_a,), outs=(vrm,))
            kb.mm_group(
                [(lambda e, dc=dc: e.matmul(
                    p_a.ap[:, 0:256],
                    xh[dc].ap[:, :],
                    wk.ap[:, dc * 256:(dc + 1) * 256],
                    start=(dc == 0), stop=(dc == 7))) for dc in range(8)],
                ins=tuple(xh) + (wk,), outs=(p_a,))
            kb.op("vector", lambda e: e.tensor_copy(krm.ap[:, :], p_a.ap[:, 0:256]),
                  ins=(p_a,), outs=(krm,))
            for half in range(2):
                s = sq[0]
                kb.op("scalar", lambda e, half=half, s=s: e.activation(
                    out=s.ap[:, :], in_=kfm[half].ap[:, :], func=AF.Square),
                    ins=(kfm[half],), outs=(s,))
                for sub in range(2):
                    h = half * 2 + sub
                    kb.op("tensor", lambda e, s=s, sub=sub, h=h: e.matmul(
                        p_n.ap[:, h * 128:(h + 1) * 128],
                        ones.ap[sub * 64:(sub + 1) * 64, 0:1],
                        s.ap[sub * 64:(sub + 1) * 64, :],
                        start=True, stop=True),
                        ins=(s, ones), outs=(p_n,))
            kb.op("scalar", lambda e: e.activation(out=nrm.ap[:, :], in_=p_n.ap[:, :],
                                                   func=AF.Sqrt),
                  ins=(p_n,), outs=(nrm,))
            kb.op("vector", lambda e: e.tensor_scalar_max(nrm.ap[:, :], nrm.ap[:, :], 1e-12),
                  ins=(nrm,), outs=(nrm,))
            kb.op("vector", lambda e: e.reciprocal(nrm.ap[:, :], nrm.ap[:, :]),
                  ins=(nrm,), outs=(nrm,))
            for half in range(2):
                for sub in range(2):
                    h = half * 2 + sub
                    kb.op("tensor", lambda e, h=h, sub=sub: e.matmul(
                        p_bc.ap[sub * 64:(sub + 1) * 64, :],
                        ones.ap[0:1, 0:64],
                        nrm.ap[:, h * 128:(h + 1) * 128],
                        start=True, stop=True),
                        ins=(nrm, ones), outs=(p_bc,))
                kb.op("vector", lambda e, half=half: e.tensor_mul(
                    kfm[half].ap[:, :], kfm[half].ap[:, :], p_bc.ap[:, :]),
                    ins=(kfm[half], p_bc), outs=(kfm[half],))
            for h in range(4):
                kb.op("tensor", lambda e, h=h: e.transpose(
                    p_b.ap[0:128, 0:1],
                    nrm.ap[:, h * 128:(h + 1) * 128], Ione.ap[0:1, 0:1]),
                    ins=(nrm, Ione), outs=(p_b,))
                kb.op("vector", lambda e, h=h: e.tensor_copy(
                    rcol.ap[:, h:h + 1], p_b.ap[:, 0:1]),
                    ins=(p_b,), outs=(rcol,))
            for h in range(4):
                kb.op("vector", lambda e, h=h: e.tensor_single_scalar(
                    out=krm.ap[:, h * 64:(h + 1) * 64], in_=krm.ap[:, h * 64:(h + 1) * 64],
                    scalar=rcol.ap[:, h:h + 1], op=ALU.mult),
                    ins=(krm, rcol), outs=(krm,))
            for h in range(4):
                half, sub = h // 2, h % 2
                kf = kfm[half].ap[sub * 64:(sub + 1) * 64, :]
                qf = qfm[half].ap[sub * 64:(sub + 1) * 64, :]
                kr = krm.ap[:, h * 64:(h + 1) * 64]
                vr = vrm.ap[:, h * 64:(h + 1) * 64]
                kfT, qfT, krT, vrT = kfm[half], qfm[half], krm, vrm
                acol = abcols.ap[:, h:h + 1]
                bcol = abcols.ap[:, 4 + h:5 + h]
                kb.op("vector", lambda e, kf=kf: e.tensor_copy(kf0.ap[:, :], kf),
                      ins=(kfT,), outs=(kf0,))
                kb.op("vector", lambda e, qf=qf: e.tensor_copy(qf0.ap[:, :], qf),
                      ins=(qfT,), outs=(qf0,))
                kb.op("tensor", lambda e, kf=kf: e.matmul(
                    p_b.ap[:, :], kf, kf, start=True, stop=True),
                    ins=(kfT,), outs=(p_b,))
                kb.op("vector", lambda e, h=h: e.tensor_mul(Xb[h].ap[:, :], p_b.ap[:, :], Mlo.ap[:, :]),
                      ins=(p_b, Mlo), outs=(Xb[h],))
                kb.op("vector", lambda e, h=h: e.tensor_mul(Xtb[h].ap[:, :], p_b.ap[:, :], Mup.ap[:, :]),
                      ins=(p_b, Mup), outs=(Xtb[h],))
                kb.op("vector", lambda e, h=h, acol=acol: e.tensor_single_scalar(
                    out=Xb[h].ap[:, :], in_=Xb[h].ap[:, :], scalar=acol, op=ALU.mult),
                    ins=(Xb[h], abcols), outs=(Xb[h],))
                kb.op("vector", lambda e, h=h: e.tensor_sub(Xb[h].ap[:, :], Ione.ap[:, :], Xb[h].ap[:, :]),
                      ins=(Ione, Xb[h]), outs=(Xb[h],))
                kb.op("vector", lambda e, h=h, acol=acol: e.tensor_single_scalar(
                    out=Xtb[h].ap[:, :], in_=Xtb[h].ap[:, :], scalar=acol, op=ALU.mult),
                    ins=(Xtb[h], abcols), outs=(Xtb[h],))
                kb.op("vector", lambda e, h=h: e.tensor_sub(Xtb[h].ap[:, :], Ione.ap[:, :], Xtb[h].ap[:, :]),
                      ins=(Ione, Xtb[h]), outs=(Xtb[h],))
                kb.op("vector", lambda e, h=h: e.tensor_sub(Msb[h].ap[:, :], Itwo.ap[:, :], Xb[h].ap[:, :]),
                      ins=(Itwo, Xb[h]), outs=(Msb[h],))
                kb.op("vector", lambda e, h=h: e.tensor_sub(Mtb[h].ap[:, :], Itwo.ap[:, :], Xtb[h].ap[:, :]),
                      ins=(Itwo, Xtb[h]), outs=(Mtb[h],))
                for it in range(6):
                    kb.op("tensor", lambda e, h=h: e.matmul(
                        p_b.ap[:, :], Mtb[h].ap[:, :], Xb[h].ap[:, :], start=True, stop=True),
                        ins=(Mtb[h], Xb[h]), outs=(p_b,))
                    kb.op("vector", lambda e: e.tensor_copy(Ps.ap[:, :], p_b.ap[:, :]),
                          ins=(p_b,), outs=(Ps,))
                    kb.op("tensor", lambda e, h=h: e.matmul(
                        p_b.ap[:, :], Xtb[h].ap[:, :], Ps.ap[:, :], start=True, stop=True),
                        ins=(Xtb[h], Ps), outs=(p_b,))
                    kb.op("vector", lambda e, h=h: e.scalar_tensor_tensor(
                        out=Xb[h].ap[:, :], in0=Xb[h].ap[:, :], scalar=2.0, in1=p_b.ap[:, :],
                        op0=ALU.mult, op1=ALU.subtract),
                        ins=(Xb[h], p_b), outs=(Xb[h],))
                    kb.op("tensor", lambda e, h=h: e.matmul(
                        p_b.ap[:, :], Ps.ap[:, :], Xtb[h].ap[:, :], start=True, stop=True),
                        ins=(Ps, Xtb[h]), outs=(p_b,))
                    kb.op("vector", lambda e, h=h: e.scalar_tensor_tensor(
                        out=Xtb[h].ap[:, :], in0=Xtb[h].ap[:, :], scalar=2.0, in1=p_b.ap[:, :],
                        op0=ALU.mult, op1=ALU.subtract),
                        ins=(Xtb[h], p_b), outs=(Xtb[h],))
                kb.op("tensor", lambda e, h=h: e.matmul(
                    p_b.ap[:, 0:64], kf0.ap[:, :], S[h].ap[:, :], start=True, stop=True),
                    ins=(kf0, S[h]), outs=(p_b,))
                kb.op("vector", lambda e, h=h, acol=acol: e.tensor_single_scalar(
                    out=R2.ap[:, :], in_=p_b.ap[:, 0:64], scalar=acol, op=ALU.mult),
                    ins=(p_b, abcols), outs=(R2,))
                kb.op("vector", lambda e, h=h, bcol=bcol, vr=vr: e.scalar_tensor_tensor(
                    out=R2.ap[:, :], in0=vr, scalar=bcol, in1=R2.ap[:, :],
                    op0=ALU.mult, op1=ALU.subtract),
                    ins=(vrT, abcols, R2), outs=(R2,))
                kb.op("tensor", lambda e, h=h: e.matmul(
                    p_b.ap[:, 0:64], Xtb[h].ap[:, :], R2.ap[:, :], start=True, stop=True),
                    ins=(Xtb[h], R2), outs=(p_b,))
                kb.op("vector", lambda e: e.tensor_copy(Zs.ap[:, :], p_b.ap[:, 0:64]),
                      ins=(p_b,), outs=(Zs,))
                kb.op("tensor", lambda e, kf=kf, qf=qf: e.matmul(
                    p_b.ap[:, :], kf, qf, start=True, stop=True),
                    ins=(kfT, qfT), outs=(p_b,))
                kb.op("vector", lambda e: e.tensor_mul(Us.ap[:, :], p_b.ap[:, :], Mup.ap[:, :]),
                      ins=(p_b, Mup), outs=(Us,))
                kb.mm_group(
                    [(lambda e, h=h, sub=sub: e.matmul(
                        p_y.ap[sub * 64:(sub + 1) * 64, :], S[h].ap[:, :], qf0.ap[:, :],
                        start=True, stop=False)),
                     (lambda e, h=h, sub=sub: e.matmul(
                         p_y.ap[sub * 64:(sub + 1) * 64, :], Zs.ap[:, :], Us.ap[:, :],
                         start=False, stop=True))],
                    ins=(S[h], qf0, Zs, Us), outs=(p_y,))
                kb.op("tensor", lambda e, kr=kr: e.matmul(
                    p_s.ap[:, :], kr, Zs.ap[:, :], start=True, stop=True),
                    ins=(krT, Zs), outs=(p_s,))
                kb.op("vector", lambda e, h=h: e.tensor_add(S[h].ap[:, :], S[h].ap[:, :], p_s.ap[:, :]),
                      ins=(S[h], p_s), outs=(S[h],))
                if h % 2 == 1:
                    kb.op("vector", lambda e: e.tensor_copy(yev.ap[:, :], p_y.ap[:, :]),
                          ins=(p_y,), outs=(yev,))
                    kb.dma("sync", yout, np.s_[half * 128:(half + 1) * 128, t0:t0 + L],
                           yev, np.s_[:, :])
        for h in range(4):
            kb.dma("sync", s_out, np.s_[h * 64:(h + 1) * 64, :], S[h], np.s_[:, :])


def emit_ab(kb, ab_in, abcols, head0):
    with kb.scope():
        abr = kb.sb([1, 32])
        p_c = kb.psum([128, 1])
        ones = kb.cst_ones
        kb.dma("sync", abr, np.s_[:, 0:16], ab_in, np.s_[0:1, :])
        kb.dma("sync", abr, np.s_[:, 16:32], ab_in, np.s_[1:2, :])
        kb.op("scalar", lambda e: e.activation(out=abr.ap[:, :], in_=abr.ap[:, :],
                                               func=AF.Sigmoid),
              ins=(abr,), outs=(abr,))
        kb.op("vector", lambda e: e.tensor_scalar_mul(abr.ap[:, :], abr.ap[:, :], 0.5),
              ins=(abr,), outs=(abr,))
        for j in range(8):
            c = (j // 4) * 16 + head0 + (j % 4)
            kb.op("tensor", lambda e, c=c: e.matmul(
                p_c.ap[:, :], ones.ap[0:1, 0:128], abr.ap[0:1, c:c + 1],
                start=True, stop=True),
                ins=(abr, ones), outs=(p_c,))
            kb.op("vector", lambda e, j=j: e.tensor_copy(abcols.ap[:, j:j + 1], p_c.ap[:, :]),
                  ins=(p_c,), outs=(abcols,))



def emit_moe_route(kb, x_in, gwT, xhat_tm, selw_out, selp_out, rows_out, cnt_out, ntok, C, srow_d):
    gsz = min(512, ntok)
    ng = ntok // gsz
    ones = kb.cst_ones
    with kb.scope():
        xg = [kb.sb([128, gsz]) for _ in range(8)]
        xh = [kb.sb([128, gsz]) for _ in range(8)]
        sq = [kb.sb([128, gsz]) for _ in range(2)]
        rows = [kb.sb([1, gsz]) for _ in range(4)]
        abc = [kb.sb([128, gsz]) for _ in range(2)]
        gw = kb.sb([128, 8 * 16])
        prob = kb.sb([128, 16])
        t8 = kb.sb([128, 8])
        i8 = kb.sb([128, 8], U32)
        i8f = kb.sb([128, 8])
        wcols = kb.sb([128, 2])
        sw = kb.sb([128, 4])
        xtm = kb.sb([128, 1024])
        p_sx = kb.psum([1, gsz]); p_sxx = kb.psum([1, gsz]); p_bc = kb.psum([128, gsz])
        p_l = kb.psum([128, 16])
        p_t = kb.psum([128, 128])
        pbank = {"sx": p_sx, "sxx": p_sxx, "bc": p_bc, "sq": sq, "rows": rows, "abc": abc}
        src = gwT.ap[:, :].rearrange("(a p) h -> p a h", p=128)
        kb.dma_ap("sync", gw, gw.ap[:, :].rearrange("p (a h) -> p a h", a=8), gwT, src)
        for g in range(ng):
            t0 = g * gsz
            for c in range(8):
                kb.dma("sync", xg[c], np.s_[:, :], x_in,
                       np.s_[c * 128:(c + 1) * 128, t0:t0 + gsz])
            emit_ln_fm(kb, xg, xh, gsz, DM, pbank)
            for st in range(gsz // 128):
                u0 = st * 128
                kb.mm_group(
                    [(lambda e, dc=dc, u0=u0: e.matmul(
                        p_l.ap[:, :], xh[dc].ap[:, u0:u0 + 128],
                        gw.ap[:, dc * 16:(dc + 1) * 16],
                        start=(dc == 0), stop=(dc == 7))) for dc in range(8)],
                    ins=tuple(xh) + (gw,), outs=(p_l,))
                kb.op("scalar", lambda e: e.activation(out=prob.ap[:, :], in_=p_l.ap[:, :],
                                                       func=AF.Sigmoid),
                      ins=(p_l,), outs=(prob,))
                kb.op("vector", lambda e: e.max(out=t8.ap[:, :], in_=prob.ap[:, :]),
                      ins=(prob,), outs=(t8,))
                kb.op("vector", lambda e: e.max_index(out=i8.ap[:, :], in_max=t8.ap[:, :],
                                                      in_values=prob.ap[:, :]),
                      ins=(t8, prob), outs=(i8,))
                kb.op("vector", lambda e: e.tensor_copy(i8f.ap[:, :], i8.ap[:, :]),
                      ins=(i8,), outs=(i8f,))
                kb.op("vector", lambda e: e.tensor_add(wcols.ap[:, 0:1], t8.ap[:, 0:1],
                                                       t8.ap[:, 1:2]),
                      ins=(t8,), outs=(wcols,))
                kb.op("vector", lambda e: e.tensor_scalar_add(wcols.ap[:, 0:1],
                                                              wcols.ap[:, 0:1], 1e-6),
                      ins=(wcols,), outs=(wcols,))
                kb.op("vector", lambda e: e.reciprocal(wcols.ap[:, 1:2], wcols.ap[:, 0:1]),
                      ins=(wcols,), outs=(wcols,))
                kb.op("vector", lambda e: e.tensor_mul(sw.ap[:, 2:3], t8.ap[:, 0:1],
                                                       wcols.ap[:, 1:2]),
                      ins=(t8, wcols), outs=(sw,))
                kb.op("vector", lambda e: e.tensor_mul(sw.ap[:, 3:4], t8.ap[:, 1:2],
                                                       wcols.ap[:, 1:2]),
                      ins=(t8, wcols), outs=(sw,))
                kb.op("vector", lambda e: e.tensor_copy(sw.ap[:, 0:2], i8f.ap[:, 0:2]),
                      ins=(i8f,), outs=(sw,))
                kb.dma("sync", selw_out, np.s_[t0 + u0:t0 + u0 + 128, :], sw, np.s_[:, :])
                kb.dma("sync", selp_out, np.s_[t0 + u0:t0 + u0 + 128, :], sw, np.s_[:, 0:2])
                for dc in range(8):
                    kb.op("tensor", lambda e, dc=dc, u0=u0: e.transpose(
                        p_t.ap[:, :], xh[dc].ap[:, u0:u0 + 128], Icst(kb).ap[:, :]),
                        ins=(xh[dc], Icst(kb)), outs=(p_t,))
                    kb.op("vector", lambda e, dc=dc: e.tensor_copy(
                        xtm.ap[:, dc * 128:(dc + 1) * 128], p_t.ap[:, :]),
                        ins=(p_t,), outs=(xtm,))
                kb.dma("sync", xhat_tm, np.s_[t0 + u0:t0 + u0 + 128, :], xtm, np.s_[:, :])
    N2 = 2 * ntok
    RB = min(2048, ntok)
    with kb.scope():
        selc = kb.sb([1, RB])
        selb = kb.sb([16, RB])
        scan = kb.sb([16, RB])
        rl = kb.sb([1, RB])
        carry = kb.sb([16, 1])
        ecol = kb.sb([16, 1], I32)
        ecolf = kb.sb([16, 1])
        cnt = kb.sb([16, 2])
        QC = min(512, RB)
        p_r = kb.psum([1, QC])
        p_bb = kb.psum([16, QC])
        ones = kb.cst_ones
        kb.op("gpsimd", lambda e: e.iota(ecol.ap[:, :], pattern=[[0, 1]], base=0,
                                         channel_multiplier=1), outs=(ecol,))
        kb.op("vector", lambda e: e.tensor_copy(ecolf.ap[:, :], ecol.ap[:, :]),
              ins=(ecol,), outs=(ecolf,))
        kb.op("vector", lambda e: e.memset(carry.ap[:, :], 0.0), outs=(carry,))
        selp_flat = selp_out.ap[:, :].rearrange("t k -> (t k)").unsqueeze(0)
        for j in range(N2 // RB):
            kb.dma_ap("sync", selc, selc.ap[0:1, :], selp_out,
                      selp_flat[:, j * RB:(j + 1) * RB])
            kb.dma_ap("sync", rows_out, rows_out.ap[0:1, j * RB:(j + 1) * RB],
                      selc, selc.ap[0:1, :])
            for q in range(RB // QC):
                kb.op("tensor", lambda e, q=q: e.matmul(
                    p_bb.ap[:, :], ones.ap[0:1, 0:16], selc.ap[:, q * QC:(q + 1) * QC],
                    start=True, stop=True),
                    ins=(selc, ones), outs=(p_bb,))
                kb.op("vector", lambda e, q=q: e.tensor_copy(
                    selb.ap[:, q * QC:(q + 1) * QC], p_bb.ap[:, :]),
                    ins=(p_bb,), outs=(selb,))
            kb.op("vector", lambda e: e.tensor_single_scalar(
                out=selb.ap[:, :], in_=selb.ap[:, :], scalar=ecolf.ap[:, :],
                op=ALU.is_equal),
                ins=(selb, ecolf), outs=(selb,))
            kb.op("vector", lambda e: e.tensor_tensor_scan(
                out=scan.ap[:, :], data0=selb.ap[:, :], data1=selb.ap[:, :],
                initial=carry.ap[:, :], op0=ALU.add, op1=ALU.bypass),
                ins=(selb, carry), outs=(scan,))
            kb.op("vector", lambda e: e.tensor_copy(carry.ap[:, :],
                                                    scan.ap[:, RB - 1:RB]),
                  ins=(scan,), outs=(carry,))
            if (j + 1) * RB == ntok:
                kb.op("vector", lambda e: e.tensor_copy(cnt.ap[:, 0:1], carry.ap[:, :]),
                      ins=(carry,), outs=(cnt,))
            kb.op("vector", lambda e: e.tensor_scalar_add(scan.ap[:, :], scan.ap[:, :],
                                                          -1.0),
                  ins=(scan,), outs=(scan,))
            kb.op("vector", lambda e: e.tensor_mul(selb.ap[:, :], selb.ap[:, :],
                                                   scan.ap[:, :]),
                  ins=(selb, scan), outs=(selb,))
            for q in range(RB // QC):
                kb.op("tensor", lambda e, q=q: e.matmul(
                    p_r.ap[:, :], ones.ap[0:16, 0:1], selb.ap[:, q * QC:(q + 1) * QC],
                    start=True, stop=True),
                    ins=(selb, ones), outs=(p_r,))
                kb.op("vector", lambda e, q=q: e.tensor_copy(
                    rl.ap[:, q * QC:(q + 1) * QC], p_r.ap[:, :]),
                    ins=(p_r,), outs=(rl,))
            kb.dma("sync", rows_out, np.s_[1:2, j * RB:(j + 1) * RB], rl, np.s_[0:1, :])
            OOB = float(16 * C + 100)
            kb.op("vector", lambda e: e.tensor_single_scalar(
                out=scan.ap[0:1, :], in_=rl.ap[:, :], scalar=float(C), op=ALU.is_lt),
                ins=(rl,), outs=(scan,))
            kb.op("vector", lambda e: e.scalar_tensor_tensor(
                out=rl.ap[:, :], in0=selc.ap[:, :], scalar=float(C), in1=rl.ap[:, :],
                op0=ALU.mult, op1=ALU.add),
                ins=(selc, rl), outs=(rl,))
            kb.op("vector", lambda e: e.tensor_mul(rl.ap[:, :], rl.ap[:, :],
                                                   scan.ap[0:1, :]),
                  ins=(rl, scan), outs=(rl,))
            kb.op("vector", lambda e: e.scalar_tensor_tensor(
                out=rl.ap[:, :], in0=scan.ap[0:1, :], scalar=-OOB, in1=rl.ap[:, :],
                op0=ALU.mult, op1=ALU.add),
                ins=(scan, rl), outs=(rl,))
            kb.op("vector", lambda e: e.tensor_scalar_add(rl.ap[:, :], rl.ap[:, :], OOB),
                  ins=(rl,), outs=(rl,))
            kb.dma("sync", srow_d, np.s_[0:1, j * RB:(j + 1) * RB], rl, np.s_[0:1, :])
        kb.op("vector", lambda e: e.tensor_copy(cnt.ap[:, 1:2], carry.ap[:, :]),
              ins=(carry,), outs=(cnt,))
        kb.dma("sync", cnt_out, np.s_[:, :], cnt, np.s_[:, :])

def Icst(kb):
    return kb.ident


_MOEB_NEXP = 16
_MOEB_STAGES = {'z', 'o', 't', 'r', 'e'}


def emit_moe_experts(kb, rows_in, cnts3, w1_in, w2_in,
                     ntok, C, CAP_G, ebuf, obuf, vrow_d):
    N2 = 2 * ntok
    RB = min(2048, ntok)
    nr = ntok // 128
    ner = C // 128
    ones = kb.cst_ones
    OOB = float(16 * C + 100)
    with kb.scope():
        cnts = kb.sb([96, 16])
        offc = kb.sb([16, 2])
        QC = min(512, RB)
        p_o = kb.psum([1, max(QC, 16)])
        p_bb = kb.psum([16, QC])
        p_t = kb.psum([128, 128])
        kb.dma("sync", cnts, np.s_[:, :], cnts3, np.s_[:, :])
        zt = kb.sb([128, 1024])
        kb.op("vector", lambda e: e.memset(zt.ap[:, :], 0.0), outs=(zt,))
        nbrows = obuf.ap.shape[0]
        if 'z' in _MOEB_STAGES:
            for r0 in range(0, nbrows - 127, 128):
                kb.dma("sync", obuf, np.s_[r0:r0 + 128, :], zt, np.s_[:, :])
            if nbrows % 128:
                r0 = nbrows - (nbrows % 128)
                kb.dma("sync", obuf, np.s_[r0:nbrows, :], zt, np.s_[0:nbrows - r0, :])
        if 'o' not in _MOEB_STAGES:
            kb.op("vector", lambda e: e.memset(offc.ap[:, :], 0.0), outs=(offc,))
        colsel = kb.sb([96, 2])
        s216 = kb.sb([2, 16])
        if 'o' in _MOEB_STAGES:
            kb.op("vector", lambda e: e.memset(colsel.ap[:, :], 0.0), outs=(colsel,))
            kb.op("vector", lambda e: e.memset(colsel.ap[0:32, 0:1], 1.0), outs=(colsel,))
            kb.op("vector", lambda e: e.memset(colsel.ap[32:64, 1:2], 1.0), outs=(colsel,))
            kb.op("vector", lambda e: e.memset(colsel.ap[64:96, 1:2], 1.0), outs=(colsel,))
            kb.op("tensor", lambda e: e.matmul(p_bb.ap[0:2, 0:16], colsel.ap[:, :],
                                               cnts.ap[:, :], start=True, stop=True),
                  ins=(colsel, cnts), outs=(p_bb,))
            kb.op("vector", lambda e: e.tensor_copy(s216.ap[:, :], p_bb.ap[0:2, 0:16]),
                  ins=(p_bb,), outs=(s216,))
        if 't' in _MOEB_STAGES:
            kb.op("tensor", lambda e: e.transpose(
                p_t.ap[0:16, 0:2], s216.ap[:, :], kb.ident.ap[0:2, 0:2]),
                ins=(s216,), outs=(p_t,))
            kb.op("vector", lambda e: e.tensor_copy(offc.ap[:, :], p_t.ap[0:16, 0:2]),
                  ins=(p_t,), outs=(offc,))
        selc = kb.sb([1, RB]); rlc = kb.sb([1, RB])
        selb = kb.sb([16, RB]); tmp16 = kb.sb([16, RB])
        rgc = kb.sb([1, RB]); vlc = kb.sb([1, RB]); slc = kb.sb([1, RB])
        ecol = kb.sb([16, 1], I32); ecolf = kb.sb([16, 1])
        kb.op("gpsimd", lambda e: e.iota(ecol.ap[:, :], pattern=[[0, 1]], base=0,
                                         channel_multiplier=1), outs=(ecol,))
        kb.op("vector", lambda e: e.tensor_copy(ecolf.ap[:, :], ecol.ap[:, :]),
              ins=(ecol,), outs=(ecolf,))
        for j in (range(N2 // RB) if 'r' in _MOEB_STAGES else []):
            half = 0 if (j * RB) < ntok else 1
            kb.dma("sync", selc, np.s_[:, :], rows_in, np.s_[0:1, j * RB:(j + 1) * RB])
            kb.dma("sync", rlc, np.s_[:, :], rows_in, np.s_[1:2, j * RB:(j + 1) * RB])
            for q in range(RB // QC):
                kb.op("tensor", lambda e, q=q: e.matmul(
                    p_bb.ap[:, :], ones.ap[0:1, 0:16], selc.ap[:, q * QC:(q + 1) * QC],
                    start=True, stop=True),
                    ins=(selc, ones), outs=(p_bb,))
                kb.op("vector", lambda e, q=q: e.tensor_copy(
                    selb.ap[:, q * QC:(q + 1) * QC], p_bb.ap[:, :]),
                    ins=(p_bb,), outs=(selb,))
            kb.op("vector", lambda e: e.tensor_single_scalar(
                out=selb.ap[:, :], in_=selb.ap[:, :], scalar=ecolf.ap[:, :],
                op=ALU.is_equal),
                ins=(selb, ecolf), outs=(selb,))
            kb.op("vector", lambda e, half=half: e.tensor_single_scalar(
                out=tmp16.ap[:, :], in_=selb.ap[:, :],
                scalar=offc.ap[:, half:half + 1], op=ALU.mult),
                ins=(selb, offc), outs=(tmp16,))
            for q in range(RB // QC):
                kb.op("tensor", lambda e, q=q: e.matmul(
                    p_o.ap[:, :], ones.ap[0:16, 0:1], tmp16.ap[:, q * QC:(q + 1) * QC],
                    start=True, stop=True),
                    ins=(tmp16, ones), outs=(p_o,))
                kb.op("vector", lambda e, q=q: e.tensor_add(
                    rgc.ap[:, q * QC:(q + 1) * QC], rlc.ap[:, q * QC:(q + 1) * QC],
                    p_o.ap[:, :]),
                    ins=(rlc, p_o), outs=(rgc,))
            kb.op("vector", lambda e: e.tensor_single_scalar(
                out=vlc.ap[:, :], in_=rgc.ap[:, :], scalar=float(CAP_G), op=ALU.is_lt),
                ins=(rgc,), outs=(vlc,))
            kb.op("vector", lambda e: e.tensor_single_scalar(
                out=slc.ap[:, :], in_=rlc.ap[:, :], scalar=float(C), op=ALU.is_lt),
                ins=(rlc,), outs=(slc,))
            kb.op("vector", lambda e: e.tensor_mul(vlc.ap[:, :], vlc.ap[:, :],
                                                   slc.ap[:, :]),
                  ins=(vlc, slc), outs=(vlc,))
            kb.dma("sync", vrow_d, np.s_[0:1, j * RB:(j + 1) * RB], vlc, np.s_[:, :])
        w1s = [kb.sb([128, 8 * 128]) for _ in range(2)]
        w1b = [kb.sb([128, 8 * 128], BF16) for _ in range(2)]
        w2h = [kb.sb([128, 512]) for _ in range(16)]
        w2b = [kb.sb([128, 512], BF16) for _ in range(16)]
        xe = [kb.sb([128, C], BF16) for _ in range(8)]
        he = [kb.sb([128, C], BF16) for _ in range(16)]
        erow = kb.sb([128, 1024])
        p_h = kb.psum([128, 512])
        p_h2 = kb.psum([128, 256])
        for e_ in range(_MOEB_NEXP):
            for r in range(ner):
                kb.dma("sync", erow, np.s_[:, :], ebuf,
                       np.s_[e_ * C + r * 128:e_ * C + (r + 1) * 128, :])
                for dc in range(8):
                    kb.op("tensor", lambda e, dc=dc: e.transpose(
                        p_t.ap[:, :], erow.ap[:, dc * 128:(dc + 1) * 128],
                        kb.ident.ap[:, :]),
                        ins=(erow,), outs=(p_t,))
                    kb.op("vector", lambda e, dc=dc, r=r: e.tensor_copy(
                        xe[dc].ap[:, r * 128:(r + 1) * 128], p_t.ap[:, :]),
                        ins=(p_t,), outs=(xe[dc],))
            for fc in range(16):
                w = w1s[fc % 2]
                wb = w1b[fc % 2]
                src = w1_in.ap[e_][:, fc * 128:(fc + 1) * 128].rearrange(
                    "(a p) h -> p a h", p=128)
                kb.dma_ap("sync", w, w.ap[:, :].rearrange("p (a h) -> p a h", a=8),
                          w1_in, src)
                kb.op("vector", lambda e, w=w, wb=wb: e.tensor_copy(wb.ap[:, :], w.ap[:, :]),
                      ins=(w,), outs=(wb,))
                nfree = (C + 511) // 512
                for fg in range(nfree):
                    f0 = fg * 512
                    f1 = min(C, f0 + 512)
                    pp = p_h if (f1 - f0) == 512 else p_h2
                    kb.mm_group(
                        [(lambda e, dc=dc, wb=wb, f0=f0, f1=f1, pp=pp: e.matmul(
                            pp.ap[:, 0:f1 - f0], wb.ap[:, dc * 128:(dc + 1) * 128],
                            xe[dc].ap[:, f0:f1],
                            start=(dc == 0), stop=(dc == 7))) for dc in range(8)],
                        ins=tuple(xe) + (wb,), outs=(pp,))
                    kb.op("scalar", lambda e, fc=fc, f0=f0, f1=f1, pp=pp: e.activation(
                        out=he[fc].ap[:, f0:f1], in_=pp.ap[:, 0:f1 - f0], func=AF.Gelu),
                        ins=(pp,), outs=(he[fc],))
            for colh in range(2):
                for fc in range(16):
                    kb.dma("sync", w2h[fc], np.s_[:, :], w2_in,
                           np.s_[e_, fc * 128:(fc + 1) * 128, colh * 512:(colh + 1) * 512])
                    kb.op("vector", lambda e, fc=fc: e.tensor_copy(w2b[fc].ap[:, :],
                                                                   w2h[fc].ap[:, :]),
                          ins=(w2h[fc],), outs=(w2b[fc],))
                for r in range(ner):
                    kb.mm_group(
                        [(lambda e, fc=fc, r=r: e.matmul(
                            p_h.ap[:, :], he[fc].ap[:, r * 128:(r + 1) * 128],
                            w2b[fc].ap[:, :],
                            start=(fc == 0), stop=(fc == 15))) for fc in range(16)],
                        ins=tuple(he) + tuple(w2b), outs=(p_h,))
                    kb.op("vector", lambda e, colh=colh: e.tensor_copy(
                        erow.ap[:, colh * 512:(colh + 1) * 512], p_h.ap[:, :]),
                        ins=(p_h,), outs=(erow,))
                    kb.dma("sync", obuf,
                           np.s_[e_ * C + r * 128:e_ * C + (r + 1) * 128,
                                 colh * 512:(colh + 1) * 512],
                           erow, np.s_[:, colh * 512:(colh + 1) * 512])

def _norm_ktR(kb, ktR, ones, p_n1, p_bc5, nrow, sqk):
    for p in range(24):
        kb.op("scalar", lambda e, p=p: e.activation(out=sqk.ap[:, :], in_=ktR[p].ap[:, :],
                                                    func=AF.Square),
              ins=(ktR[p],), outs=(sqk,))
        kb.op("tensor", lambda e: e.matmul(p_n1.ap[:, :], ones.ap[:, 0:1], sqk.ap[:, :],
                                           start=True, stop=True),
              ins=(sqk, ones), outs=(p_n1,))
        kb.op("scalar", lambda e: e.activation(out=nrow.ap[:, :], in_=p_n1.ap[:, :],
                                               func=AF.Sqrt, scale=0.25),
              ins=(p_n1,), outs=(nrow,))
        kb.op("vector", lambda e: e.reciprocal(nrow.ap[:, :], nrow.ap[:, :]),
              ins=(nrow,), outs=(nrow,))
        kb.op("tensor", lambda e: e.matmul(p_bc5.ap[:, :], ones.ap[0:1, 0:128],
                                           nrow.ap[:, :], start=True, stop=True),
              ins=(nrow, ones), outs=(p_bc5,))
        kb.op("vector", lambda e, p=p: e.tensor_mul(ktR[p].ap[:, :], ktR[p].ap[:, :],
                                                    p_bc5.ap[:, :]),
              ins=(ktR[p], p_bc5), outs=(ktR[p],))


_ATT1_STAGES = {'m', 'p', 'o'}
_ATT1_MSUB = 5


def emit_attn1(kb, x_in, x_out, W, ntok, r0, nr_pairs):
    half = ntok // 2
    ones = kb.cst_ones
    with kb.scope():
        wq = [kb.sb([128, 1024]) for _ in range(6)]
        for c in range(6):
            src = W['wqT'].ap[:, c * 128:(c + 1) * 128].rearrange("(a p) h -> p a h", p=128)
            kb.dma_ap("sync", wq[c], wq[c].ap[:, :].rearrange("p (a h) -> p a h", a=8),
                      W['wqT'], src)
        wo = [kb.sb([128, 1024]) for _ in range(2)]
        ktR = [kb.sb([128, 512]) for _ in range(24)]
        for p in range(24):
            kb.dma("sync", ktR[p], np.s_[:, :], W['ktR'], np.s_[p, :, :])
        vtab = kb.sb([128, 96 * 32])
        src = W['vtab'].ap[:, :].rearrange("(n p) v -> p n v", p=128)
        kb.dma_ap("sync", vtab, vtab.ap[:, :].rearrange("p (n v) -> p n v", n=96),
                  W['vtab'], src)
        wres8 = kb.sb([128, 64]); wmemT = kb.sb([32, 64])
        bres_c = kb.sb([32, 1]); bmem_r = kb.sb([1, 64])
        bqc = kb.sb([128, 6]); gqc = kb.sb([128, 1]); bqlnc = kb.sb([128, 1])
        E8 = kb.sb([128, 8]); E8b = kb.sb([8, 128])
        goc = kb.sb([128, 8]); boc = kb.sb([128, 8])
        for t, k_ in ((wres8, 'wres8'), (wmemT, 'wmemT'), (bres_c, 'bres_col'),
                      (bmem_r, 'bmem_row'), (bqc, 'bq_cols'), (gqc, 'gq_col'),
                      (bqlnc, 'bqln_col'), (E8, 'E8'), (E8b, 'E8b'),
                      (goc, 'gout_cols'), (boc, 'bout_cols')):
            kb.dma("sync", t, np.s_[:, :], W[k_], np.s_[:, :])
        with kb.scope():
            p_n1 = kb.psum([1, 512])
            p_bc5 = kb.psum([128, 512])
            nrow = kb.sb([1, 512]); sqk = kb.sb([128, 512])
            _norm_ktR(kb, ktR, ones, p_n1, p_bc5, nrow, sqk)
        xg = [[kb.sb([128, 128]) for _ in range(8)] for _ in range(2)]
        xh = [[kb.sb([128, 128]) for _ in range(8)] for _ in range(2)]
        sq2 = [kb.sb([128, 128]) for _ in range(2)]
        rows4 = [kb.sb([1, 128]) for _ in range(4)]
        abc2 = [kb.sb([128, 128]) for _ in range(2)]
        qf = [[kb.sb([128, 128]) for _ in range(6)] for _ in range(2)]
        mu8 = kb.sb([8, 128]); v8 = kb.sb([8, 128]); m8s = kb.sb([8, 128])
        t8 = [kb.sb([128, 48 * 8]) for _ in range(2)]
        u8 = kb.sb([128, 8])
        rzc = kb.sb([128, 1]); rzr = kb.sb([1, 128])
        sc2 = kb.sb([128, 512]); m2 = kb.sb([128, 512])
        sck = [kb.sb([128, 128]) for _ in range(4)]
        o2s = kb.sb([32, 128])
        out3 = [kb.sb([128, 48 * 64]) for _ in range(2)]
        lo = [kb.sb([128, 1024]) for _ in range(2)]
        dta = kb.sb([128, 1024])
        dred = kb.sb([128, 16 * 8])
        lofm = [[kb.sb([128, 128]) for _ in range(8)] for _ in range(2)]
        tg = [[kb.sb([128, 128]) for _ in range(8)] for _ in range(2)]
        p_at = kb.psum([128, 128])
        p_q = kb.psum([128, 128])
        p_s32 = kb.psum([32, 128])
        p_sc = kb.psum([128, 512])
        p_t128 = kb.psum([128, 192])
        p_stats = kb.psum([1, 256])
        class _V:
            def __init__(self, base, ap):
                self.base = base; self.ap = ap
            @property
            def w(self): return self.base.w
            @w.setter
            def w(self, v): self.base.w = v
            @property
            def rs(self): return self.base.rs
            @rs.setter
            def rs(self, v): self.base.rs = v
        p_s8 = _V(p_s32, p_s32.ap[0:8, :])
        p_o2 = p_s32
        p_t = _V(p_t128, p_t128.ap[:, 0:128])
        p_o3 = _V(p_t128, p_t128.ap[:, 128:192])
        pbank = {"sx": _V(p_stats, p_stats.ap[:, 0:128]),
                 "sxx": _V(p_stats, p_stats.ap[:, 128:256]),
                 "bc": p_q, "sq": sq2, "rows": rows4, "abc": abc2}
        for rp in range(r0, r0 + nr_pairs):
            for side in range(2):
                t0 = rp * 128 + side * half
                for c in range(8):
                    kb.dma("sync", xg[side][c], np.s_[:, :], x_in,
                           np.s_[c * 128:(c + 1) * 128, t0:t0 + 128])
                emit_ln_fm(kb, xg[side], xh[side], 128, DM, pbank)
                for c in range(6):
                    kb.mm_group(
                        [(lambda e, dc=dc, c=c, side=side: e.matmul(
                            p_q.ap[:, :], wq[c].ap[:, dc * 128:(dc + 1) * 128],
                            xh[side][dc].ap[:, :],
                            start=(dc == 0), stop=(dc == 7))) for dc in range(8)],
                        ins=tuple(xh[side]) + (wq[c],), outs=(p_q,))
                    q = qf[side][c]
                    kb.op("scalar", lambda e, c=c, q=q: e.activation(
                        out=q.ap[:, :], in_=p_q.ap[:, :], func=AF.Identity,
                        bias=bqc.ap[:, c:c + 1]),
                        ins=(p_q, bqc), outs=(q,))
                    kb.op("tensor", lambda e, q=q: e.matmul(
                        p_s8.ap[:, :], E8.ap[:, :], q.ap[:, :], start=True, stop=True),
                        ins=(q, E8), outs=(p_s8,))
                    kb.op("vector", lambda e: e.tensor_scalar_mul(
                        mu8.ap[:, :], p_s8.ap[:, :], 1.0 / 16), ins=(p_s8,), outs=(mu8,))
                    kb.op("scalar", lambda e, q=q: e.activation(
                        out=sq2[0].ap[:, :], in_=q.ap[:, :], func=AF.Square),
                        ins=(q,), outs=(sq2[0],))
                    kb.op("tensor", lambda e: e.matmul(
                        p_s8.ap[:, :], E8.ap[:, :], sq2[0].ap[:, :], start=True, stop=True),
                        ins=(sq2[0], E8), outs=(p_s8,))
                    kb.op("vector", lambda e: e.tensor_scalar_mul(
                        v8.ap[:, :], p_s8.ap[:, :], 1.0 / 16), ins=(p_s8,), outs=(v8,))
                    kb.op("vector", lambda e: e.tensor_mul(m8s.ap[:, :], mu8.ap[:, :],
                                                           mu8.ap[:, :]),
                          ins=(mu8,), outs=(m8s,))
                    kb.op("vector", lambda e: e.tensor_sub(v8.ap[:, :], v8.ap[:, :],
                                                           m8s.ap[:, :]),
                          ins=(v8, m8s), outs=(v8,))
                    kb.op("vector", lambda e: e.tensor_scalar_add(v8.ap[:, :], v8.ap[:, :],
                                                                  1e-5),
                          ins=(v8,), outs=(v8,))
                    kb.op("scalar", lambda e: e.activation(out=v8.ap[:, :], in_=v8.ap[:, :],
                                                           func=AF.Sqrt),
                          ins=(v8,), outs=(v8,))
                    kb.op("vector", lambda e: e.reciprocal(v8.ap[:, :], v8.ap[:, :]),
                          ins=(v8,), outs=(v8,))
                    kb.op("tensor", lambda e: e.matmul(p_q.ap[:, :], E8b.ap[:, :],
                                                       mu8.ap[:, :], start=True, stop=True),
                          ins=(mu8, E8b), outs=(p_q,))
                    kb.op("vector", lambda e, q=q: e.tensor_sub(q.ap[:, :], q.ap[:, :],
                                                                p_q.ap[:, :]),
                          ins=(q, p_q), outs=(q,))
                    kb.op("tensor", lambda e: e.matmul(p_q.ap[:, :], E8b.ap[:, :],
                                                       v8.ap[:, :], start=True, stop=True),
                          ins=(v8, E8b), outs=(p_q,))
                    kb.op("vector", lambda e, q=q: e.tensor_mul(q.ap[:, :], q.ap[:, :],
                                                                p_q.ap[:, :]),
                          ins=(q, p_q), outs=(q,))
                    kb.op("vector", lambda e, q=q: e.tensor_single_scalar(
                        out=q.ap[:, :], in_=q.ap[:, :], scalar=gqc.ap[:, 0:1], op=ALU.mult),
                        ins=(q, gqc), outs=(q,))
                    kb.op("vector", lambda e, q=q: e.tensor_single_scalar(
                        out=q.ap[:, :], in_=q.ap[:, :], scalar=bqlnc.ap[:, 0:1], op=ALU.add),
                        ins=(q, bqlnc), outs=(q,))
                for sp in (range(24) if 'm' in _ATT1_STAGES else []):
                    c, g = sp // 4, sp % 4
                    q = qf[side][c]
                    tp = (96, 0) if g == 3 else None
                    kb.op("tensor", lambda e, q=q, g=g, sp=sp, tp=tp: e.matmul(
                        p_sc.ap[:, :], q.ap[g * 32:(g + 1) * 32, :],
                        ktR[sp].ap[g * 32:(g + 1) * 32, :],
                        start=True, stop=True, tile_position=tp),
                        ins=(q, ktR[sp]), outs=(p_sc,))
                    for sl in range(2):
                        kb.op("vector", lambda e, sp=sp, sl=sl, side=side: e.max(
                            out=t8[side].ap[:, (2 * sp + sl) * 8:(2 * sp + sl + 1) * 8],
                            in_=p_sc.ap[:, sl * 256:(sl + 1) * 256]),
                            ins=(p_sc,), outs=(t8[side],))
                    if _ATT1_MSUB < 2:
                        continue
                    kb.op("vector", lambda e: e.tensor_copy(sc2.ap[:, :], p_sc.ap[:, :]),
                          ins=(p_sc,), outs=(sc2,))
                    kb.op("scalar", lambda e: e.activation(out=sc2.ap[:, :],
                                                           in_=sc2.ap[:, :], func=AF.Exp),
                          ins=(sc2,), outs=(sc2,))
                    for sl_ in range(2):
                        kb.op("vector", lambda e, sp=sp, side=side, sl_=sl_: e.tensor_tensor(
                            out=m2.ap[:, sl_ * 256:(sl_ + 1) * 256],
                            in0=p_sc.ap[:, sl_ * 256:(sl_ + 1) * 256],
                            in1=t8[side].ap[:, (2 * sp + sl_) * 8 + 7:(2 * sp + sl_) * 8 + 8]
                                .to_broadcast([128, 256]),
                            op=ALU.is_ge),
                            ins=(p_sc, t8[side]), outs=(m2,))
                    kb.op("vector", lambda e: e.tensor_mul(sc2.ap[:, :], sc2.ap[:, :],
                                                           m2.ap[:, :]),
                          ins=(sc2, m2), outs=(sc2,))
                    if _ATT1_MSUB < 3:
                        continue
                    for q4 in range(4):
                        kb.op("tensor", lambda e, q4=q4: e.transpose(
                            p_t.ap[:, :], sc2.ap[:, q4 * 128:(q4 + 1) * 128],
                            kb.ident.ap[:, :]),
                            ins=(sc2,), outs=(p_t,))
                        kb.op("vector", lambda e, q4=q4: e.tensor_copy(
                            sck[q4].ap[:, :], p_t.ap[:, :]),
                            ins=(p_t,), outs=(sck[q4],))
                    if _ATT1_MSUB < 4:
                        continue
                    for sl in range(2):
                        s_ = 2 * sp + sl
                        kb.op("scalar", lambda e, s_=s_, side=side: e.activation(
                            out=u8.ap[:, :], in_=t8[side].ap[:, s_ * 8:(s_ + 1) * 8],
                            func=AF.Exp),
                            ins=(t8[side],), outs=(u8,))
                        kb.op("vector", lambda e: e.tensor_reduce(
                            out=rzc.ap[:, :], in_=u8.ap[:, :],
                            axis=mybir.AxisListType.X, op=ALU.add),
                            ins=(u8,), outs=(rzc,))
                        kb.op("vector", lambda e: e.reciprocal(rzc.ap[:, :], rzc.ap[:, :]),
                              ins=(rzc,), outs=(rzc,))
                        kb.op("tensor", lambda e: e.transpose(
                            p_t.ap[0:1, 0:128], rzc.ap[:, :], kb.ident.ap[:, :]),
                            ins=(rzc,), outs=(p_t,))
                        kb.op("vector", lambda e: e.tensor_copy(rzr.ap[:, :],
                                                                p_t.ap[0:1, 0:128]),
                              ins=(p_t,), outs=(rzr,))
                        kb.op("tensor", lambda e: e.matmul(
                            p_q.ap[0:32, :], ones.ap[0:1, 0:32], rzr.ap[:, :],
                            start=True, stop=True),
                            ins=(rzr, ones), outs=(p_q,))
                        kb.mm_group(
                            [(lambda e, s_=s_, sl=sl, cc=cc: e.matmul(
                                p_o2.ap[:, :],
                                vtab.ap[:, (2 * s_ + cc) * 32:(2 * s_ + cc + 1) * 32],
                                sck[2 * sl + cc].ap[:, :],
                                start=(cc == 0), stop=(cc == 1))) for cc in range(2)],
                            ins=(vtab, sck[2 * sl], sck[2 * sl + 1]), outs=(p_o2,))
                        kb.op("scalar", lambda e: e.activation(
                            out=o2s.ap[:, :], in_=p_o2.ap[:, :], func=AF.Identity),
                            ins=(p_o2,), outs=(o2s,))
                        kb.op("vector", lambda e: e.tensor_mul(o2s.ap[:, :], o2s.ap[:, :],
                                                               p_q.ap[0:32, :]),
                              ins=(o2s, p_q), outs=(o2s,))
                        kb.op("tensor", lambda e, g=g, q=q, sl=sl, tp=tp: e.matmul(
                            p_o2.ap[:, :],
                            wres8.ap[g * 32:(g + 1) * 32, sl * 32:(sl + 1) * 32],
                            q.ap[g * 32:(g + 1) * 32, :],
                            start=True, stop=True, tile_position=tp),
                            ins=(q, wres8), outs=(p_o2,))
                        kb.op("vector", lambda e: e.tensor_add(o2s.ap[:, :], o2s.ap[:, :],
                                                               p_o2.ap[:, :]),
                              ins=(o2s, p_o2), outs=(o2s,))
                        kb.op("vector", lambda e: e.tensor_single_scalar(
                            out=o2s.ap[:, :], in_=o2s.ap[:, :],
                            scalar=bres_c.ap[:, 0:1], op=ALU.add),
                            ins=(o2s, bres_c), outs=(o2s,))
                        kb.mm_group(
                            [(lambda e: e.matmul(p_o3.ap[:, :], o2s.ap[:, :],
                                                 wmemT.ap[:, :], start=True, stop=False)),
                             (lambda e: e.matmul(p_o3.ap[:, :], ones.ap[0:1, 0:128],
                                                 bmem_r.ap[:, :], start=False, stop=True))],
                            ins=(o2s, wmemT, bmem_r, ones), outs=(p_o3,))
                        kb.op("scalar", lambda e, s_=s_, side=side: e.activation(
                            out=out3[side].ap[:, s_ * 64:(s_ + 1) * 64], in_=p_o3.ap[:, :],
                            func=AF.Identity),
                            ins=(p_o3,), outs=(out3[side],))
            if 'm' not in _ATT1_STAGES or _ATT1_MSUB < 5:
                for side in range(2):
                    kb.op("vector", lambda e, side=side: e.memset(out3[side].ap[:, :], 0.01),
                          outs=(out3[side],))
            for side in (range(2) if 'p' in _ATT1_STAGES else []):
                for c0 in (0, 2048):
                    seg = out3[side].ap[:, c0:c0 + 1024]
                    kb.op("vector", lambda e, seg=seg: e.tensor_single_scalar(
                        out=dta.ap[:, :], in_=seg, scalar=0.0, op=ALU.min),
                        ins=(out3[side],), outs=(dta,))
                    kb.op("scalar", lambda e: e.activation(out=dta.ap[:, :],
                                                           in_=dta.ap[:, :], func=AF.Exp),
                          ins=(dta,), outs=(dta,))
                    kb.op("vector", lambda e, seg=seg, side=side: e.scalar_tensor_tensor(
                        out=seg, in0=seg, scalar=1.0, in1=dta.ap[:, :],
                        op0=ALU.add, op1=ALU.max),
                        ins=(out3[side], dta), outs=(out3[side],))
            if 'p' not in _ATT1_STAGES:
                for side in range(2):
                    kb.op("vector", lambda e, side=side: e.memset(lo[side].ap[:, :], 0.01),
                          outs=(lo[side],))
            q0 = out3[0].ap[:, 2048:3072]; k0 = out3[0].ap[:, 0:1024]
            q1 = out3[1].ap[:, 2048:3072]; k1 = out3[1].ap[:, 0:1024]
            for (col, aa, bb) in (((0, q0, k0), (1, q1, k0), (2, q1, k1)) if 'p' in _ATT1_STAGES else []):
                kb.op("vector", lambda e, aa=aa, bb=bb: e.tensor_tensor(
                    out=dta.ap[:, :], in0=aa, in1=bb, op=ALU.mult),
                    ins=(out3[0], out3[1]), outs=(dta,))
                kb.op("vector", lambda e, col=col: e.tensor_reduce(
                    out=dred.ap[:, col * 16:(col + 1) * 16],
                    in_=dta.ap[:, :].rearrange("p (h d) -> p h d", h=16),
                    axis=mybir.AxisListType.X, op=ALU.add),
                    ins=(dta,), outs=(dred,))
            if 'p' in _ATT1_STAGES:
                kb.op("vector", lambda e: e.tensor_add(dred.ap[:, 48:64], dred.ap[:, 16:32],
                                                       dred.ap[:, 32:48]),
                      ins=(dred,), outs=(dred,))
                for (src, dst) in ((0, 4), (3, 5)):
                    kb.op("vector", lambda e, src=src, dst=dst: e.tensor_scalar(
                        out=dred.ap[:, dst * 16:(dst + 1) * 16],
                        in0=dred.ap[:, src * 16:(src + 1) * 16],
                        scalar1=1e-5, scalar2=8.0, op0=ALU.add, op1=ALU.mult),
                        ins=(dred,), outs=(dred,))
                    kb.op("vector", lambda e, dst=dst: e.reciprocal(
                        dred.ap[:, dst * 16:(dst + 1) * 16], dred.ap[:, dst * 16:(dst + 1) * 16]),
                        ins=(dred,), outs=(dred,))
                kb.op("vector", lambda e: e.tensor_mul(dred.ap[:, 96:112], dred.ap[:, 0:16],
                                                       dred.ap[:, 64:80]),
                      ins=(dred,), outs=(dred,))
                kb.op("vector", lambda e: e.tensor_mul(dred.ap[:, 16:32], dred.ap[:, 16:32],
                                                       dred.ap[:, 80:96]),
                      ins=(dred,), outs=(dred,))
                kb.op("vector", lambda e: e.tensor_mul(dred.ap[:, 32:48], dred.ap[:, 32:48],
                                                       dred.ap[:, 80:96]),
                      ins=(dred,), outs=(dred,))
                v0 = out3[0].ap[:, 1024:2048].rearrange("p (h d) -> p h d", h=16)
                v1 = out3[1].ap[:, 1024:2048].rearrange("p (h d) -> p h d", h=16)
                kb.op("vector", lambda e, v0=v0: e.tensor_tensor(
                    out=lo[0].ap[:, :].rearrange("p (h d) -> p h d", h=16), in0=v0,
                    in1=dred.ap[:, 96:112].rearrange("p (h o) -> p h o", o=1)
                        .to_broadcast([128, 16, 64]),
                    op=ALU.mult),
                    ins=(out3[0], dred), outs=(lo[0],))
                kb.op("vector", lambda e, v0=v0: e.tensor_tensor(
                    out=lo[1].ap[:, :].rearrange("p (h d) -> p h d", h=16), in0=v0,
                    in1=dred.ap[:, 16:32].rearrange("p (h o) -> p h o", o=1)
                        .to_broadcast([128, 16, 64]),
                    op=ALU.mult),
                    ins=(out3[0], dred), outs=(lo[1],))
                kb.op("vector", lambda e, v1=v1: e.tensor_tensor(
                    out=dta.ap[:, :].rearrange("p (h d) -> p h d", h=16), in0=v1,
                    in1=dred.ap[:, 32:48].rearrange("p (h o) -> p h o", o=1)
                        .to_broadcast([128, 16, 64]),
                    op=ALU.mult),
                    ins=(out3[1], dred), outs=(dta,))
                kb.op("vector", lambda e: e.tensor_add(lo[1].ap[:, :], lo[1].ap[:, :],
                                                       dta.ap[:, :]),
                      ins=(lo[1], dta), outs=(lo[1],))
            for side in range(2):
                t0 = rp * 128 + side * half
                for dc in range(8):
                    kb.op("tensor", lambda e, dc=dc, side=side: e.transpose(
                        p_t.ap[:, :], lo[side].ap[:, dc * 128:(dc + 1) * 128],
                        kb.ident.ap[:, :]),
                        ins=(lo[side],), outs=(p_t,))
                    kb.op("vector", lambda e, dc=dc, side=side: e.tensor_copy(
                        lofm[side][dc].ap[:, :], p_t.ap[:, :]),
                        ins=(p_t,), outs=(lofm[side][dc],))
                for oc in range(8):
                    wt = wo[oc % 2]
                    src = W['woT'].ap[:, oc * 128:(oc + 1) * 128].rearrange(
                        "(a p) h -> p a h", p=128)
                    kb.dma_ap("sync", wt, wt.ap[:, :].rearrange("p (a h) -> p a h", a=8),
                              W['woT'], src)
                    kb.mm_group(
                        [(lambda e, hc=hc, wt=wt, side=side: e.matmul(
                            p_at.ap[:, :], wt.ap[:, hc * 128:(hc + 1) * 128],
                            lofm[side][hc].ap[:, :],
                            start=(hc == 0), stop=(hc == 7))) for hc in range(8)],
                        ins=tuple(lofm[side]) + (wt,), outs=(p_at,))
                    kb.op("vector", lambda e, oc=oc, side=side: e.tensor_add(
                        tg[side][oc].ap[:, :], xh[side][oc].ap[:, :], p_at.ap[:, :]),
                        ins=(xh[side][oc], p_at), outs=(tg[side][oc],))
                emit_ln_fm(kb, tg[side], tg[side], 128, DM, pbank,
                           gam_col=(goc, 0), bet_col=(boc, 0))
                for oc in range(8):
                    kb.op("vector", lambda e, oc=oc, side=side: e.tensor_add(
                        tg[side][oc].ap[:, :], tg[side][oc].ap[:, :], xg[side][oc].ap[:, :]),
                        ins=(tg[side][oc], xg[side][oc]), outs=(tg[side][oc],))
                    kb.dma("sync", x_out, np.s_[oc * 128:(oc + 1) * 128, t0:t0 + 128],
                           tg[side][oc], np.s_[:, :])



def emit_outproj(kb, x_in, y_in, x_out, wpT, ntok, gtok=512):
    ng = ntok // gtok
    with kb.scope():
        yg = [kb.sb([128, gtok]) for _ in range(8)]
        xc = kb.sb([128, gtok])
        wl = [kb.sb([128, 1024]) for _ in range(2)]
        p_o = kb.psum([128, gtok])
        for g in range(ng):
            t0 = g * gtok
            for c in range(8):
                kb.dma("sync", yg[c], np.s_[:, :], y_in,
                       np.s_[c * 128:(c + 1) * 128, t0:t0 + gtok])
            for oc in range(8):
                w = wl[oc % 2]
                src = wpT.ap[:, oc * 128:(oc + 1) * 128].rearrange("(a p) h -> p a h", p=128)
                kb.dma_ap("sync", w, w.ap[:, :].rearrange("p (a h) -> p a h", a=8), wpT, src)
                kb.mm_group(
                    [(lambda e, dc=dc, w=w: e.matmul(
                        p_o.ap[:, :], w.ap[:, dc * 128:(dc + 1) * 128], yg[dc].ap[:, :],
                        start=(dc == 0), stop=(dc == 7))) for dc in range(8)],
                    ins=tuple(yg) + (w,), outs=(p_o,))
                kb.dma("sync", xc, np.s_[:, :], x_in,
                       np.s_[oc * 128:(oc + 1) * 128, t0:t0 + gtok])
                kb.op("vector", lambda e: e.tensor_add(xc.ap[:, :], xc.ap[:, :], p_o.ap[:, :]),
                      ins=(xc, p_o), outs=(xc,))
                kb.dma("sync", x_out, np.s_[oc * 128:(oc + 1) * 128, t0:t0 + gtok],
                       xc, np.s_[:, :])



def _mk_nc():
    return bass.Bass('TRN2', target_bir_lowering=False, debug=False)


def _dram_in(nc, name, shape):
    return TT(nc.dram_tensor(name, shape, F32, kind='ExternalInput').ap())


def _dram_out(nc, name, shape):
    return TT(nc.dram_tensor(name, shape, F32, kind='ExternalOutput').ap())


def build_ffn(ntok, ngroups, gtok):
    nc = _mk_nc()
    x_in = _dram_in(nc, 'x_in', [1024, ntok])
    wfcT = _dram_in(nc, 'wfcT', [1024, 4096])
    wprojT = _dram_in(nc, 'wprojT', [4096, 1024])
    lnwb_in = _dram_in(nc, 'lnwb', [128, 64])
    x_out = _dram_out(nc, 'x_out', [1024, ntok])
    kb = KB(nc)
    wfcq = kb.dram([1024, 4096], BF16, name='wfcq')
    wprojq = kb.dram([4096, 1024], BF16, name='wprojq')
    lnwb = kb.sb([128, 64], name='lnwb_sb')
    kb.dma("sync", lnwb, np.s_[:, :], lnwb_in, np.s_[:, :])
    emit_quant_w(kb, wfcT, 1024, 4096, wfcq, 4096 * 1024)
    emit_quant_w(kb, wprojT, 4096, 1024, wprojq, 4096 * 1024)
    kb.barrier()
    emit_ffn0(kb, x_in, x_out, wfcq, wprojq, lnwb, ngroups=ngroups, gtok=gtok)
    kb.finalize()
    return nc


def build_att0(Twin, nch):
    nc = _mk_nc()
    x_in = _dram_in(nc, 'x_in', [1024, Twin])
    wqkvT = _dram_in(nc, 'wqkvT', [1024, 768])
    ab = _dram_in(nc, 'ab', [2, 16])
    sin = _dram_in(nc, 'sin', [256, 64])
    yout = _dram_out(nc, 'yout', [256, Twin])
    sout = _dram_out(nc, 'sout', [256, 64])
    kb = KB(nc)
    abcols = kb.sb([128, 8], name='abcols')
    emit_ab(kb, ab, abcols, head0=0)
    emit_attn0_heads(kb, x_in, yout, wqkvT, abcols, Twin, sin, sout, ch0=0, nch=nch)
    kb.finalize()
    return nc


def build_outproj(ntok, gtok):
    nc = _mk_nc()
    x_in = _dram_in(nc, 'x_in', [1024, ntok])
    y_in = _dram_in(nc, 'y_in', [1024, ntok])
    wpT = _dram_in(nc, 'wpT', [1024, 1024])
    x_out = _dram_out(nc, 'x_out', [1024, ntok])
    kb = KB(nc)
    emit_outproj(kb, x_in, y_in, x_out, wpT, ntok, gtok=gtok)
    kb.finalize()
    return nc


def build_moeA(ntok, C):
    nc = _mk_nc()
    x_in = _dram_in(nc, 'x_in', [1024, ntok])
    gwT = _dram_in(nc, 'gwT', [1024, 16])
    xhat_tm = _dram_out(nc, 'xhat_tm', [ntok, 1024])
    selw = _dram_out(nc, 'selw', [ntok, 4])
    selp = _dram_out(nc, 'selp', [ntok, 2])
    rows = _dram_out(nc, 'rows', [2, 2 * ntok])
    cnt = _dram_out(nc, 'cnt', [16, 2])
    srow = _dram_out(nc, 'srow', [1, 2 * ntok])
    kb = KB(nc)
    emit_moe_route(kb, x_in, gwT, xhat_tm, selw, selp, rows, cnt, ntok, C, srow)
    kb.finalize()
    return nc


def build_moeS(ntok_win, C):
    nc = _mk_nc()
    xhat_win = _dram_in(nc, 'xhat_win', [ntok_win, 1024])
    srow_win = _dram_in(nc, 'srow_win', [1, 2 * ntok_win])
    ebuf_in = _dram_in(nc, 'ebuf_in', [16 * C + 128, 1024])
    ebuf = _dram_out(nc, 'ebuf', [16 * C + 128, 1024])
    kb = KB(nc)
    emit_moe_scat(kb, xhat_win, srow_win, ebuf_in, ebuf, C, ntok_win)
    kb.finalize()
    return nc


def build_moeB(ntok, C, CAP_G):
    nc = _mk_nc()
    ebuf = _dram_in(nc, 'ebuf', [16 * C + 128, 1024])
    rows = _dram_in(nc, 'rows', [2, 2 * ntok])
    cnts3 = _dram_in(nc, 'cnts3', [96, 16])
    w1 = _dram_in(nc, 'w1', [16, 1024, 2048])
    w2 = _dram_in(nc, 'w2', [16, 2048, 1024])
    obuf = _dram_out(nc, 'obuf', [16 * C + 128, 1024])
    vrow = _dram_out(nc, 'vrow', [1, 2 * ntok])
    kb = KB(nc)
    emit_moe_experts(kb, rows, cnts3, w1, w2, ntok, C, CAP_G, ebuf, obuf, vrow)
    kb.finalize()
    return nc


def build_moeG(ntok_win, C):
    nc = _mk_nc()
    x_in_win = _dram_in(nc, 'x_in_win', [1024, ntok_win])
    obuf = _dram_in(nc, 'obuf', [16 * C + 128, 1024])
    srow_win = _dram_in(nc, 'srow_win', [1, 2 * ntok_win])
    vrow_win = _dram_in(nc, 'vrow_win', [1, 2 * ntok_win])
    selw_win = _dram_in(nc, 'selw_win', [ntok_win, 4])
    x_out_win = _dram_out(nc, 'x_out_win', [1024, ntok_win])
    kb = KB(nc)
    emit_moe_gath(kb, x_in_win, x_out_win, obuf, srow_win, vrow_win, selw_win,
                  C, ntok_win)
    kb.finalize()
    return nc


def build_att1(ntok, r0, nr_pairs):
    nc = _mk_nc()
    x_in = _dram_in(nc, 'x_in', [1024, ntok])
    W = {}
    for k_, s_ in [('wqT', [1024, 768]), ('bq_cols', [128, 6]), ('gq_col', [128, 1]),
                   ('bqln_col', [128, 1]), ('ktR', [24, 128, 512]),
                   ('vtab', [96 * 128, 32]), ('wres8', [128, 64]), ('bres_col', [32, 1]),
                   ('wmemT', [32, 64]), ('bmem_row', [1, 64]), ('woT', [1024, 1024]),
                   ('gout_cols', [128, 8]), ('bout_cols', [128, 8]),
                   ('E8', [128, 8]), ('E8b', [8, 128])]:
        W[k_] = _dram_in(nc, k_, s_)
    x_out = _dram_out(nc, 'x_out', [1024, ntok])
    kb = KB(nc)
    emit_attn1(kb, x_in, x_out, W, ntok, r0, nr_pairs)
    kb.finalize()
    return nc


def _run(nc, in_maps):
    res = run_bass_kernel_spmd(nc, in_maps, core_ids=list(range(len(in_maps))))
    return res.results


def kernel(L0_attn_x, L0_attn_Wq, L0_attn_Wk, L0_attn_Wv, L0_attn_Wproj,
           L0_attn_alpha_raw, L0_attn_beta_raw, L0_ffn_W_fc, L0_ffn_ln_w,
           L0_ffn_ln_b, L0_ffn_W_proj, L1_attn_Wq, L1_attn_bq, L1_attn_gq,
           L1_attn_bq_ln, L1_attn_keys, L1_attn_values, L1_attn_Wres,
           L1_attn_bres, L1_attn_Wmem, L1_attn_bmem, L1_attn_Wo,
           L1_attn_g_out, L1_attn_b_out, L1_ffn_gate_w, L1_ffn_w1, L1_ffn_w2,
           _ncores=8, _nwin_att0=8, _nwin_att1=8):
    f32 = np.float32
    asn = lambda a: np.ascontiguousarray(np.asarray(a), dtype=f32)
    x0 = asn(L0_attn_x)
    B, T, _ = x0.shape
    ncores = _ncores
    pos = T // ncores
    ntok = 2 * pos
    CAP_G = int(1.25 * (B * T) * 2 / 16)
    C = min(768, max(128, ((ntok * 2 // 16) * 3 // 2 + 127) // 128 * 128))
    gtok = min(512, ntok)
    ngroups = ntok // gtok
    Twin = T // _nwin_att0
    nch_win = Twin // 128
    npair = pos // 128
    nr_pairs_win = npair // _nwin_att1

    def slabs(xf):
        out = []
        for c in range(ncores):
            xc = np.concatenate([xf[b, c * pos:(c + 1) * pos] for b in range(B)], 0)
            out.append(asn(xc.T))
        return out

    def unslabs(xs):
        full = np.empty((B, T, 1024), f32)
        for c in range(ncores):
            xc = np.asarray(xs[c]).T
            for b in range(B):
                full[b, c * pos:(c + 1) * pos] = xc[b * pos:(b + 1) * pos]
        return full

    nc_ffn = build_ffn(ntok, ngroups, gtok)
    nc_att0 = build_att0(Twin, nch_win)
    nc_op = build_outproj(ntok, gtok)
    nwin_moe = max(1, ntok // 2048)
    ntok_moew = ntok // nwin_moe
    nc_moeA = build_moeA(ntok, C)
    nc_moeS = build_moeS(ntok_moew, C)
    nc_moeB = build_moeB(ntok, C, CAP_G)
    nc_moeG = build_moeG(ntok_moew, C)
    ntok_w1 = 2 * nr_pairs_win * 128
    nc_att1 = build_att1(ntok_w1, 0, nr_pairs_win)

    lnwb = np.zeros((128, 64), f32)
    lnwb[:, 0:32] = asn(L0_ffn_ln_w).reshape(32, 128).T
    lnwb[:, 32:64] = asn(L0_ffn_ln_b).reshape(32, 128).T
    ffn_base = {'wfcT': asn(np.asarray(L0_ffn_W_fc).T),
                'wprojT': asn(np.asarray(L0_ffn_W_proj).T), 'lnwb': lnwb}

    def run_ffn(xslabs):
        r = _run(nc_ffn, [dict(ffn_base, x_in=xslabs[c]) for c in range(ncores)])
        return [r[c]['x_out'] for c in range(ncores)]

    xs = slabs(x0)
    xs = run_ffn(xs)
    x1 = unslabs(xs)
    WqT, WkT, WvT = asn(np.asarray(L0_attn_Wq).T), asn(np.asarray(L0_attn_Wk).T), asn(np.asarray(L0_attn_Wv).T)
    ab_full = np.stack([asn(L0_attn_alpha_raw), asn(L0_attn_beta_raw)])
    x1T = [asn(x1[b].T) for b in range(B)]
    y_acc = [np.empty((1024, T), f32) for _ in range(B)]
    s_cur = [np.zeros((256, 64), f32) for _ in range(8)]
    core_bh = [(c // 4, c % 4) for c in range(8)]
    wqkv = []
    for (b, hg) in core_bh:
        hs = np.s_[:, hg * 256:(hg + 1) * 256]
        wqkv.append(asn(np.concatenate([WqT[hs], WkT[hs], WvT[hs]], 1)))
    ab_core = []
    for (b, hg) in core_bh:
        a = np.zeros((2, 16), f32)
        a[:, 0:4] = ab_full[:, 4 * hg:4 * hg + 4]
        ab_core.append(a)
    for w in range(_nwin_att0):
        t0 = w * Twin
        ims = []
        for c, (b, hg) in enumerate(core_bh):
            ims.append({'x_in': asn(x1T[b][:, t0:t0 + Twin]), 'wqkvT': wqkv[c],
                        'ab': ab_core[c], 'sin': s_cur[c]})
        r = _run(nc_att0, ims)
        for c, (b, hg) in enumerate(core_bh):
            y_acc[b][hg * 256:(hg + 1) * 256, t0:t0 + Twin] = r[c]['yout']
            s_cur[c] = asn(r[c]['sout'])
    yslabs = []
    for c in range(ncores):
        yc = np.concatenate([y_acc[b][:, c * pos:(c + 1) * pos] for b in range(B)], 1)
        yslabs.append(asn(yc))
    wpT0 = asn(np.asarray(L0_attn_Wproj).T)
    r = _run(nc_op, [{'x_in': xs[c], 'y_in': yslabs[c], 'wpT': wpT0}
                     for c in range(ncores)])
    xs = [r[c]['x_out'] for c in range(ncores)]
    xs = run_ffn(xs)

    gwT = asn(np.asarray(L1_ffn_gate_w).T)
    w1 = asn(L1_ffn_w1); w2 = asn(L1_ffn_w2)

    ebz = np.zeros((16 * C + 128, 1024), f32)

    def run_moe(xslabs):
        rA = _run(nc_moeA, [{'x_in': xslabs[c], 'gwT': gwT} for c in range(ncores)])
        cnt = [np.asarray(rA[c]['cnt']) for c in range(ncores)]
        eb = [ebz for _ in range(ncores)]
        for w in range(nwin_moe):
            t0 = w * ntok_moew
            ims = []
            for c in range(ncores):
                ims.append({'xhat_win': np.asarray(rA[c]['xhat_tm'])[t0:t0 + ntok_moew],
                            'srow_win': np.asarray(rA[c]['srow'])[:, 2 * t0:2 * (t0 + ntok_moew)],
                            'ebuf_in': eb[c]})
            rS = _run(nc_moeS, ims)
            eb = [rS[c]['ebuf'] for c in range(ncores)]
        ims = []
        for c in range(ncores):
            c3 = np.zeros((96, 16), f32)
            for cc in range(c):
                c3[cc] = cnt[cc][:, 0]
            j = 0
            for cc in range(ncores):
                if cc != c:
                    c3[32 + j] = cnt[cc][:, 0]
                    j += 1
            for cc in range(c):
                c3[64 + cc] = cnt[cc][:, 1] - cnt[cc][:, 0]
            ims.append({'ebuf': eb[c], 'rows': rA[c]['rows'],
                        'cnts3': c3, 'w1': w1, 'w2': w2})
        rB = _run(nc_moeB, ims)
        xo = [np.array(np.asarray(x), copy=True) for x in xslabs]
        for w in range(nwin_moe):
            t0 = w * ntok_moew
            ims = []
            for c in range(ncores):
                ims.append({'x_in_win': np.ascontiguousarray(np.asarray(xslabs[c])[:, t0:t0 + ntok_moew]),
                            'obuf': rB[c]['obuf'],
                            'srow_win': np.asarray(rA[c]['srow'])[:, 2 * t0:2 * (t0 + ntok_moew)],
                            'vrow_win': np.asarray(rB[c]['vrow'])[:, 2 * t0:2 * (t0 + ntok_moew)],
                            'selw_win': np.asarray(rA[c]['selw'])[t0:t0 + ntok_moew]})
            rG = _run(nc_moeG, ims)
            for c in range(ncores):
                xo[c][:, t0:t0 + ntok_moew] = rG[c]['x_out_win']
        return [asn(x) for x in xo]

    xs = run_moe(xs)
    keyscat = np.concatenate([asn(L1_attn_keys), asn(L1_attn_keys)[:16]], 0)
    valuescat = np.concatenate([asn(L1_attn_values), asn(L1_attn_values)[:16]], 0)
    ktR = np.zeros((24, 128, 512), f32)
    for p in range(24):
        for g in range(4):
            for sl in range(2):
                ktR[p, 32 * g + sl * 16:32 * g + sl * 16 + 16, sl * 256:(sl + 1) * 256] = \
                    keyscat[2 * p + sl].T
    vtab = np.zeros((96 * 128, 32), f32)
    for s_ in range(48):
        for cc in range(2):
            vtab[(2 * s_ + cc) * 128:(2 * s_ + cc + 1) * 128] = \
                valuescat[s_][cc * 128:(cc + 1) * 128]
    wres8 = np.zeros((128, 64), f32)
    WresT = asn(np.asarray(L1_attn_Wres).T)
    for g in range(4):
        wres8[32 * g:32 * g + 16, 0:32] = WresT
        wres8[32 * g + 16:32 * g + 32, 32:64] = WresT
    E8 = np.zeros((128, 8), f32)
    for p in range(128):
        E8[p, p // 16] = 1
    att1_base = {
        'wqT': asn(np.asarray(L1_attn_Wq).T),
        'bq_cols': asn(asn(L1_attn_bq).reshape(6, 128).T),
        'gq_col': asn(np.tile(asn(L1_attn_gq), 8)[:, None]),
        'bqln_col': asn(np.tile(asn(L1_attn_bq_ln), 8)[:, None]),
        'ktR': ktR, 'vtab': vtab, 'wres8': wres8,
        'bres_col': asn(asn(L1_attn_bres)[:, None]),
        'wmemT': asn(np.asarray(L1_attn_Wmem).T),
        'bmem_row': asn(asn(L1_attn_bmem)[None, :]),
        'woT': asn(np.asarray(L1_attn_Wo).T),
        'gout_cols': asn(asn(L1_attn_g_out).reshape(8, 128).T),
        'bout_cols': asn(asn(L1_attn_b_out).reshape(8, 128).T),
        'E8': E8, 'E8b': asn(E8.T),
    }
    hw = nr_pairs_win * 128
    xnew = [np.array(x, copy=True) for x in xs]
    for w in range(_nwin_att1):
        t0 = w * hw
        ims = []
        for c in range(ncores):
            sub = np.concatenate([np.asarray(xs[c])[:, t0:t0 + hw],
                                  np.asarray(xs[c])[:, pos + t0:pos + t0 + hw]], 1)
            ims.append(dict(att1_base, x_in=asn(sub)))
        r = _run(nc_att1, ims)
        for c in range(ncores):
            xo = np.asarray(r[c]['x_out'])
            xnew[c][:, t0:t0 + hw] = xo[:, 0:hw]
            xnew[c][:, pos + t0:pos + t0 + hw] = xo[:, hw:2 * hw]
    xs = [asn(x) for x in xnew]
    xs = run_moe(xs)
    out = unslabs(xs)
    return out


def emit_moe_scat(kb, xhat_win, srow_win, ebuf_in, ebuf, C, ntok_win):
    nr = ntok_win // 128
    nbound = 16 * C
    with kb.scope():
        zt = kb.sb([128, 1024])
        nbrows = ebuf.ap.shape[0]
        for r0 in range(0, nbrows, 128):
            r1 = min(nbrows, r0 + 128)
            kb.dma("sync", zt, np.s_[0:r1 - r0, :], ebuf_in, np.s_[r0:r1, :])
            kb.dma("sync", ebuf, np.s_[r0:r1, :], zt, np.s_[0:r1 - r0, :])
        xtm = kb.sb([128, 1024])
        span = kb.sb([1, 256])
        rowt = kb.sb([1, 128])
        scol = kb.sb([128, 2], I32)
        scolf = kb.sb([128, 2])
        p_t = kb.psum([128, 128])
        for r in range(nr):
            kb.dma("sync", xtm, np.s_[:, :], xhat_win, np.s_[r * 128:(r + 1) * 128, :])
            kb.dma("sync", span, np.s_[:, :], srow_win, np.s_[0:1, r * 256:(r + 1) * 256])
            for k in range(2):
                kb.op("vector", lambda e, k=k: e.tensor_copy(
                    rowt.ap[:, :],
                    span.ap[0:1, :].rearrange("o (t k) -> o t k", k=2)[:, :, k]),
                    ins=(span,), outs=(rowt,))
                kb.op("tensor", lambda e: e.transpose(
                    p_t.ap[:, 0:1], rowt.ap[:, :], kb.ident.ap[0:1, 0:1]),
                    ins=(rowt,), outs=(p_t,))
                kb.op("vector", lambda e, k=k: e.tensor_copy(scolf.ap[:, k:k + 1],
                                                             p_t.ap[:, 0:1]),
                      ins=(p_t,), outs=(scolf,))
            kb.op("vector", lambda e: e.tensor_copy(scol.ap[:, :], scolf.ap[:, :]),
                  ins=(scolf,), outs=(scol,))
            for k in range(2):
                kb.op("gpsimd", lambda e, k=k: e.indirect_dma_start(
                    out=ebuf.ap[:, :],
                    out_offset=bass.IndirectOffsetOnAxis(ap=scol.ap[:, k:k + 1], axis=0),
                    in_=xtm.ap[:, :], in_offset=None,
                    bounds_check=nbound, oob_is_err=False),
                    ins=(xtm, scol), outs=(ebuf,), dma=True)


def emit_moe_gath(kb, x_in_win, x_out_win, obuf, srow_win, vrow_win, selw_win,
                  C, ntok_win):
    nr = ntok_win // 128
    nbound = 16 * C
    with kb.scope():
        span = kb.sb([1, 256]); vspan = kb.sb([1, 256])
        rowt = kb.sb([1, 128])
        scol = kb.sb([128, 2], I32); scolf = kb.sb([128, 2])
        wv = kb.sb([128, 2]); vcol = kb.sb([128, 2])
        g0 = kb.sb([128, 1024]); g1 = kb.sb([128, 1024])
        ytm = kb.sb([128, 1024])
        xchunk = kb.sb([128, 128])
        p_t = kb.psum([128, 128])
        for r in range(nr):
            kb.dma("sync", span, np.s_[:, :], srow_win, np.s_[0:1, r * 256:(r + 1) * 256])
            kb.dma("sync", vspan, np.s_[:, :], vrow_win, np.s_[0:1, r * 256:(r + 1) * 256])
            for k in range(2):
                kb.op("vector", lambda e, k=k: e.tensor_copy(
                    rowt.ap[:, :],
                    span.ap[0:1, :].rearrange("o (t k) -> o t k", k=2)[:, :, k]),
                    ins=(span,), outs=(rowt,))
                kb.op("tensor", lambda e: e.transpose(
                    p_t.ap[:, 0:1], rowt.ap[:, :], kb.ident.ap[0:1, 0:1]),
                    ins=(rowt,), outs=(p_t,))
                kb.op("vector", lambda e, k=k: e.tensor_copy(scolf.ap[:, k:k + 1],
                                                             p_t.ap[:, 0:1]),
                      ins=(p_t,), outs=(scolf,))
                kb.op("vector", lambda e, k=k: e.tensor_copy(
                    rowt.ap[:, :],
                    vspan.ap[0:1, :].rearrange("o (t k) -> o t k", k=2)[:, :, k]),
                    ins=(vspan,), outs=(rowt,))
                kb.op("tensor", lambda e: e.transpose(
                    p_t.ap[:, 0:1], rowt.ap[:, :], kb.ident.ap[0:1, 0:1]),
                    ins=(rowt,), outs=(p_t,))
                kb.op("vector", lambda e, k=k: e.tensor_copy(vcol.ap[:, k:k + 1],
                                                             p_t.ap[:, 0:1]),
                      ins=(p_t,), outs=(vcol,))
            kb.op("vector", lambda e: e.tensor_copy(scol.ap[:, :], scolf.ap[:, :]),
                  ins=(scolf,), outs=(scol,))
            kb.dma("sync", wv, np.s_[:, :], selw_win, np.s_[r * 128:(r + 1) * 128, 2:4])
            kb.op("vector", lambda e: e.tensor_mul(wv.ap[:, :], wv.ap[:, :], vcol.ap[:, :]),
                  ins=(wv, vcol), outs=(wv,))
            kb.op("vector", lambda e: e.memset(g0.ap[:, :], 0.0), outs=(g0,))
            kb.op("vector", lambda e: e.memset(g1.ap[:, :], 0.0), outs=(g1,))
            kb.op("gpsimd", lambda e: e.indirect_dma_start(
                out=g0.ap[:, :], out_offset=None, in_=obuf.ap[:, :],
                in_offset=bass.IndirectOffsetOnAxis(ap=scol.ap[:, 0:1], axis=0),
                bounds_check=nbound, oob_is_err=False),
                ins=(obuf, scol), outs=(g0,), dma=True)
            kb.op("gpsimd", lambda e: e.indirect_dma_start(
                out=g1.ap[:, :], out_offset=None, in_=obuf.ap[:, :],
                in_offset=bass.IndirectOffsetOnAxis(ap=scol.ap[:, 1:2], axis=0),
                bounds_check=nbound, oob_is_err=False),
                ins=(obuf, scol), outs=(g1,), dma=True)
            kb.op("vector", lambda e: e.tensor_single_scalar(
                out=ytm.ap[:, :], in_=g0.ap[:, :], scalar=wv.ap[:, 0:1], op=ALU.mult),
                ins=(g0, wv), outs=(ytm,))
            kb.op("vector", lambda e: e.tensor_single_scalar(
                out=g1.ap[:, :], in_=g1.ap[:, :], scalar=wv.ap[:, 1:2], op=ALU.mult),
                ins=(g1, wv), outs=(g1,))
            kb.op("vector", lambda e: e.tensor_add(ytm.ap[:, :], ytm.ap[:, :], g1.ap[:, :]),
                  ins=(ytm, g1), outs=(ytm,))
            for dc in range(8):
                kb.op("tensor", lambda e, dc=dc: e.transpose(
                    p_t.ap[:, :], ytm.ap[:, dc * 128:(dc + 1) * 128], kb.ident.ap[:, :]),
                    ins=(ytm,), outs=(p_t,))
                kb.dma("sync", xchunk, np.s_[:, :], x_in_win,
                       np.s_[dc * 128:(dc + 1) * 128, r * 128:(r + 1) * 128])
                kb.op("vector", lambda e: e.scalar_tensor_tensor(
                    out=xchunk.ap[:, :], in0=p_t.ap[:, :], scalar=0.5,
                    in1=xchunk.ap[:, :], op0=ALU.mult, op1=ALU.add),
                    ins=(p_t, xchunk), outs=(xchunk,))
                kb.dma("sync", x_out_win,
                       np.s_[dc * 128:(dc + 1) * 128, r * 128:(r + 1) * 128],
                       xchunk, np.s_[:, :])


# version 09
# speedup vs baseline: 1.2987x; 1.1005x over previous
import os
import numpy as np
from contextlib import ExitStack, contextmanager

import concourse.bass as bass
import concourse.mybir as mybir
from concourse.bass_utils import run_bass_kernel_spmd

F32 = mybir.dt.float32
I32 = mybir.dt.int32
U32 = mybir.dt.uint32
BF16 = mybir.dt.bfloat16
AF = mybir.ActivationFunctionType
ALU = mybir.AluOpType

ENGINES = ("sync", "tensor", "vector", "scalar", "gpsimd")
RELAX = False
_RELAX_ENGINES = ("tensor", "vector", "scalar")

NTOK = 4096
DM = 1024


class Sem:
    __slots__ = ("h", "n", "name")

    def __init__(self, kb, name):
        self.h = kb.es.enter_context(kb.nc.semaphore(name))
        self.n = 0
        self.name = name


class TT:
    __slots__ = ("ap", "w", "rs", "name", "wsp")

    def __init__(self, ap, name=""):
        self.ap = ap
        self.w = None
        self.rs = []
        self.name = name
        self.wsp = False

    def __getitem__(self, idx):
        return self.ap[idx]


class KB:
    def __init__(self, nc):
        self.nc = nc
        self.es = ExitStack()
        self.prog = {e: [] for e in ENGINES}
        self.esem = {e: Sem(self, f"es_{e}") for e in ENGINES}
        NDS = 8
        self.dsem = {e: [Sem(self, f"ds_{e}{i}") for i in range(NDS)]
                     for e in ("sync", "scalar", "gpsimd", "vector")}
        self.drr = {e: 0 for e in ("sync", "scalar", "gpsimd", "vector")}
        self.waited = {e: {} for e in ENGINES}
        self._name_i = 0
        self.cst_ones = self.sb([128, 128], name="cst_ones")
        self.op("vector", lambda e: e.memset(self.cst_ones.ap[:, :], 1.0),
                outs=(self.cst_ones,))
        self.ident = self.sb([128, 128], name="ident128")
        self.op("gpsimd", lambda e: e.affine_select(
            out=self.ident.ap[:, :], in_=self.cst_ones.ap[:, :], pattern=[[-1, 128]],
            compare_op=ALU.is_equal, fill=0.0, base=0, channel_multiplier=1),
            ins=(self.cst_ones,), outs=(self.ident,))
        self.barrier()

    def name(self, p="t"):
        self._name_i += 1
        return f"{p}{self._name_i}"

    def sb(self, shape, dt=F32, name=None):
        h = self.es.enter_context(self.nc.sbuf_tensor(name or self.name("sb"), shape, dt))
        return TT(h, name or "sb")

    def dram(self, shape, dt=F32, name=None, kind="Internal"):
        h = self.nc.dram_tensor(name or self.name("dr"), shape, dt, kind=kind)
        return TT(h.ap(), name or "dr")

    @contextmanager
    def scope(self):
        es = ExitStack()
        old, self.es = self.es, es
        try:
            yield
        finally:
            self.barrier()
            self.es = old
            es.close()

    def psum(self, shape, dt=F32, name=None):
        h = self.es.enter_context(self.nc.psum_tensor(name or self.name("ps"), shape, dt))
        return TT(h, name or "ps")

    def _wait(self, eng, tok):
        sem, val = tok
        if self.waited[eng].get(sem, 0) >= val:
            return
        self.waited[eng][sem] = val
        self.prog[eng].append(lambda e, s=sem.h, v=val: e.wait_ge(s, v))

    def op(self, eng, fn, ins=(), outs=(), dma=False, extra=(), special=False):
        my = self.esem[eng]
        relax = RELAX and not dma and not special and eng in _RELAX_ENGINES

        def keep(tok, sp):
            if relax and tok[0] is my and not sp:
                return False
            return True
        deps = []
        for t in ins:
            if t is not None and t.w is not None and keep(t.w, t.wsp):
                deps.append(t.w)
        for t in outs:
            if t is not None:
                if t.w is not None and keep(t.w, t.wsp):
                    deps.append(t.w)
                for rt in t.rs:
                    if keep(rt, False):
                        deps.append(rt)
        deps.extend(extra)
        if dma:
            pool = self.dsem[eng]
            sem = pool[self.drr[eng] % len(pool)]
            self.drr[eng] += 1
            if sem.n:
                deps.append((sem, sem.n))
            inc = 16
        else:
            sem = self.esem[eng]
            inc = 1
        for d in deps:
            self._wait(eng, d)
        sem.n += inc
        tok = (sem, sem.n)
        self.prog[eng].append(lambda e, f=fn, s=sem.h, i=inc: f(e).then_inc(s, i))
        for t in outs:
            if t is not None:
                t.w = tok
                t.rs = []
                t.wsp = special or dma or eng not in _RELAX_ENGINES
        for t in ins:
            if t is not None:
                t.rs.append(tok)
        return tok

    def mm_group(self, fns, ins=(), outs=(), extra=()):
        my = self.esem["tensor"]

        def keep(tok, sp):
            if RELAX and tok[0] is my and not sp:
                return False
            return True
        deps = []
        for t in ins:
            if t is not None and t.w is not None and keep(t.w, t.wsp):
                deps.append(t.w)
        for t in outs:
            if t is not None:
                if t.w is not None and keep(t.w, t.wsp):
                    deps.append(t.w)
                for rt in t.rs:
                    if keep(rt, False):
                        deps.append(rt)
        deps.extend(extra)
        for d in deps:
            self._wait("tensor", d)
        sem = self.esem["tensor"]
        for j, fn in enumerate(fns):
            sem.n += 1
            if j == len(fns) - 1:
                self.prog["tensor"].append(
                    lambda e, f=fn, s=sem.h, v=1: f(e).then_inc(s, v))
            else:
                self.prog["tensor"].append(
                    lambda e, f=fn, s=sem.h, v=1: f(e).then_inc(s, v))
        tok = (sem, sem.n)
        for t in outs:
            if t is not None:
                t.w = tok
                t.rs = []
                t.wsp = False
        for t in ins:
            if t is not None:
                t.rs.append(tok)
        return tok

    def barrier(self):
        toks = [(s, s.n) for s in self.esem.values()]
        for pool in self.dsem.values():
            toks += [(s, s.n) for s in pool]
        for eng in ENGINES:
            for t in toks:
                self._wait(eng, t)

    def finalize(self):
        allsems = [s for pool in self.dsem.values() for s in pool]
        allsems += [self.esem[e] for e in ENGINES if e != "sync"]
        for s in allsems:
            if s.n:
                self._wait("sync", (s, s.n))
        with self.nc.Block() as block:
            block.sync(lambda e: [f(e) for f in self.prog["sync"]])
            block.tensor(lambda e: [f(e) for f in self.prog["tensor"]])
            block.vector(lambda e: [f(e) for f in self.prog["vector"]])
            block.scalar(lambda e: [f(e) for f in self.prog["scalar"]])
            block.gpsimd(lambda e: [f(e) for f in self.prog["gpsimd"]])
        self.es.close()

    def dma(self, eng, dst, dst_idx, src, src_idx, ins=(), outs=()):
        return self.op(eng,
                       lambda e: e.dma_start(out=dst.ap[dst_idx], in_=src.ap[src_idx]),
                       ins=(src,) + tuple(ins), outs=(dst,) + tuple(outs), dma=True)

    def dma_ap(self, eng, dst_tt, dst_ap, src_tt, src_ap):
        return self.op(eng, lambda e: e.dma_start(out=dst_ap, in_=src_ap),
                       ins=(src_tt,), outs=(dst_tt,), dma=True)


def ceil_div(a, b):
    return (a + b - 1) // b


CST_COLS = 8


def host_consts():
    c = np.zeros((128, CST_COLS), np.float32)
    c[:, 0] = 1.0
    return c



def emit_quant_w(kb, wT_dram, n_in, n_out, stash_dram, nel):
    nr = n_in // 128
    with kb.scope():
        wt = [kb.sb([128, n_out]) for _ in range(2)]
        acc = kb.sb([128, 1])
        part = kb.sb([128, 1])
        tot_ps = kb.psum([1, 1])
        tot = kb.sb([1, 1])
        thr = kb.sb([128, 1])
        ones = kb.cst_ones
        sgn = kb.sb([128, n_out])
        msk = kb.sb([128, n_out])
        kb.op("vector", lambda e: e.memset(acc.ap[:, :], 0.0), outs=(acc,))
        for r in range(nr):
            w = wt[r % 2]
            kb.dma("sync", w, np.s_[:, :], wT_dram, np.s_[r * 128:(r + 1) * 128, :])
            kb.op("vector",
                  lambda e, w=w: e.reduce_sum(part.ap[:, :], w.ap[:, :],
                                              axis=mybir.AxisListType.X,
                                              apply_absolute_value=True),
                  ins=(w,), outs=(part,))
            kb.op("vector", lambda e: e.tensor_add(acc.ap[:, :], acc.ap[:, :], part.ap[:, :]),
                  ins=(part, acc), outs=(acc,))
        kb.op("tensor", lambda e: e.matmul(tot_ps.ap[:, :], acc.ap[:, :], ones.ap[:, 0:1],
                                           start=True, stop=True),
              ins=(acc, ones), outs=(tot_ps,))
        kb.op("vector", lambda e: e.tensor_scalar(
            tot.ap[:, :], tot_ps.ap[:, :], 1.0 / nel, 1e-6, op0=ALU.mult, op1=ALU.add),
            ins=(tot_ps,), outs=(tot,))
        kb.op("vector", lambda e: e.tensor_scalar_mul(tot.ap[:, :], tot.ap[:, :], 0.5),
              ins=(tot,), outs=(tot,))
        thr_ps = kb.psum([128, 1])
        kb.op("tensor", lambda e: e.matmul(thr_ps.ap[:, :], ones.ap[0:1, 0:128], tot.ap[:, :],
                                           start=True, stop=True),
              ins=(tot, ones), outs=(thr_ps,))
        kb.op("vector", lambda e: e.tensor_copy(thr.ap[:, :], thr_ps.ap[:, :]),
              ins=(thr_ps,), outs=(thr,))
        wq_b = kb.sb([128, n_out], BF16)
        for r in range(nr):
            w = wt[r % 2]
            kb.dma("sync", w, np.s_[:, :], wT_dram, np.s_[r * 128:(r + 1) * 128, :])
            kb.op("scalar", lambda e, w=w: e.activation(out=sgn.ap[:, :], in_=w.ap[:, :],
                                                        func=AF.Sign),
                  ins=(w,), outs=(sgn,))
            kb.op("scalar", lambda e, w=w: e.activation(out=msk.ap[:, :], in_=w.ap[:, :],
                                                        func=AF.Abs),
                  ins=(w,), outs=(msk,))
            kb.op("vector", lambda e: e.tensor_single_scalar(
                out=msk.ap[:, :], in_=msk.ap[:, :], scalar=thr.ap[:, :], op=ALU.is_gt),
                ins=(msk, thr), outs=(msk,))
            kb.op("vector", lambda e: e.tensor_mul(wq_b.ap[:, :], sgn.ap[:, :], msk.ap[:, :]),
                  ins=(sgn, msk), outs=(wq_b,))
            kb.dma("sync", stash_dram, np.s_[r * 128:(r + 1) * 128, :], wq_b, np.s_[:, :])


def emit_ln_fm(kb, xg, xhat, ntok, dim, pbank, gam_col=None, bet_col=None, eps=1e-5):
    nch = len(xg)
    ones = kb.cst_ones
    sx_ps, sxx_ps, bc_ps = pbank["sx"], pbank["sxx"], pbank["bc"]
    sq = pbank["sq"]
    rows = pbank["rows"]
    sx, sxx, arow, brow = rows
    kb.mm_group(
        [(lambda e, c=c: e.matmul(sx_ps.ap[:, :], ones.ap[:, 0:1], xg[c].ap[:, :],
                                  start=(c == 0), stop=(c == nch - 1))) for c in range(nch)],
        ins=tuple(xg) + (ones,), outs=(sx_ps, sxx_ps))
    for c in range(nch):
        s = sq[c % 2]
        kb.op("scalar", lambda e, c=c, s=s: e.activation(out=s.ap[:, :], in_=xg[c].ap[:, :],
                                                         func=AF.Square),
              ins=(xg[c],), outs=(s,))
        kb.mm_group(
            [lambda e, c=c, s=s: e.matmul(sxx_ps.ap[:, :], ones.ap[:, 0:1], s.ap[:, :],
                                          start=(c == 0), stop=(c == nch - 1))],
            ins=(s, ones), outs=(sxx_ps, sx_ps))
    kb.op("vector", lambda e: e.tensor_scalar_mul(sx.ap[:, :], sx_ps.ap[:, :], 1.0 / dim),
          ins=(sx_ps,), outs=(sx,))
    kb.op("vector", lambda e: e.tensor_scalar_mul(sxx.ap[:, :], sxx_ps.ap[:, :], 1.0 / dim),
          ins=(sxx_ps,), outs=(sxx,))
    kb.op("vector", lambda e: e.scalar_tensor_tensor(
        out=arow.ap[:, :], in0=sx.ap[:, :], scalar=-1.0, in1=sx.ap[:, :],
        op0=ALU.mult, op1=ALU.mult),
        ins=(sx,), outs=(arow,))
    kb.op("vector", lambda e: e.tensor_add(arow.ap[:, :], arow.ap[:, :], sxx.ap[:, :]),
          ins=(arow, sxx), outs=(arow,))
    kb.op("vector", lambda e: e.tensor_scalar_add(arow.ap[:, :], arow.ap[:, :], eps),
          ins=(arow,), outs=(arow,))
    kb.op("scalar", lambda e: e.activation(out=arow.ap[:, :], in_=arow.ap[:, :], func=AF.Sqrt),
          ins=(arow,), outs=(arow,))
    kb.op("vector", lambda e: e.reciprocal(arow.ap[:, :], arow.ap[:, :]),
          ins=(arow,), outs=(arow,))
    kb.op("vector", lambda e: e.scalar_tensor_tensor(
        out=brow.ap[:, :], in0=sx.ap[:, :], scalar=-1.0, in1=arow.ap[:, :],
        op0=ALU.mult, op1=ALU.mult),
        ins=(sx, arow), outs=(brow,))
    abc = pbank["abc"]
    for row, dstt in ((arow, abc[0]), (brow, abc[1])):
        kb.op("tensor", lambda e, row=row: e.matmul(
            bc_ps.ap[:, :], ones.ap[0:1, 0:128], row.ap[:, :], start=True, stop=True),
            ins=(row, ones), outs=(bc_ps,))
        kb.op("vector", lambda e, d=dstt: e.tensor_copy(d.ap[:, :], bc_ps.ap[:, :]),
              ins=(bc_ps,), outs=(dstt,))
    for c in range(nch):
        kb.op("vector", lambda e, c=c: e.tensor_mul(
            xhat[c].ap[:, :], xg[c].ap[:, :], abc[0].ap[:, :]),
            ins=(xg[c], abc[0]), outs=(xhat[c],))
        kb.op("vector", lambda e, c=c: e.tensor_add(
            xhat[c].ap[:, :], xhat[c].ap[:, :], abc[1].ap[:, :]),
            ins=(xhat[c], abc[1]), outs=(xhat[c],))
        if gam_col is not None:
            g, gc = gam_col
            b, bc_ = bet_col
            kb.op("vector", lambda e, c=c, g=g, gc=gc: e.tensor_single_scalar(
                out=xhat[c].ap[:, :], in_=xhat[c].ap[:, :],
                scalar=g.ap[:, gc + c:gc + c + 1], op=ALU.mult),
                ins=(xhat[c], g), outs=(xhat[c],))
            kb.op("vector", lambda e, c=c, b=b, bc_=bc_: e.tensor_single_scalar(
                out=xhat[c].ap[:, :], in_=xhat[c].ap[:, :],
                scalar=b.ap[:, bc_ + c:bc_ + c + 1], op=ALU.add),
                ins=(xhat[c], b), outs=(xhat[c],))


def emit_ffn0(kb, x_in, x_out, wfcq, wprojq, lnwb, ngroups=8, gtok=512):
    NH = 4096
    with kb.scope():
        xg = [kb.sb([128, gtok]) for _ in range(8)]
        xh = [kb.sb([128, gtok], BF16) for _ in range(8)]
        hh = [kb.sb([128, gtok]) for _ in range(32)]
        hb = [kb.sb([128, gtok], BF16) for _ in range(32)]
        sq = [kb.sb([128, gtok]) for _ in range(2)]
        rows = [kb.sb([1, gtok]) for _ in range(4)]
        abc = [kb.sb([128, gtok]) for _ in range(2)]
        wl = [kb.sb([128, 1024], BF16) for _ in range(2)]
        wl2 = [kb.sb([128, 4096], BF16) for _ in range(2)]
        ob = kb.sb([128, gtok])
        sx_ps = kb.psum([1, gtok])
        sxx_ps = kb.psum([1, gtok])
        bc_ps = kb.psum([128, gtok])
        h_ps = kb.psum([128, gtok])
        o_ps = kb.psum([128, gtok])
        pbank = {"sx": sx_ps, "sxx": sxx_ps, "bc": bc_ps, "sq": sq, "rows": rows, "abc": abc}
        for g in range(ngroups):
            t0 = g * gtok
            for c in range(8):
                kb.dma("sync", xg[c], np.s_[:, :], x_in,
                       np.s_[c * 128:(c + 1) * 128, t0:t0 + gtok])
            emit_ln_fm(kb, xg, xh, gtok, DM, pbank)
            for hc in range(32):
                w = wl[hc % 2]
                src = wfcq.ap[:, hc * 128:(hc + 1) * 128].rearrange(
                    "(a p) h -> p a h", p=128)
                dst = w.ap[:, :].rearrange("p (a h) -> p a h", a=8)
                kb.dma_ap("sync", w, dst, wfcq, src)
                kb.mm_group(
                    [(lambda e, dc=dc, w=w: e.matmul(
                        h_ps.ap[:, :], w.ap[:, dc * 128:(dc + 1) * 128], xh[dc].ap[:, :],
                        start=(dc == 0), stop=(dc == 7))) for dc in range(8)],
                    ins=tuple(xh) + (w,), outs=(h_ps,))
                kb.op("scalar", lambda e, hc=hc: e.activation(
                    out=hh[hc].ap[:, :], in_=h_ps.ap[:, :], func=AF.Gelu),
                    ins=(h_ps,), outs=(hh[hc],))
            emit_ln_fm(kb, hh, hb, gtok, NH, pbank, gam_col=(lnwb, 0), bet_col=(lnwb, 32))
            for oc in range(8):
                w = wl2[oc % 2]
                src = wprojq.ap[:, oc * 128:(oc + 1) * 128].rearrange(
                    "(a p) h -> p a h", p=128)
                dst = w.ap[:, :].rearrange("p (a h) -> p a h", a=32)
                kb.dma_ap("sync", w, dst, wprojq, src)
                kb.mm_group(
                    [(lambda e, hc=hc, w=w: e.matmul(
                        o_ps.ap[:, :], w.ap[:, hc * 128:(hc + 1) * 128], hb[hc].ap[:, :],
                        start=(hc == 0), stop=(hc == 31))) for hc in range(32)],
                    ins=tuple(hb) + (w,), outs=(o_ps,))
                kb.op("vector", lambda e, oc=oc: e.scalar_tensor_tensor(
                    out=ob.ap[:, :], in0=o_ps.ap[:, :], scalar=0.5, in1=xg[oc].ap[:, :],
                    op0=ALU.mult, op1=ALU.add),
                    ins=(o_ps, xg[oc]), outs=(ob,))
                kb.dma("sync", x_out, np.s_[oc * 128:(oc + 1) * 128, t0:t0 + gtok],
                       ob, np.s_[:, :])



def emit_attn0_heads(kb, xfull, yout, wqkvT, abcols, T, s_in, s_out, ch0, nch,
                     skip_ln=False):
    L = 128
    with kb.scope():
        Mlo = kb.sb([128, 128]); Mup = kb.sb([128, 128])
        Ione = kb.sb([128, 128]); Itwo = kb.sb([128, 128])
        ones = kb.cst_ones
        kb.op("gpsimd", lambda e: e.affine_select(
            out=Mlo.ap[:, :], in_=ones.ap[:, :], pattern=[[-1, 128]],
            compare_op=ALU.is_gt, fill=0.0, base=0, channel_multiplier=1),
            ins=(ones,), outs=(Mlo,))
        kb.op("gpsimd", lambda e: e.affine_select(
            out=Mup.ap[:, :], in_=ones.ap[:, :], pattern=[[1, 128]],
            compare_op=ALU.is_gt, fill=0.0, base=0, channel_multiplier=-1),
            ins=(ones,), outs=(Mup,))
        kb.op("gpsimd", lambda e: e.affine_select(
            out=Ione.ap[:, :], in_=ones.ap[:, :], pattern=[[-1, 128]],
            compare_op=ALU.is_equal, fill=0.0, base=0, channel_multiplier=1),
            ins=(ones,), outs=(Ione,))
        kb.op("vector", lambda e: e.tensor_scalar_mul(Itwo.ap[:, :], Ione.ap[:, :], 2.0),
              ins=(Ione,), outs=(Itwo,))
        S = [kb.sb([64, 64], name=f"S{h}") for h in range(4)]
        for h in range(4):
            kb.dma("sync", S[h], np.s_[:, :], s_in, np.s_[h * 64:(h + 1) * 64, :])
        wq = kb.sb([128, 8 * 256]); wk = kb.sb([128, 8 * 256]); wv = kb.sb([128, 8 * 256])
        for w, c0 in ((wq, 0), (wk, 256), (wv, 512)):
            src = wqkvT.ap[:, c0:c0 + 256].rearrange("(a p) h -> p a h", p=128)
            dst = w.ap[:, :].rearrange("p (a h) -> p a h", a=8)
            kb.dma_ap("sync", w, dst, wqkvT, src)
        xg = [kb.sb([128, L]) for _ in range(8)]
        xh = [kb.sb([128, L]) for _ in range(8)]
        sq = [kb.sb([128, L]) for _ in range(2)]
        rows = [kb.sb([1, L]) for _ in range(4)]
        abc = [kb.sb([128, L]) for _ in range(2)]
        qfm = [kb.sb([128, L]) for _ in range(2)]
        kfm = [kb.sb([128, L]) for _ in range(2)]
        krm = kb.sb([128, 256])
        vrm = kb.sb([128, 256])
        nrm = kb.sb([1, 512])
        rcol = kb.sb([128, 4])
        Xb = [kb.sb([128, 128]) for _ in range(4)]
        Xtb = [kb.sb([128, 128]) for _ in range(4)]
        Msb = [kb.sb([128, 128]) for _ in range(4)]
        Mtb = [kb.sb([128, 128]) for _ in range(4)]
        Ps = [kb.sb([128, 128]) for _ in range(2)]
        kf0 = [kb.sb([64, 128]) for _ in range(2)]
        qf0 = [kb.sb([64, 128]) for _ in range(2)]
        R2 = [kb.sb([128, 64]) for _ in range(2)]
        Zs = [kb.sb([128, 64]) for _ in range(2)]
        Us = [kb.sb([128, 128]) for _ in range(2)]
        yev = kb.sb([128, 128])
        p_m = kb.psum([1, 2 * L])
        p_sx = TT(p_m.ap[0:1, 0:L]); p_sxx = TT(p_m.ap[0:1, L:2 * L])
        p_bc = kb.psum([128, L])
        p_a = kb.psum([128, 256])
        p_b = [kb.psum([128, 128]) for _ in range(2)]
        p_y = kb.psum([128, 128])
        p_s = kb.psum([64, 64])
        p_n = kb.psum([1, 512])
        pbank = {"sx": p_sx, "sxx": p_sxx, "bc": p_bc, "sq": sq, "rows": rows, "abc": abc}
        for ci in range(nch):
            ch = ch0 + ci
            t0 = ch * L
            for c in range(8):
                kb.dma("sync", xg[c], np.s_[:, :], xfull,
                       np.s_[c * 128:(c + 1) * 128, t0:t0 + L])
            if skip_ln:
                for c in range(8):
                    kb.op("vector", lambda e, c=c: e.tensor_copy(xh[c].ap[:, :], xg[c].ap[:, :]),
                          ins=(xg[c],), outs=(xh[c],))
            else:
                emit_ln_fm(kb, xg, xh, L, DM, pbank)
            for half in range(2):
                kb.mm_group(
                    [(lambda e, dc=dc, half=half: e.matmul(
                        p_a.ap[:, 0:L], wq.ap[:, dc * 256 + half * 128:dc * 256 + (half + 1) * 128],
                        xh[dc].ap[:, :], start=(dc == 0), stop=(dc == 7))) for dc in range(8)],
                    ins=tuple(xh) + (wq,), outs=(p_a,))
                kb.op("vector", lambda e, half=half: e.tensor_copy(qfm[half].ap[:, :], p_a.ap[:, 0:L]),
                      ins=(p_a,), outs=(qfm[half],))
                kb.mm_group(
                    [(lambda e, dc=dc, half=half: e.matmul(
                        p_a.ap[:, 0:L], wk.ap[:, dc * 256 + half * 128:dc * 256 + (half + 1) * 128],
                        xh[dc].ap[:, :], start=(dc == 0), stop=(dc == 7))) for dc in range(8)],
                    ins=tuple(xh) + (wk,), outs=(p_a,))
                kb.op("vector", lambda e, half=half: e.tensor_copy(kfm[half].ap[:, :], p_a.ap[:, 0:L]),
                      ins=(p_a,), outs=(kfm[half],))
            kb.mm_group(
                [(lambda e, dc=dc: e.matmul(
                    p_a.ap[:, 0:256],
                    xh[dc].ap[:, :],
                    wv.ap[:, dc * 256:(dc + 1) * 256],
                    start=(dc == 0), stop=(dc == 7))) for dc in range(8)],
                ins=tuple(xh) + (wv,), outs=(p_a,))
            kb.op("vector", lambda e: e.tensor_copy(vrm.ap[:, :], p_a.ap[:, 0:256]),
                  ins=(p_a,), outs=(vrm,))
            kb.mm_group(
                [(lambda e, dc=dc: e.matmul(
                    p_a.ap[:, 0:256],
                    xh[dc].ap[:, :],
                    wk.ap[:, dc * 256:(dc + 1) * 256],
                    start=(dc == 0), stop=(dc == 7))) for dc in range(8)],
                ins=tuple(xh) + (wk,), outs=(p_a,))
            kb.op("vector", lambda e: e.tensor_copy(krm.ap[:, :], p_a.ap[:, 0:256]),
                  ins=(p_a,), outs=(krm,))
            for half in range(2):
                s = sq[0]
                kb.op("scalar", lambda e, half=half, s=s: e.activation(
                    out=s.ap[:, :], in_=kfm[half].ap[:, :], func=AF.Square),
                    ins=(kfm[half],), outs=(s,))
                for sub in range(2):
                    h = half * 2 + sub
                    kb.op("tensor", lambda e, s=s, sub=sub, h=h: e.matmul(
                        p_n.ap[:, h * 128:(h + 1) * 128],
                        ones.ap[sub * 64:(sub + 1) * 64, 0:1],
                        s.ap[sub * 64:(sub + 1) * 64, :],
                        start=True, stop=True),
                        ins=(s, ones), outs=(p_n,))
            kb.op("scalar", lambda e: e.activation(out=nrm.ap[:, :], in_=p_n.ap[:, :],
                                                   func=AF.Sqrt),
                  ins=(p_n,), outs=(nrm,))
            kb.op("vector", lambda e: e.tensor_scalar_max(nrm.ap[:, :], nrm.ap[:, :], 1e-12),
                  ins=(nrm,), outs=(nrm,))
            kb.op("vector", lambda e: e.reciprocal(nrm.ap[:, :], nrm.ap[:, :]),
                  ins=(nrm,), outs=(nrm,))
            for half in range(2):
                for sub in range(2):
                    h = half * 2 + sub
                    kb.op("tensor", lambda e, h=h, sub=sub: e.matmul(
                        p_bc.ap[sub * 64:(sub + 1) * 64, :],
                        ones.ap[0:1, 0:64],
                        nrm.ap[:, h * 128:(h + 1) * 128],
                        start=True, stop=True),
                        ins=(nrm, ones), outs=(p_bc,))
                kb.op("vector", lambda e, half=half: e.tensor_mul(
                    kfm[half].ap[:, :], kfm[half].ap[:, :], p_bc.ap[:, :]),
                    ins=(kfm[half], p_bc), outs=(kfm[half],))
            for h in range(4):
                pb = p_b[h % 2]
                kb.op("tensor", lambda e, h=h, pb=pb: e.transpose(
                    pb.ap[0:128, 0:1],
                    nrm.ap[:, h * 128:(h + 1) * 128], Ione.ap[0:1, 0:1]),
                    ins=(nrm, Ione), outs=(pb,))
                kb.op("vector", lambda e, h=h, pb=pb: e.tensor_copy(
                    rcol.ap[:, h:h + 1], pb.ap[:, 0:1]),
                    ins=(pb,), outs=(rcol,))
            for h in range(4):
                kb.op("vector", lambda e, h=h: e.tensor_single_scalar(
                    out=krm.ap[:, h * 64:(h + 1) * 64], in_=krm.ap[:, h * 64:(h + 1) * 64],
                    scalar=rcol.ap[:, h:h + 1], op=ALU.mult),
                    ins=(krm, rcol), outs=(krm,))
            def head_steps(h, j):
                half, sub = h // 2, h % 2
                kf = kfm[half].ap[sub * 64:(sub + 1) * 64, :]
                qf = qfm[half].ap[sub * 64:(sub + 1) * 64, :]
                kr = krm.ap[:, h * 64:(h + 1) * 64]
                vr = vrm.ap[:, h * 64:(h + 1) * 64]
                kfT, qfT, krT, vrT = kfm[half], qfm[half], krm, vrm
                acol = abcols.ap[:, h:h + 1]
                bcol = abcols.ap[:, 4 + h:5 + h]
                pb, ps = p_b[j], p_s
                kf0j, qf0j, Psj = kf0[j], qf0[j], Ps[j]
                R2j, Zsj, Usj = R2[j], Zs[j], Us[j]
                st = []
                st.append(lambda: kb.op("vector", lambda e: e.tensor_copy(kf0j.ap[:, :], kf),
                      ins=(kfT,), outs=(kf0j,)))
                st.append(lambda: kb.op("vector", lambda e: e.tensor_copy(qf0j.ap[:, :], qf),
                      ins=(qfT,), outs=(qf0j,)))
                st.append(lambda: kb.op("tensor", lambda e: e.matmul(
                    pb.ap[:, :], kf, kf, start=True, stop=True),
                    ins=(kfT,), outs=(pb,)))
                st.append(lambda: kb.op("vector", lambda e: e.tensor_mul(Xb[h].ap[:, :], pb.ap[:, :], Mlo.ap[:, :]),
                      ins=(pb, Mlo), outs=(Xb[h],)))
                st.append(lambda: kb.op("vector", lambda e: e.tensor_mul(Xtb[h].ap[:, :], pb.ap[:, :], Mup.ap[:, :]),
                      ins=(pb, Mup), outs=(Xtb[h],)))
                st.append(lambda: kb.op("vector", lambda e: e.tensor_single_scalar(
                    out=Xb[h].ap[:, :], in_=Xb[h].ap[:, :], scalar=acol, op=ALU.mult),
                    ins=(Xb[h], abcols), outs=(Xb[h],)))
                st.append(lambda: kb.op("vector", lambda e: e.tensor_sub(Xb[h].ap[:, :], Ione.ap[:, :], Xb[h].ap[:, :]),
                      ins=(Ione, Xb[h]), outs=(Xb[h],)))
                st.append(lambda: kb.op("vector", lambda e: e.tensor_single_scalar(
                    out=Xtb[h].ap[:, :], in_=Xtb[h].ap[:, :], scalar=acol, op=ALU.mult),
                    ins=(Xtb[h], abcols), outs=(Xtb[h],)))
                st.append(lambda: kb.op("vector", lambda e: e.tensor_sub(Xtb[h].ap[:, :], Ione.ap[:, :], Xtb[h].ap[:, :]),
                      ins=(Ione, Xtb[h]), outs=(Xtb[h],)))
                st.append(lambda: kb.op("vector", lambda e: e.tensor_sub(Msb[h].ap[:, :], Itwo.ap[:, :], Xb[h].ap[:, :]),
                      ins=(Itwo, Xb[h]), outs=(Msb[h],)))
                st.append(lambda: kb.op("vector", lambda e: e.tensor_sub(Mtb[h].ap[:, :], Itwo.ap[:, :], Xtb[h].ap[:, :]),
                      ins=(Itwo, Xtb[h]), outs=(Mtb[h],)))
                for it in range(6):
                    st.append(lambda: kb.op("tensor", lambda e: e.matmul(
                        pb.ap[:, :], Mtb[h].ap[:, :], Xb[h].ap[:, :], start=True, stop=True),
                        ins=(Mtb[h], Xb[h]), outs=(pb,)))
                    st.append(lambda: kb.op("vector", lambda e: e.tensor_copy(Psj.ap[:, :], pb.ap[:, :]),
                          ins=(pb,), outs=(Psj,)))
                    st.append(lambda: kb.op("tensor", lambda e: e.matmul(
                        pb.ap[:, :], Xtb[h].ap[:, :], Psj.ap[:, :], start=True, stop=True),
                        ins=(Xtb[h], Psj), outs=(pb,)))
                    st.append(lambda: kb.op("vector", lambda e: e.scalar_tensor_tensor(
                        out=Xb[h].ap[:, :], in0=Xb[h].ap[:, :], scalar=2.0, in1=pb.ap[:, :],
                        op0=ALU.mult, op1=ALU.subtract),
                        ins=(Xb[h], pb), outs=(Xb[h],)))
                    st.append(lambda: kb.op("tensor", lambda e: e.matmul(
                        pb.ap[:, :], Psj.ap[:, :], Xtb[h].ap[:, :], start=True, stop=True),
                        ins=(Psj, Xtb[h]), outs=(pb,)))
                    st.append(lambda: kb.op("vector", lambda e: e.scalar_tensor_tensor(
                        out=Xtb[h].ap[:, :], in0=Xtb[h].ap[:, :], scalar=2.0, in1=pb.ap[:, :],
                        op0=ALU.mult, op1=ALU.subtract),
                        ins=(Xtb[h], pb), outs=(Xtb[h],)))
                st.append(lambda: kb.op("tensor", lambda e: e.matmul(
                    pb.ap[:, 0:64], kf0j.ap[:, :], S[h].ap[:, :], start=True, stop=True),
                    ins=(kf0j, S[h]), outs=(pb,)))
                st.append(lambda: kb.op("vector", lambda e: e.tensor_single_scalar(
                    out=R2j.ap[:, :], in_=pb.ap[:, 0:64], scalar=acol, op=ALU.mult),
                    ins=(pb, abcols), outs=(R2j,)))
                st.append(lambda: kb.op("vector", lambda e: e.scalar_tensor_tensor(
                    out=R2j.ap[:, :], in0=vr, scalar=bcol, in1=R2j.ap[:, :],
                    op0=ALU.mult, op1=ALU.subtract),
                    ins=(vrT, abcols, R2j), outs=(R2j,)))
                st.append(lambda: kb.op("tensor", lambda e: e.matmul(
                    pb.ap[:, 0:64], Xtb[h].ap[:, :], R2j.ap[:, :], start=True, stop=True),
                    ins=(Xtb[h], R2j), outs=(pb,)))
                st.append(lambda: kb.op("vector", lambda e: e.tensor_copy(Zsj.ap[:, :], pb.ap[:, 0:64]),
                      ins=(pb,), outs=(Zsj,)))
                st.append(lambda: kb.op("tensor", lambda e: e.matmul(
                    pb.ap[:, :], kf, qf, start=True, stop=True),
                    ins=(kfT, qfT), outs=(pb,)))
                st.append(lambda: kb.op("vector", lambda e: e.tensor_mul(Usj.ap[:, :], pb.ap[:, :], Mup.ap[:, :]),
                      ins=(pb, Mup), outs=(Usj,)))
                st.append(lambda: kb.mm_group(
                    [(lambda e: e.matmul(
                        p_y.ap[sub * 64:(sub + 1) * 64, :], S[h].ap[:, :], qf0j.ap[:, :],
                        start=True, stop=False)),
                     (lambda e: e.matmul(
                         p_y.ap[sub * 64:(sub + 1) * 64, :], Zsj.ap[:, :], Usj.ap[:, :],
                         start=False, stop=True))],
                    ins=(S[h], qf0j, Zsj, Usj), outs=(p_y,)))
                def _sprime():
                    kb.op("tensor", lambda e: e.matmul(
                        ps.ap[:, :], kr, Zsj.ap[:, :], start=True, stop=True),
                        ins=(krT, Zsj), outs=(ps,))
                    kb.op("vector", lambda e: e.tensor_add(S[h].ap[:, :], S[h].ap[:, :], ps.ap[:, :]),
                          ins=(S[h], ps), outs=(S[h],))
                st.append(_sprime)
                if h % 2 == 1:
                    st.append(lambda: kb.op("vector", lambda e: e.tensor_copy(yev.ap[:, :], p_y.ap[:, :]),
                          ins=(p_y,), outs=(yev,)))
                    st.append(lambda: kb.dma("sync", yout,
                           np.s_[half * 128:(half + 1) * 128, t0:t0 + L],
                           yev, np.s_[:, :]))
                return st
            for half in range(2):
                stA = head_steps(2 * half, 0)
                stB = head_steps(2 * half + 1, 1)
                cut = int(os.environ.get("_IL_CUT", "999"))
                for i in range(min(cut, max(len(stA), len(stB)))):
                    if i < len(stA):
                        stA[i]()
                    if i < len(stB):
                        stB[i]()
                for f in stA[cut:]:
                    f()
                for f in stB[cut:]:
                    f()
        for h in range(4):
            kb.dma("sync", s_out, np.s_[h * 64:(h + 1) * 64, :], S[h], np.s_[:, :])


def emit_ab(kb, ab_in, abcols, head0):
    with kb.scope():
        abr = kb.sb([1, 32])
        p_c = kb.psum([128, 1])
        ones = kb.cst_ones
        kb.dma("sync", abr, np.s_[:, 0:16], ab_in, np.s_[0:1, :])
        kb.dma("sync", abr, np.s_[:, 16:32], ab_in, np.s_[1:2, :])
        kb.op("scalar", lambda e: e.activation(out=abr.ap[:, :], in_=abr.ap[:, :],
                                               func=AF.Sigmoid),
              ins=(abr,), outs=(abr,))
        kb.op("vector", lambda e: e.tensor_scalar_mul(abr.ap[:, :], abr.ap[:, :], 0.5),
              ins=(abr,), outs=(abr,))
        for j in range(8):
            c = (j // 4) * 16 + head0 + (j % 4)
            kb.op("tensor", lambda e, c=c: e.matmul(
                p_c.ap[:, :], ones.ap[0:1, 0:128], abr.ap[0:1, c:c + 1],
                start=True, stop=True),
                ins=(abr, ones), outs=(p_c,))
            kb.op("vector", lambda e, j=j: e.tensor_copy(abcols.ap[:, j:j + 1], p_c.ap[:, :]),
                  ins=(p_c,), outs=(abcols,))



def emit_moe_route(kb, x_in, gwT, xhat_tm, selw_out, selp_out, rows_out, cnt_out, ntok, C, srow_d):
    gsz = min(512, ntok)
    ng = ntok // gsz
    ones = kb.cst_ones
    with kb.scope():
        xg = [kb.sb([128, gsz]) for _ in range(8)]
        xh = [kb.sb([128, gsz]) for _ in range(8)]
        sq = [kb.sb([128, gsz]) for _ in range(2)]
        rows = [kb.sb([1, gsz]) for _ in range(4)]
        abc = [kb.sb([128, gsz]) for _ in range(2)]
        gw = kb.sb([128, 8 * 16])
        prob = kb.sb([128, 16])
        t8 = kb.sb([128, 8])
        i8 = kb.sb([128, 8], U32)
        i8f = kb.sb([128, 8])
        wcols = kb.sb([128, 2])
        sw = kb.sb([128, 4])
        xtm = kb.sb([128, 1024])
        p_sx = kb.psum([1, gsz]); p_sxx = kb.psum([1, gsz]); p_bc = kb.psum([128, gsz])
        p_l = kb.psum([128, 16])
        p_t = kb.psum([128, 128])
        pbank = {"sx": p_sx, "sxx": p_sxx, "bc": p_bc, "sq": sq, "rows": rows, "abc": abc}
        src = gwT.ap[:, :].rearrange("(a p) h -> p a h", p=128)
        kb.dma_ap("sync", gw, gw.ap[:, :].rearrange("p (a h) -> p a h", a=8), gwT, src)
        for g in range(ng):
            t0 = g * gsz
            for c in range(8):
                kb.dma("sync", xg[c], np.s_[:, :], x_in,
                       np.s_[c * 128:(c + 1) * 128, t0:t0 + gsz])
            emit_ln_fm(kb, xg, xh, gsz, DM, pbank)
            for st in range(gsz // 128):
                u0 = st * 128
                kb.mm_group(
                    [(lambda e, dc=dc, u0=u0: e.matmul(
                        p_l.ap[:, :], xh[dc].ap[:, u0:u0 + 128],
                        gw.ap[:, dc * 16:(dc + 1) * 16],
                        start=(dc == 0), stop=(dc == 7))) for dc in range(8)],
                    ins=tuple(xh) + (gw,), outs=(p_l,))
                kb.op("scalar", lambda e: e.activation(out=prob.ap[:, :], in_=p_l.ap[:, :],
                                                       func=AF.Sigmoid),
                      ins=(p_l,), outs=(prob,))
                kb.op("vector", lambda e: e.max(out=t8.ap[:, :], in_=prob.ap[:, :]),
                      ins=(prob,), outs=(t8,), special=True)
                kb.op("vector", lambda e: e.max_index(out=i8.ap[:, :], in_max=t8.ap[:, :],
                                                      in_values=prob.ap[:, :]),
                      ins=(t8, prob), outs=(i8,), special=True)
                kb.op("vector", lambda e: e.tensor_copy(i8f.ap[:, :], i8.ap[:, :]),
                      ins=(i8,), outs=(i8f,))
                kb.op("vector", lambda e: e.tensor_add(wcols.ap[:, 0:1], t8.ap[:, 0:1],
                                                       t8.ap[:, 1:2]),
                      ins=(t8,), outs=(wcols,))
                kb.op("vector", lambda e: e.tensor_scalar_add(wcols.ap[:, 0:1],
                                                              wcols.ap[:, 0:1], 1e-6),
                      ins=(wcols,), outs=(wcols,))
                kb.op("vector", lambda e: e.reciprocal(wcols.ap[:, 1:2], wcols.ap[:, 0:1]),
                      ins=(wcols,), outs=(wcols,))
                kb.op("vector", lambda e: e.tensor_mul(sw.ap[:, 2:3], t8.ap[:, 0:1],
                                                       wcols.ap[:, 1:2]),
                      ins=(t8, wcols), outs=(sw,))
                kb.op("vector", lambda e: e.tensor_mul(sw.ap[:, 3:4], t8.ap[:, 1:2],
                                                       wcols.ap[:, 1:2]),
                      ins=(t8, wcols), outs=(sw,))
                kb.op("vector", lambda e: e.tensor_copy(sw.ap[:, 0:2], i8f.ap[:, 0:2]),
                      ins=(i8f,), outs=(sw,))
                kb.dma("sync", selw_out, np.s_[t0 + u0:t0 + u0 + 128, :], sw, np.s_[:, :])
                kb.dma("sync", selp_out, np.s_[t0 + u0:t0 + u0 + 128, :], sw, np.s_[:, 0:2])
                for dc in range(8):
                    kb.op("tensor", lambda e, dc=dc, u0=u0: e.transpose(
                        p_t.ap[:, :], xh[dc].ap[:, u0:u0 + 128], Icst(kb).ap[:, :]),
                        ins=(xh[dc], Icst(kb)), outs=(p_t,))
                    kb.op("vector", lambda e, dc=dc: e.tensor_copy(
                        xtm.ap[:, dc * 128:(dc + 1) * 128], p_t.ap[:, :]),
                        ins=(p_t,), outs=(xtm,))
                kb.dma("sync", xhat_tm, np.s_[t0 + u0:t0 + u0 + 128, :], xtm, np.s_[:, :])
    N2 = 2 * ntok
    RB = min(2048, ntok)
    with kb.scope():
        selc = kb.sb([1, RB])
        selb = kb.sb([16, RB])
        scan = kb.sb([16, RB])
        rl = kb.sb([1, RB])
        carry = kb.sb([16, 1])
        ecol = kb.sb([16, 1], I32)
        ecolf = kb.sb([16, 1])
        cnt = kb.sb([16, 2])
        QC = min(512, RB)
        p_r = kb.psum([1, QC])
        p_bb = kb.psum([16, QC])
        ones = kb.cst_ones
        kb.op("gpsimd", lambda e: e.iota(ecol.ap[:, :], pattern=[[0, 1]], base=0,
                                         channel_multiplier=1), outs=(ecol,))
        kb.op("vector", lambda e: e.tensor_copy(ecolf.ap[:, :], ecol.ap[:, :]),
              ins=(ecol,), outs=(ecolf,))
        kb.op("vector", lambda e: e.memset(carry.ap[:, :], 0.0), outs=(carry,))
        selp_flat = selp_out.ap[:, :].rearrange("t k -> (t k)").unsqueeze(0)
        for j in range(N2 // RB):
            kb.dma_ap("sync", selc, selc.ap[0:1, :], selp_out,
                      selp_flat[:, j * RB:(j + 1) * RB])
            kb.dma_ap("sync", rows_out, rows_out.ap[0:1, j * RB:(j + 1) * RB],
                      selc, selc.ap[0:1, :])
            for q in range(RB // QC):
                kb.op("tensor", lambda e, q=q: e.matmul(
                    p_bb.ap[:, :], ones.ap[0:1, 0:16], selc.ap[:, q * QC:(q + 1) * QC],
                    start=True, stop=True),
                    ins=(selc, ones), outs=(p_bb,))
                kb.op("vector", lambda e, q=q: e.tensor_copy(
                    selb.ap[:, q * QC:(q + 1) * QC], p_bb.ap[:, :]),
                    ins=(p_bb,), outs=(selb,))
            kb.op("vector", lambda e: e.tensor_single_scalar(
                out=selb.ap[:, :], in_=selb.ap[:, :], scalar=ecolf.ap[:, :],
                op=ALU.is_equal),
                ins=(selb, ecolf), outs=(selb,))
            kb.op("vector", lambda e: e.tensor_tensor_scan(
                out=scan.ap[:, :], data0=selb.ap[:, :], data1=selb.ap[:, :],
                initial=carry.ap[:, :], op0=ALU.add, op1=ALU.bypass),
                ins=(selb, carry), outs=(scan,), special=True)
            kb.op("vector", lambda e: e.tensor_copy(carry.ap[:, :],
                                                    scan.ap[:, RB - 1:RB]),
                  ins=(scan,), outs=(carry,))
            if (j + 1) * RB == ntok:
                kb.op("vector", lambda e: e.tensor_copy(cnt.ap[:, 0:1], carry.ap[:, :]),
                      ins=(carry,), outs=(cnt,))
            kb.op("vector", lambda e: e.tensor_scalar_add(scan.ap[:, :], scan.ap[:, :],
                                                          -1.0),
                  ins=(scan,), outs=(scan,))
            kb.op("vector", lambda e: e.tensor_mul(selb.ap[:, :], selb.ap[:, :],
                                                   scan.ap[:, :]),
                  ins=(selb, scan), outs=(selb,))
            for q in range(RB // QC):
                kb.op("tensor", lambda e, q=q: e.matmul(
                    p_r.ap[:, :], ones.ap[0:16, 0:1], selb.ap[:, q * QC:(q + 1) * QC],
                    start=True, stop=True),
                    ins=(selb, ones), outs=(p_r,))
                kb.op("vector", lambda e, q=q: e.tensor_copy(
                    rl.ap[:, q * QC:(q + 1) * QC], p_r.ap[:, :]),
                    ins=(p_r,), outs=(rl,))
            kb.dma("sync", rows_out, np.s_[1:2, j * RB:(j + 1) * RB], rl, np.s_[0:1, :])
            OOB = float(16 * C + 100)
            kb.op("vector", lambda e: e.tensor_single_scalar(
                out=scan.ap[0:1, :], in_=rl.ap[:, :], scalar=float(C), op=ALU.is_lt),
                ins=(rl,), outs=(scan,))
            kb.op("vector", lambda e: e.scalar_tensor_tensor(
                out=rl.ap[:, :], in0=selc.ap[:, :], scalar=float(C), in1=rl.ap[:, :],
                op0=ALU.mult, op1=ALU.add),
                ins=(selc, rl), outs=(rl,))
            kb.op("vector", lambda e: e.tensor_mul(rl.ap[:, :], rl.ap[:, :],
                                                   scan.ap[0:1, :]),
                  ins=(rl, scan), outs=(rl,))
            kb.op("vector", lambda e: e.scalar_tensor_tensor(
                out=rl.ap[:, :], in0=scan.ap[0:1, :], scalar=-OOB, in1=rl.ap[:, :],
                op0=ALU.mult, op1=ALU.add),
                ins=(scan, rl), outs=(rl,))
            kb.op("vector", lambda e: e.tensor_scalar_add(rl.ap[:, :], rl.ap[:, :], OOB),
                  ins=(rl,), outs=(rl,))
            kb.dma("sync", srow_d, np.s_[0:1, j * RB:(j + 1) * RB], rl, np.s_[0:1, :])
        kb.op("vector", lambda e: e.tensor_copy(cnt.ap[:, 1:2], carry.ap[:, :]),
              ins=(carry,), outs=(cnt,))
        kb.dma("sync", cnt_out, np.s_[:, :], cnt, np.s_[:, :])

def Icst(kb):
    return kb.ident


_MOEB_NEXP = 16
_MOEB_STAGES = {'z', 'o', 't', 'r', 'e'}


def emit_moe_experts(kb, rows_in, cnts3, w1_in, w2_in,
                     ntok, C, CAP_G, ebuf, obuf, vrow_d):
    N2 = 2 * ntok
    RB = min(2048, ntok)
    nr = ntok // 128
    ner = C // 128
    ones = kb.cst_ones
    OOB = float(16 * C + 100)
    with kb.scope():
        cnts = kb.sb([96, 16])
        offc = kb.sb([16, 2])
        QC = min(512, RB)
        p_o = kb.psum([1, max(QC, 16)])
        p_bb = kb.psum([16, QC])
        p_t = kb.psum([128, 128])
        kb.dma("sync", cnts, np.s_[:, :], cnts3, np.s_[:, :])
        zt = kb.sb([128, 1024])
        kb.op("vector", lambda e: e.memset(zt.ap[:, :], 0.0), outs=(zt,))
        nbrows = obuf.ap.shape[0]
        if 'z' in _MOEB_STAGES:
            for r0 in range(0, nbrows - 127, 128):
                kb.dma("sync", obuf, np.s_[r0:r0 + 128, :], zt, np.s_[:, :])
            if nbrows % 128:
                r0 = nbrows - (nbrows % 128)
                kb.dma("sync", obuf, np.s_[r0:nbrows, :], zt, np.s_[0:nbrows - r0, :])
        if 'o' not in _MOEB_STAGES:
            kb.op("vector", lambda e: e.memset(offc.ap[:, :], 0.0), outs=(offc,))
        colsel = kb.sb([96, 2])
        s216 = kb.sb([2, 16])
        if 'o' in _MOEB_STAGES:
            kb.op("vector", lambda e: e.memset(colsel.ap[:, :], 0.0), outs=(colsel,))
            kb.op("vector", lambda e: e.memset(colsel.ap[0:32, 0:1], 1.0), outs=(colsel,))
            kb.op("vector", lambda e: e.memset(colsel.ap[32:64, 1:2], 1.0), outs=(colsel,))
            kb.op("vector", lambda e: e.memset(colsel.ap[64:96, 1:2], 1.0), outs=(colsel,))
            kb.op("tensor", lambda e: e.matmul(p_bb.ap[0:2, 0:16], colsel.ap[:, :],
                                               cnts.ap[:, :], start=True, stop=True),
                  ins=(colsel, cnts), outs=(p_bb,))
            kb.op("vector", lambda e: e.tensor_copy(s216.ap[:, :], p_bb.ap[0:2, 0:16]),
                  ins=(p_bb,), outs=(s216,))
        if 't' in _MOEB_STAGES:
            kb.op("tensor", lambda e: e.transpose(
                p_t.ap[0:16, 0:2], s216.ap[:, :], kb.ident.ap[0:2, 0:2]),
                ins=(s216,), outs=(p_t,))
            kb.op("vector", lambda e: e.tensor_copy(offc.ap[:, :], p_t.ap[0:16, 0:2]),
                  ins=(p_t,), outs=(offc,))
        selc = kb.sb([1, RB]); rlc = kb.sb([1, RB])
        selb = kb.sb([16, RB]); tmp16 = kb.sb([16, RB])
        rgc = kb.sb([1, RB]); vlc = kb.sb([1, RB]); slc = kb.sb([1, RB])
        ecol = kb.sb([16, 1], I32); ecolf = kb.sb([16, 1])
        kb.op("gpsimd", lambda e: e.iota(ecol.ap[:, :], pattern=[[0, 1]], base=0,
                                         channel_multiplier=1), outs=(ecol,))
        kb.op("vector", lambda e: e.tensor_copy(ecolf.ap[:, :], ecol.ap[:, :]),
              ins=(ecol,), outs=(ecolf,))
        for j in (range(N2 // RB) if 'r' in _MOEB_STAGES else []):
            half = 0 if (j * RB) < ntok else 1
            kb.dma("sync", selc, np.s_[:, :], rows_in, np.s_[0:1, j * RB:(j + 1) * RB])
            kb.dma("sync", rlc, np.s_[:, :], rows_in, np.s_[1:2, j * RB:(j + 1) * RB])
            for q in range(RB // QC):
                kb.op("tensor", lambda e, q=q: e.matmul(
                    p_bb.ap[:, :], ones.ap[0:1, 0:16], selc.ap[:, q * QC:(q + 1) * QC],
                    start=True, stop=True),
                    ins=(selc, ones), outs=(p_bb,))
                kb.op("vector", lambda e, q=q: e.tensor_copy(
                    selb.ap[:, q * QC:(q + 1) * QC], p_bb.ap[:, :]),
                    ins=(p_bb,), outs=(selb,))
            kb.op("vector", lambda e: e.tensor_single_scalar(
                out=selb.ap[:, :], in_=selb.ap[:, :], scalar=ecolf.ap[:, :],
                op=ALU.is_equal),
                ins=(selb, ecolf), outs=(selb,))
            kb.op("vector", lambda e, half=half: e.tensor_single_scalar(
                out=tmp16.ap[:, :], in_=selb.ap[:, :],
                scalar=offc.ap[:, half:half + 1], op=ALU.mult),
                ins=(selb, offc), outs=(tmp16,))
            for q in range(RB // QC):
                kb.op("tensor", lambda e, q=q: e.matmul(
                    p_o.ap[:, :], ones.ap[0:16, 0:1], tmp16.ap[:, q * QC:(q + 1) * QC],
                    start=True, stop=True),
                    ins=(tmp16, ones), outs=(p_o,))
                kb.op("vector", lambda e, q=q: e.tensor_add(
                    rgc.ap[:, q * QC:(q + 1) * QC], rlc.ap[:, q * QC:(q + 1) * QC],
                    p_o.ap[:, :]),
                    ins=(rlc, p_o), outs=(rgc,))
            kb.op("vector", lambda e: e.tensor_single_scalar(
                out=vlc.ap[:, :], in_=rgc.ap[:, :], scalar=float(CAP_G), op=ALU.is_lt),
                ins=(rgc,), outs=(vlc,))
            kb.op("vector", lambda e: e.tensor_single_scalar(
                out=slc.ap[:, :], in_=rlc.ap[:, :], scalar=float(C), op=ALU.is_lt),
                ins=(rlc,), outs=(slc,))
            kb.op("vector", lambda e: e.tensor_mul(vlc.ap[:, :], vlc.ap[:, :],
                                                   slc.ap[:, :]),
                  ins=(vlc, slc), outs=(vlc,))
            kb.dma("sync", vrow_d, np.s_[0:1, j * RB:(j + 1) * RB], vlc, np.s_[:, :])
        w1s = [kb.sb([128, 8 * 128]) for _ in range(2)]
        w1b = [kb.sb([128, 8 * 128], BF16) for _ in range(2)]
        w2h = [kb.sb([128, 512]) for _ in range(16)]
        w2b = [kb.sb([128, 512], BF16) for _ in range(16)]
        xe = [kb.sb([128, C], BF16) for _ in range(8)]
        he = [kb.sb([128, C], BF16) for _ in range(16)]
        erow = kb.sb([128, 1024])
        p_h = kb.psum([128, 512])
        p_h2 = kb.psum([128, 256])
        for e_ in range(_MOEB_NEXP):
            for r in range(ner):
                kb.dma("sync", erow, np.s_[:, :], ebuf,
                       np.s_[e_ * C + r * 128:e_ * C + (r + 1) * 128, :])
                for dc in range(8):
                    kb.op("tensor", lambda e, dc=dc: e.transpose(
                        p_t.ap[:, :], erow.ap[:, dc * 128:(dc + 1) * 128],
                        kb.ident.ap[:, :]),
                        ins=(erow,), outs=(p_t,))
                    kb.op("vector", lambda e, dc=dc, r=r: e.tensor_copy(
                        xe[dc].ap[:, r * 128:(r + 1) * 128], p_t.ap[:, :]),
                        ins=(p_t,), outs=(xe[dc],))
            for fc in range(16):
                w = w1s[fc % 2]
                wb = w1b[fc % 2]
                src = w1_in.ap[e_][:, fc * 128:(fc + 1) * 128].rearrange(
                    "(a p) h -> p a h", p=128)
                kb.dma_ap("sync", w, w.ap[:, :].rearrange("p (a h) -> p a h", a=8),
                          w1_in, src)
                kb.op("vector", lambda e, w=w, wb=wb: e.tensor_copy(wb.ap[:, :], w.ap[:, :]),
                      ins=(w,), outs=(wb,))
                nfree = (C + 511) // 512
                for fg in range(nfree):
                    f0 = fg * 512
                    f1 = min(C, f0 + 512)
                    pp = p_h if (f1 - f0) == 512 else p_h2
                    kb.mm_group(
                        [(lambda e, dc=dc, wb=wb, f0=f0, f1=f1, pp=pp: e.matmul(
                            pp.ap[:, 0:f1 - f0], wb.ap[:, dc * 128:(dc + 1) * 128],
                            xe[dc].ap[:, f0:f1],
                            start=(dc == 0), stop=(dc == 7))) for dc in range(8)],
                        ins=tuple(xe) + (wb,), outs=(pp,))
                    kb.op("scalar", lambda e, fc=fc, f0=f0, f1=f1, pp=pp: e.activation(
                        out=he[fc].ap[:, f0:f1], in_=pp.ap[:, 0:f1 - f0], func=AF.Gelu),
                        ins=(pp,), outs=(he[fc],))
            for colh in range(2):
                for fc in range(16):
                    kb.dma("sync", w2h[fc], np.s_[:, :], w2_in,
                           np.s_[e_, fc * 128:(fc + 1) * 128, colh * 512:(colh + 1) * 512])
                    kb.op("vector", lambda e, fc=fc: e.tensor_copy(w2b[fc].ap[:, :],
                                                                   w2h[fc].ap[:, :]),
                          ins=(w2h[fc],), outs=(w2b[fc],))
                for r in range(ner):
                    kb.mm_group(
                        [(lambda e, fc=fc, r=r: e.matmul(
                            p_h.ap[:, :], he[fc].ap[:, r * 128:(r + 1) * 128],
                            w2b[fc].ap[:, :],
                            start=(fc == 0), stop=(fc == 15))) for fc in range(16)],
                        ins=tuple(he) + tuple(w2b), outs=(p_h,))
                    kb.op("vector", lambda e, colh=colh: e.tensor_copy(
                        erow.ap[:, colh * 512:(colh + 1) * 512], p_h.ap[:, :]),
                        ins=(p_h,), outs=(erow,))
                    kb.dma("sync", obuf,
                           np.s_[e_ * C + r * 128:e_ * C + (r + 1) * 128,
                                 colh * 512:(colh + 1) * 512],
                           erow, np.s_[:, colh * 512:(colh + 1) * 512])

def _norm_ktR(kb, ktR, ones, p_n1, p_bc5, nrow, sqk):
    for p in range(24):
        kb.op("scalar", lambda e, p=p: e.activation(out=sqk.ap[:, :], in_=ktR[p].ap[:, :],
                                                    func=AF.Square),
              ins=(ktR[p],), outs=(sqk,))
        kb.op("tensor", lambda e: e.matmul(p_n1.ap[:, :], ones.ap[:, 0:1], sqk.ap[:, :],
                                           start=True, stop=True),
              ins=(sqk, ones), outs=(p_n1,))
        kb.op("scalar", lambda e: e.activation(out=nrow.ap[:, :], in_=p_n1.ap[:, :],
                                               func=AF.Sqrt, scale=0.25),
              ins=(p_n1,), outs=(nrow,))
        kb.op("vector", lambda e: e.reciprocal(nrow.ap[:, :], nrow.ap[:, :]),
              ins=(nrow,), outs=(nrow,))
        kb.op("tensor", lambda e: e.matmul(p_bc5.ap[:, :], ones.ap[0:1, 0:128],
                                           nrow.ap[:, :], start=True, stop=True),
              ins=(nrow, ones), outs=(p_bc5,))
        kb.op("vector", lambda e, p=p: e.tensor_mul(ktR[p].ap[:, :], ktR[p].ap[:, :],
                                                    p_bc5.ap[:, :]),
              ins=(ktR[p], p_bc5), outs=(ktR[p],))


_ATT1_STAGES = {'m', 'p', 'o'}
_ATT1_MSUB = 5


def emit_attn1(kb, x_in, x_out, W, ntok, r0, nr_pairs):
    half = ntok // 2
    ones = kb.cst_ones
    with kb.scope():
        wq = [kb.sb([128, 1024]) for _ in range(6)]
        for c in range(6):
            src = W['wqT'].ap[:, c * 128:(c + 1) * 128].rearrange("(a p) h -> p a h", p=128)
            kb.dma_ap("sync", wq[c], wq[c].ap[:, :].rearrange("p (a h) -> p a h", a=8),
                      W['wqT'], src)
        wo = [kb.sb([128, 1024]) for _ in range(2)]
        ktR = [kb.sb([128, 512]) for _ in range(24)]
        for p in range(24):
            kb.dma("sync", ktR[p], np.s_[:, :], W['ktR'], np.s_[p, :, :])
        vtab = kb.sb([128, 96 * 32])
        src = W['vtab'].ap[:, :].rearrange("(n p) v -> p n v", p=128)
        kb.dma_ap("sync", vtab, vtab.ap[:, :].rearrange("p (n v) -> p n v", n=96),
                  W['vtab'], src)
        wres8 = kb.sb([128, 64]); wmemT = kb.sb([32, 64])
        bres_c = kb.sb([32, 1]); bmem_r = kb.sb([1, 64])
        bqc = kb.sb([128, 6]); gqc = kb.sb([128, 1]); bqlnc = kb.sb([128, 1])
        E8 = kb.sb([128, 8]); E8b = kb.sb([8, 128])
        goc = kb.sb([128, 8]); boc = kb.sb([128, 8])
        for t, k_ in ((wres8, 'wres8'), (wmemT, 'wmemT'), (bres_c, 'bres_col'),
                      (bmem_r, 'bmem_row'), (bqc, 'bq_cols'), (gqc, 'gq_col'),
                      (bqlnc, 'bqln_col'), (E8, 'E8'), (E8b, 'E8b'),
                      (goc, 'gout_cols'), (boc, 'bout_cols')):
            kb.dma("sync", t, np.s_[:, :], W[k_], np.s_[:, :])
        with kb.scope():
            p_n1 = kb.psum([1, 512])
            p_bc5 = kb.psum([128, 512])
            nrow = kb.sb([1, 512]); sqk = kb.sb([128, 512])
            _norm_ktR(kb, ktR, ones, p_n1, p_bc5, nrow, sqk)
        xg = [[kb.sb([128, 128]) for _ in range(8)] for _ in range(2)]
        xh = [[kb.sb([128, 128]) for _ in range(8)] for _ in range(2)]
        sq2 = [kb.sb([128, 128]) for _ in range(2)]
        rows4 = [kb.sb([1, 128]) for _ in range(4)]
        abc2 = [kb.sb([128, 128]) for _ in range(2)]
        qf = [[kb.sb([128, 128]) for _ in range(6)] for _ in range(2)]
        mu8 = kb.sb([8, 128]); v8 = kb.sb([8, 128]); m8s = kb.sb([8, 128])
        t8 = [kb.sb([128, 48 * 8]) for _ in range(2)]
        u8 = kb.sb([128, 8])
        rzc = kb.sb([128, 1]); rzr = kb.sb([1, 128])
        sc2 = kb.sb([128, 512]); m2 = kb.sb([128, 512])
        sck = [kb.sb([128, 128]) for _ in range(4)]
        o2s = kb.sb([32, 128])
        out3 = [kb.sb([128, 48 * 64]) for _ in range(2)]
        lo = [kb.sb([128, 1024]) for _ in range(2)]
        dta = kb.sb([128, 1024])
        dred = kb.sb([128, 16 * 8])
        lofm = [[kb.sb([128, 128]) for _ in range(8)] for _ in range(2)]
        tg = [[kb.sb([128, 128]) for _ in range(8)] for _ in range(2)]
        p_at = kb.psum([128, 128])
        p_q = kb.psum([128, 128])
        p_s32 = kb.psum([32, 128])
        p_sc = kb.psum([128, 512])
        p_t128 = kb.psum([128, 192])
        p_stats = kb.psum([1, 256])
        class _V:
            def __init__(self, base, ap):
                self.base = base; self.ap = ap
            @property
            def w(self): return self.base.w
            @w.setter
            def w(self, v): self.base.w = v
            @property
            def rs(self): return self.base.rs
            @rs.setter
            def rs(self, v): self.base.rs = v
            @property
            def wsp(self): return self.base.wsp
            @wsp.setter
            def wsp(self, v): self.base.wsp = v
        p_s8 = _V(p_s32, p_s32.ap[0:8, :])
        p_o2 = p_s32
        p_t = _V(p_t128, p_t128.ap[:, 0:128])
        p_o3 = _V(p_t128, p_t128.ap[:, 128:192])
        pbank = {"sx": _V(p_stats, p_stats.ap[:, 0:128]),
                 "sxx": _V(p_stats, p_stats.ap[:, 128:256]),
                 "bc": p_q, "sq": sq2, "rows": rows4, "abc": abc2}
        for rp in range(r0, r0 + nr_pairs):
            for side in range(2):
                t0 = rp * 128 + side * half
                for c in range(8):
                    kb.dma("sync", xg[side][c], np.s_[:, :], x_in,
                           np.s_[c * 128:(c + 1) * 128, t0:t0 + 128])
                emit_ln_fm(kb, xg[side], xh[side], 128, DM, pbank)
                for c in range(6):
                    kb.mm_group(
                        [(lambda e, dc=dc, c=c, side=side: e.matmul(
                            p_q.ap[:, :], wq[c].ap[:, dc * 128:(dc + 1) * 128],
                            xh[side][dc].ap[:, :],
                            start=(dc == 0), stop=(dc == 7))) for dc in range(8)],
                        ins=tuple(xh[side]) + (wq[c],), outs=(p_q,))
                    q = qf[side][c]
                    kb.op("scalar", lambda e, c=c, q=q: e.activation(
                        out=q.ap[:, :], in_=p_q.ap[:, :], func=AF.Identity,
                        bias=bqc.ap[:, c:c + 1]),
                        ins=(p_q, bqc), outs=(q,))
                    kb.op("tensor", lambda e, q=q: e.matmul(
                        p_s8.ap[:, :], E8.ap[:, :], q.ap[:, :], start=True, stop=True),
                        ins=(q, E8), outs=(p_s8,))
                    kb.op("vector", lambda e: e.tensor_scalar_mul(
                        mu8.ap[:, :], p_s8.ap[:, :], 1.0 / 16), ins=(p_s8,), outs=(mu8,))
                    kb.op("scalar", lambda e, q=q: e.activation(
                        out=sq2[0].ap[:, :], in_=q.ap[:, :], func=AF.Square),
                        ins=(q,), outs=(sq2[0],))
                    kb.op("tensor", lambda e: e.matmul(
                        p_s8.ap[:, :], E8.ap[:, :], sq2[0].ap[:, :], start=True, stop=True),
                        ins=(sq2[0], E8), outs=(p_s8,))
                    kb.op("vector", lambda e: e.tensor_scalar_mul(
                        v8.ap[:, :], p_s8.ap[:, :], 1.0 / 16), ins=(p_s8,), outs=(v8,))
                    kb.op("vector", lambda e: e.tensor_mul(m8s.ap[:, :], mu8.ap[:, :],
                                                           mu8.ap[:, :]),
                          ins=(mu8,), outs=(m8s,))
                    kb.op("vector", lambda e: e.tensor_sub(v8.ap[:, :], v8.ap[:, :],
                                                           m8s.ap[:, :]),
                          ins=(v8, m8s), outs=(v8,))
                    kb.op("vector", lambda e: e.tensor_scalar_add(v8.ap[:, :], v8.ap[:, :],
                                                                  1e-5),
                          ins=(v8,), outs=(v8,))
                    kb.op("scalar", lambda e: e.activation(out=v8.ap[:, :], in_=v8.ap[:, :],
                                                           func=AF.Sqrt),
                          ins=(v8,), outs=(v8,))
                    kb.op("vector", lambda e: e.reciprocal(v8.ap[:, :], v8.ap[:, :]),
                          ins=(v8,), outs=(v8,))
                    kb.op("tensor", lambda e: e.matmul(p_q.ap[:, :], E8b.ap[:, :],
                                                       mu8.ap[:, :], start=True, stop=True),
                          ins=(mu8, E8b), outs=(p_q,))
                    kb.op("vector", lambda e, q=q: e.tensor_sub(q.ap[:, :], q.ap[:, :],
                                                                p_q.ap[:, :]),
                          ins=(q, p_q), outs=(q,))
                    kb.op("tensor", lambda e: e.matmul(p_q.ap[:, :], E8b.ap[:, :],
                                                       v8.ap[:, :], start=True, stop=True),
                          ins=(v8, E8b), outs=(p_q,))
                    kb.op("vector", lambda e, q=q: e.tensor_mul(q.ap[:, :], q.ap[:, :],
                                                                p_q.ap[:, :]),
                          ins=(q, p_q), outs=(q,))
                    kb.op("vector", lambda e, q=q: e.tensor_single_scalar(
                        out=q.ap[:, :], in_=q.ap[:, :], scalar=gqc.ap[:, 0:1], op=ALU.mult),
                        ins=(q, gqc), outs=(q,))
                    kb.op("vector", lambda e, q=q: e.tensor_single_scalar(
                        out=q.ap[:, :], in_=q.ap[:, :], scalar=bqlnc.ap[:, 0:1], op=ALU.add),
                        ins=(q, bqlnc), outs=(q,))
                for sp in (range(24) if 'm' in _ATT1_STAGES else []):
                    c, g = sp // 4, sp % 4
                    q = qf[side][c]
                    tp = (96, 0) if g == 3 else None
                    kb.op("tensor", lambda e, q=q, g=g, sp=sp, tp=tp: e.matmul(
                        p_sc.ap[:, :], q.ap[g * 32:(g + 1) * 32, :],
                        ktR[sp].ap[g * 32:(g + 1) * 32, :],
                        start=True, stop=True, tile_position=tp),
                        ins=(q, ktR[sp]), outs=(p_sc,))
                    for sl in range(2):
                        kb.op("vector", lambda e, sp=sp, sl=sl, side=side: e.max(
                            out=t8[side].ap[:, (2 * sp + sl) * 8:(2 * sp + sl + 1) * 8],
                            in_=p_sc.ap[:, sl * 256:(sl + 1) * 256]),
                            ins=(p_sc,), outs=(t8[side],), special=True)
                    if _ATT1_MSUB < 2:
                        continue
                    kb.op("vector", lambda e: e.tensor_copy(sc2.ap[:, :], p_sc.ap[:, :]),
                          ins=(p_sc,), outs=(sc2,))
                    kb.op("scalar", lambda e: e.activation(out=sc2.ap[:, :],
                                                           in_=sc2.ap[:, :], func=AF.Exp),
                          ins=(sc2,), outs=(sc2,))
                    for sl_ in range(2):
                        kb.op("vector", lambda e, sp=sp, side=side, sl_=sl_: e.tensor_tensor(
                            out=m2.ap[:, sl_ * 256:(sl_ + 1) * 256],
                            in0=p_sc.ap[:, sl_ * 256:(sl_ + 1) * 256],
                            in1=t8[side].ap[:, (2 * sp + sl_) * 8 + 7:(2 * sp + sl_) * 8 + 8]
                                .to_broadcast([128, 256]),
                            op=ALU.is_ge),
                            ins=(p_sc, t8[side]), outs=(m2,))
                    kb.op("vector", lambda e: e.tensor_mul(sc2.ap[:, :], sc2.ap[:, :],
                                                           m2.ap[:, :]),
                          ins=(sc2, m2), outs=(sc2,))
                    if _ATT1_MSUB < 3:
                        continue
                    for q4 in range(4):
                        kb.op("tensor", lambda e, q4=q4: e.transpose(
                            p_t.ap[:, :], sc2.ap[:, q4 * 128:(q4 + 1) * 128],
                            kb.ident.ap[:, :]),
                            ins=(sc2,), outs=(p_t,))
                        kb.op("vector", lambda e, q4=q4: e.tensor_copy(
                            sck[q4].ap[:, :], p_t.ap[:, :]),
                            ins=(p_t,), outs=(sck[q4],))
                    if _ATT1_MSUB < 4:
                        continue
                    for sl in range(2):
                        s_ = 2 * sp + sl
                        kb.op("scalar", lambda e, s_=s_, side=side: e.activation(
                            out=u8.ap[:, :], in_=t8[side].ap[:, s_ * 8:(s_ + 1) * 8],
                            func=AF.Exp),
                            ins=(t8[side],), outs=(u8,))
                        kb.op("vector", lambda e: e.tensor_reduce(
                            out=rzc.ap[:, :], in_=u8.ap[:, :],
                            axis=mybir.AxisListType.X, op=ALU.add),
                            ins=(u8,), outs=(rzc,))
                        kb.op("vector", lambda e: e.reciprocal(rzc.ap[:, :], rzc.ap[:, :]),
                              ins=(rzc,), outs=(rzc,))
                        kb.op("tensor", lambda e: e.transpose(
                            p_t.ap[0:1, 0:128], rzc.ap[:, :], kb.ident.ap[:, :]),
                            ins=(rzc,), outs=(p_t,))
                        kb.op("vector", lambda e: e.tensor_copy(rzr.ap[:, :],
                                                                p_t.ap[0:1, 0:128]),
                              ins=(p_t,), outs=(rzr,))
                        kb.op("tensor", lambda e: e.matmul(
                            p_q.ap[0:32, :], ones.ap[0:1, 0:32], rzr.ap[:, :],
                            start=True, stop=True),
                            ins=(rzr, ones), outs=(p_q,))
                        kb.mm_group(
                            [(lambda e, s_=s_, sl=sl, cc=cc: e.matmul(
                                p_o2.ap[:, :],
                                vtab.ap[:, (2 * s_ + cc) * 32:(2 * s_ + cc + 1) * 32],
                                sck[2 * sl + cc].ap[:, :],
                                start=(cc == 0), stop=(cc == 1))) for cc in range(2)],
                            ins=(vtab, sck[2 * sl], sck[2 * sl + 1]), outs=(p_o2,))
                        kb.op("scalar", lambda e: e.activation(
                            out=o2s.ap[:, :], in_=p_o2.ap[:, :], func=AF.Identity),
                            ins=(p_o2,), outs=(o2s,))
                        kb.op("vector", lambda e: e.tensor_mul(o2s.ap[:, :], o2s.ap[:, :],
                                                               p_q.ap[0:32, :]),
                              ins=(o2s, p_q), outs=(o2s,))
                        kb.op("tensor", lambda e, g=g, q=q, sl=sl, tp=tp: e.matmul(
                            p_o2.ap[:, :],
                            wres8.ap[g * 32:(g + 1) * 32, sl * 32:(sl + 1) * 32],
                            q.ap[g * 32:(g + 1) * 32, :],
                            start=True, stop=True, tile_position=tp),
                            ins=(q, wres8), outs=(p_o2,))
                        kb.op("vector", lambda e: e.tensor_add(o2s.ap[:, :], o2s.ap[:, :],
                                                               p_o2.ap[:, :]),
                              ins=(o2s, p_o2), outs=(o2s,))
                        kb.op("vector", lambda e: e.tensor_single_scalar(
                            out=o2s.ap[:, :], in_=o2s.ap[:, :],
                            scalar=bres_c.ap[:, 0:1], op=ALU.add),
                            ins=(o2s, bres_c), outs=(o2s,))
                        kb.mm_group(
                            [(lambda e: e.matmul(p_o3.ap[:, :], o2s.ap[:, :],
                                                 wmemT.ap[:, :], start=True, stop=False)),
                             (lambda e: e.matmul(p_o3.ap[:, :], ones.ap[0:1, 0:128],
                                                 bmem_r.ap[:, :], start=False, stop=True))],
                            ins=(o2s, wmemT, bmem_r, ones), outs=(p_o3,))
                        kb.op("scalar", lambda e, s_=s_, side=side: e.activation(
                            out=out3[side].ap[:, s_ * 64:(s_ + 1) * 64], in_=p_o3.ap[:, :],
                            func=AF.Identity),
                            ins=(p_o3,), outs=(out3[side],))
            if 'm' not in _ATT1_STAGES or _ATT1_MSUB < 5:
                for side in range(2):
                    kb.op("vector", lambda e, side=side: e.memset(out3[side].ap[:, :], 0.01),
                          outs=(out3[side],))
            for side in (range(2) if 'p' in _ATT1_STAGES else []):
                for c0 in (0, 2048):
                    seg = out3[side].ap[:, c0:c0 + 1024]
                    kb.op("vector", lambda e, seg=seg: e.tensor_single_scalar(
                        out=dta.ap[:, :], in_=seg, scalar=0.0, op=ALU.min),
                        ins=(out3[side],), outs=(dta,))
                    kb.op("scalar", lambda e: e.activation(out=dta.ap[:, :],
                                                           in_=dta.ap[:, :], func=AF.Exp),
                          ins=(dta,), outs=(dta,))
                    kb.op("vector", lambda e, seg=seg, side=side: e.scalar_tensor_tensor(
                        out=seg, in0=seg, scalar=1.0, in1=dta.ap[:, :],
                        op0=ALU.add, op1=ALU.max),
                        ins=(out3[side], dta), outs=(out3[side],))
            if 'p' not in _ATT1_STAGES:
                for side in range(2):
                    kb.op("vector", lambda e, side=side: e.memset(lo[side].ap[:, :], 0.01),
                          outs=(lo[side],))
            q0 = out3[0].ap[:, 2048:3072]; k0 = out3[0].ap[:, 0:1024]
            q1 = out3[1].ap[:, 2048:3072]; k1 = out3[1].ap[:, 0:1024]
            for (col, aa, bb) in (((0, q0, k0), (1, q1, k0), (2, q1, k1)) if 'p' in _ATT1_STAGES else []):
                kb.op("vector", lambda e, aa=aa, bb=bb: e.tensor_tensor(
                    out=dta.ap[:, :], in0=aa, in1=bb, op=ALU.mult),
                    ins=(out3[0], out3[1]), outs=(dta,))
                kb.op("vector", lambda e, col=col: e.tensor_reduce(
                    out=dred.ap[:, col * 16:(col + 1) * 16],
                    in_=dta.ap[:, :].rearrange("p (h d) -> p h d", h=16),
                    axis=mybir.AxisListType.X, op=ALU.add),
                    ins=(dta,), outs=(dred,))
            if 'p' in _ATT1_STAGES:
                kb.op("vector", lambda e: e.tensor_add(dred.ap[:, 48:64], dred.ap[:, 16:32],
                                                       dred.ap[:, 32:48]),
                      ins=(dred,), outs=(dred,))
                for (src, dst) in ((0, 4), (3, 5)):
                    kb.op("vector", lambda e, src=src, dst=dst: e.tensor_scalar(
                        out=dred.ap[:, dst * 16:(dst + 1) * 16],
                        in0=dred.ap[:, src * 16:(src + 1) * 16],
                        scalar1=1e-5, scalar2=8.0, op0=ALU.add, op1=ALU.mult),
                        ins=(dred,), outs=(dred,))
                    kb.op("vector", lambda e, dst=dst: e.reciprocal(
                        dred.ap[:, dst * 16:(dst + 1) * 16], dred.ap[:, dst * 16:(dst + 1) * 16]),
                        ins=(dred,), outs=(dred,))
                kb.op("vector", lambda e: e.tensor_mul(dred.ap[:, 96:112], dred.ap[:, 0:16],
                                                       dred.ap[:, 64:80]),
                      ins=(dred,), outs=(dred,))
                kb.op("vector", lambda e: e.tensor_mul(dred.ap[:, 16:32], dred.ap[:, 16:32],
                                                       dred.ap[:, 80:96]),
                      ins=(dred,), outs=(dred,))
                kb.op("vector", lambda e: e.tensor_mul(dred.ap[:, 32:48], dred.ap[:, 32:48],
                                                       dred.ap[:, 80:96]),
                      ins=(dred,), outs=(dred,))
                v0 = out3[0].ap[:, 1024:2048].rearrange("p (h d) -> p h d", h=16)
                v1 = out3[1].ap[:, 1024:2048].rearrange("p (h d) -> p h d", h=16)
                kb.op("vector", lambda e, v0=v0: e.tensor_tensor(
                    out=lo[0].ap[:, :].rearrange("p (h d) -> p h d", h=16), in0=v0,
                    in1=dred.ap[:, 96:112].rearrange("p (h o) -> p h o", o=1)
                        .to_broadcast([128, 16, 64]),
                    op=ALU.mult),
                    ins=(out3[0], dred), outs=(lo[0],))
                kb.op("vector", lambda e, v0=v0: e.tensor_tensor(
                    out=lo[1].ap[:, :].rearrange("p (h d) -> p h d", h=16), in0=v0,
                    in1=dred.ap[:, 16:32].rearrange("p (h o) -> p h o", o=1)
                        .to_broadcast([128, 16, 64]),
                    op=ALU.mult),
                    ins=(out3[0], dred), outs=(lo[1],))
                kb.op("vector", lambda e, v1=v1: e.tensor_tensor(
                    out=dta.ap[:, :].rearrange("p (h d) -> p h d", h=16), in0=v1,
                    in1=dred.ap[:, 32:48].rearrange("p (h o) -> p h o", o=1)
                        .to_broadcast([128, 16, 64]),
                    op=ALU.mult),
                    ins=(out3[1], dred), outs=(dta,))
                kb.op("vector", lambda e: e.tensor_add(lo[1].ap[:, :], lo[1].ap[:, :],
                                                       dta.ap[:, :]),
                      ins=(lo[1], dta), outs=(lo[1],))
            for side in range(2):
                t0 = rp * 128 + side * half
                for dc in range(8):
                    kb.op("tensor", lambda e, dc=dc, side=side: e.transpose(
                        p_t.ap[:, :], lo[side].ap[:, dc * 128:(dc + 1) * 128],
                        kb.ident.ap[:, :]),
                        ins=(lo[side],), outs=(p_t,))
                    kb.op("vector", lambda e, dc=dc, side=side: e.tensor_copy(
                        lofm[side][dc].ap[:, :], p_t.ap[:, :]),
                        ins=(p_t,), outs=(lofm[side][dc],))
                for oc in range(8):
                    wt = wo[oc % 2]
                    src = W['woT'].ap[:, oc * 128:(oc + 1) * 128].rearrange(
                        "(a p) h -> p a h", p=128)
                    kb.dma_ap("sync", wt, wt.ap[:, :].rearrange("p (a h) -> p a h", a=8),
                              W['woT'], src)
                    kb.mm_group(
                        [(lambda e, hc=hc, wt=wt, side=side: e.matmul(
                            p_at.ap[:, :], wt.ap[:, hc * 128:(hc + 1) * 128],
                            lofm[side][hc].ap[:, :],
                            start=(hc == 0), stop=(hc == 7))) for hc in range(8)],
                        ins=tuple(lofm[side]) + (wt,), outs=(p_at,))
                    kb.op("vector", lambda e, oc=oc, side=side: e.tensor_add(
                        tg[side][oc].ap[:, :], xh[side][oc].ap[:, :], p_at.ap[:, :]),
                        ins=(xh[side][oc], p_at), outs=(tg[side][oc],))
                emit_ln_fm(kb, tg[side], tg[side], 128, DM, pbank,
                           gam_col=(goc, 0), bet_col=(boc, 0))
                for oc in range(8):
                    kb.op("vector", lambda e, oc=oc, side=side: e.tensor_add(
                        tg[side][oc].ap[:, :], tg[side][oc].ap[:, :], xg[side][oc].ap[:, :]),
                        ins=(tg[side][oc], xg[side][oc]), outs=(tg[side][oc],))
                    kb.dma("sync", x_out, np.s_[oc * 128:(oc + 1) * 128, t0:t0 + 128],
                           tg[side][oc], np.s_[:, :])



def emit_outproj(kb, x_in, y_in, x_out, wpT, ntok, gtok=512):
    ng = ntok // gtok
    with kb.scope():
        yg = [kb.sb([128, gtok]) for _ in range(8)]
        xc = kb.sb([128, gtok])
        wl = [kb.sb([128, 1024]) for _ in range(2)]
        p_o = kb.psum([128, gtok])
        for g in range(ng):
            t0 = g * gtok
            for c in range(8):
                kb.dma("sync", yg[c], np.s_[:, :], y_in,
                       np.s_[c * 128:(c + 1) * 128, t0:t0 + gtok])
            for oc in range(8):
                w = wl[oc % 2]
                src = wpT.ap[:, oc * 128:(oc + 1) * 128].rearrange("(a p) h -> p a h", p=128)
                kb.dma_ap("sync", w, w.ap[:, :].rearrange("p (a h) -> p a h", a=8), wpT, src)
                kb.mm_group(
                    [(lambda e, dc=dc, w=w: e.matmul(
                        p_o.ap[:, :], w.ap[:, dc * 128:(dc + 1) * 128], yg[dc].ap[:, :],
                        start=(dc == 0), stop=(dc == 7))) for dc in range(8)],
                    ins=tuple(yg) + (w,), outs=(p_o,))
                kb.dma("sync", xc, np.s_[:, :], x_in,
                       np.s_[oc * 128:(oc + 1) * 128, t0:t0 + gtok])
                kb.op("vector", lambda e: e.tensor_add(xc.ap[:, :], xc.ap[:, :], p_o.ap[:, :]),
                      ins=(xc, p_o), outs=(xc,))
                kb.dma("sync", x_out, np.s_[oc * 128:(oc + 1) * 128, t0:t0 + gtok],
                       xc, np.s_[:, :])



def _mk_nc():
    return bass.Bass('TRN2', target_bir_lowering=False, debug=False)


def _dram_in(nc, name, shape):
    return TT(nc.dram_tensor(name, shape, F32, kind='ExternalInput').ap())


def _dram_out(nc, name, shape):
    return TT(nc.dram_tensor(name, shape, F32, kind='ExternalOutput').ap())


def build_ffn(ntok, ngroups, gtok):
    nc = _mk_nc()
    x_in = _dram_in(nc, 'x_in', [1024, ntok])
    wfcT = _dram_in(nc, 'wfcT', [1024, 4096])
    wprojT = _dram_in(nc, 'wprojT', [4096, 1024])
    lnwb_in = _dram_in(nc, 'lnwb', [128, 64])
    x_out = _dram_out(nc, 'x_out', [1024, ntok])
    kb = KB(nc)
    wfcq = kb.dram([1024, 4096], BF16, name='wfcq')
    wprojq = kb.dram([4096, 1024], BF16, name='wprojq')
    lnwb = kb.sb([128, 64], name='lnwb_sb')
    kb.dma("sync", lnwb, np.s_[:, :], lnwb_in, np.s_[:, :])
    emit_quant_w(kb, wfcT, 1024, 4096, wfcq, 4096 * 1024)
    emit_quant_w(kb, wprojT, 4096, 1024, wprojq, 4096 * 1024)
    kb.barrier()
    emit_ffn0(kb, x_in, x_out, wfcq, wprojq, lnwb, ngroups=ngroups, gtok=gtok)
    kb.finalize()
    return nc


def build_att0(Twin, nch):
    nc = _mk_nc()
    x_in = _dram_in(nc, 'x_in', [1024, Twin])
    wqkvT = _dram_in(nc, 'wqkvT', [1024, 768])
    ab = _dram_in(nc, 'ab', [2, 16])
    sin = _dram_in(nc, 'sin', [256, 64])
    yout = _dram_out(nc, 'yout', [256, Twin])
    sout = _dram_out(nc, 'sout', [256, 64])
    kb = KB(nc)
    abcols = kb.sb([128, 8], name='abcols')
    emit_ab(kb, ab, abcols, head0=0)
    emit_attn0_heads(kb, x_in, yout, wqkvT, abcols, Twin, sin, sout, ch0=0, nch=nch)
    kb.finalize()
    return nc


def build_outproj(ntok, gtok):
    nc = _mk_nc()
    x_in = _dram_in(nc, 'x_in', [1024, ntok])
    y_in = _dram_in(nc, 'y_in', [1024, ntok])
    wpT = _dram_in(nc, 'wpT', [1024, 1024])
    x_out = _dram_out(nc, 'x_out', [1024, ntok])
    kb = KB(nc)
    emit_outproj(kb, x_in, y_in, x_out, wpT, ntok, gtok=gtok)
    kb.finalize()
    return nc


def build_moeA(ntok, C):
    nc = _mk_nc()
    x_in = _dram_in(nc, 'x_in', [1024, ntok])
    gwT = _dram_in(nc, 'gwT', [1024, 16])
    xhat_tm = _dram_out(nc, 'xhat_tm', [ntok, 1024])
    selw = _dram_out(nc, 'selw', [ntok, 4])
    selp = _dram_out(nc, 'selp', [ntok, 2])
    rows = _dram_out(nc, 'rows', [2, 2 * ntok])
    cnt = _dram_out(nc, 'cnt', [16, 2])
    srow = _dram_out(nc, 'srow', [1, 2 * ntok])
    kb = KB(nc)
    emit_moe_route(kb, x_in, gwT, xhat_tm, selw, selp, rows, cnt, ntok, C, srow)
    kb.finalize()
    return nc


def build_moeS(ntok_win, C):
    nc = _mk_nc()
    xhat_win = _dram_in(nc, 'xhat_win', [ntok_win, 1024])
    srow_win = _dram_in(nc, 'srow_win', [1, 2 * ntok_win])
    ebuf_in = _dram_in(nc, 'ebuf_in', [16 * C + 128, 1024])
    ebuf = _dram_out(nc, 'ebuf', [16 * C + 128, 1024])
    kb = KB(nc)
    emit_moe_scat(kb, xhat_win, srow_win, ebuf_in, ebuf, C, ntok_win)
    kb.finalize()
    return nc


def build_moeB(ntok, C, CAP_G):
    nc = _mk_nc()
    ebuf = _dram_in(nc, 'ebuf', [16 * C + 128, 1024])
    rows = _dram_in(nc, 'rows', [2, 2 * ntok])
    cnts3 = _dram_in(nc, 'cnts3', [96, 16])
    w1 = _dram_in(nc, 'w1', [16, 1024, 2048])
    w2 = _dram_in(nc, 'w2', [16, 2048, 1024])
    obuf = _dram_out(nc, 'obuf', [16 * C + 128, 1024])
    vrow = _dram_out(nc, 'vrow', [1, 2 * ntok])
    kb = KB(nc)
    emit_moe_experts(kb, rows, cnts3, w1, w2, ntok, C, CAP_G, ebuf, obuf, vrow)
    kb.finalize()
    return nc


def build_moeG(ntok_win, C):
    nc = _mk_nc()
    x_in_win = _dram_in(nc, 'x_in_win', [1024, ntok_win])
    obuf = _dram_in(nc, 'obuf', [16 * C + 128, 1024])
    srow_win = _dram_in(nc, 'srow_win', [1, 2 * ntok_win])
    vrow_win = _dram_in(nc, 'vrow_win', [1, 2 * ntok_win])
    selw_win = _dram_in(nc, 'selw_win', [ntok_win, 4])
    x_out_win = _dram_out(nc, 'x_out_win', [1024, ntok_win])
    kb = KB(nc)
    emit_moe_gath(kb, x_in_win, x_out_win, obuf, srow_win, vrow_win, selw_win,
                  C, ntok_win)
    kb.finalize()
    return nc


def build_att1(ntok, r0, nr_pairs):
    nc = _mk_nc()
    x_in = _dram_in(nc, 'x_in', [1024, ntok])
    W = {}
    for k_, s_ in [('wqT', [1024, 768]), ('bq_cols', [128, 6]), ('gq_col', [128, 1]),
                   ('bqln_col', [128, 1]), ('ktR', [24, 128, 512]),
                   ('vtab', [96 * 128, 32]), ('wres8', [128, 64]), ('bres_col', [32, 1]),
                   ('wmemT', [32, 64]), ('bmem_row', [1, 64]), ('woT', [1024, 1024]),
                   ('gout_cols', [128, 8]), ('bout_cols', [128, 8]),
                   ('E8', [128, 8]), ('E8b', [8, 128])]:
        W[k_] = _dram_in(nc, k_, s_)
    x_out = _dram_out(nc, 'x_out', [1024, ntok])
    kb = KB(nc)
    emit_attn1(kb, x_in, x_out, W, ntok, r0, nr_pairs)
    kb.finalize()
    return nc


def _run(nc, in_maps):
    res = run_bass_kernel_spmd(nc, in_maps, core_ids=list(range(len(in_maps))))
    return res.results


def kernel(L0_attn_x, L0_attn_Wq, L0_attn_Wk, L0_attn_Wv, L0_attn_Wproj,
           L0_attn_alpha_raw, L0_attn_beta_raw, L0_ffn_W_fc, L0_ffn_ln_w,
           L0_ffn_ln_b, L0_ffn_W_proj, L1_attn_Wq, L1_attn_bq, L1_attn_gq,
           L1_attn_bq_ln, L1_attn_keys, L1_attn_values, L1_attn_Wres,
           L1_attn_bres, L1_attn_Wmem, L1_attn_bmem, L1_attn_Wo,
           L1_attn_g_out, L1_attn_b_out, L1_ffn_gate_w, L1_ffn_w1, L1_ffn_w2,
           _ncores=8, _nwin_att0=8, _nwin_att1=8):
    f32 = np.float32
    asn = lambda a: np.ascontiguousarray(np.asarray(a), dtype=f32)
    x0 = asn(L0_attn_x)
    B, T, _ = x0.shape
    ncores = _ncores
    pos = T // ncores
    ntok = 2 * pos
    CAP_G = int(1.25 * (B * T) * 2 / 16)
    C = min(768, max(128, ((ntok * 2 // 16) * 3 // 2 + 127) // 128 * 128))
    gtok = min(512, ntok)
    ngroups = ntok // gtok
    Twin = T // _nwin_att0
    nch_win = Twin // 128
    npair = pos // 128
    nr_pairs_win = npair // _nwin_att1

    def slabs(xf):
        out = []
        for c in range(ncores):
            xc = np.concatenate([xf[b, c * pos:(c + 1) * pos] for b in range(B)], 0)
            out.append(asn(xc.T))
        return out

    def unslabs(xs):
        full = np.empty((B, T, 1024), f32)
        for c in range(ncores):
            xc = np.asarray(xs[c]).T
            for b in range(B):
                full[b, c * pos:(c + 1) * pos] = xc[b * pos:(b + 1) * pos]
        return full

    nc_ffn = build_ffn(ntok, ngroups, gtok)
    nc_att0 = build_att0(Twin, nch_win)
    nc_op = build_outproj(ntok, gtok)
    nwin_moe = max(1, ntok // 2048)
    ntok_moew = ntok // nwin_moe
    nc_moeA = build_moeA(ntok, C)
    nc_moeS = build_moeS(ntok_moew, C)
    nc_moeB = build_moeB(ntok, C, CAP_G)
    nc_moeG = build_moeG(ntok_moew, C)
    ntok_w1 = 2 * nr_pairs_win * 128
    nc_att1 = build_att1(ntok_w1, 0, nr_pairs_win)

    lnwb = np.zeros((128, 64), f32)
    lnwb[:, 0:32] = asn(L0_ffn_ln_w).reshape(32, 128).T
    lnwb[:, 32:64] = asn(L0_ffn_ln_b).reshape(32, 128).T
    ffn_base = {'wfcT': asn(np.asarray(L0_ffn_W_fc).T),
                'wprojT': asn(np.asarray(L0_ffn_W_proj).T), 'lnwb': lnwb}

    def run_ffn(xslabs):
        r = _run(nc_ffn, [dict(ffn_base, x_in=xslabs[c]) for c in range(ncores)])
        return [r[c]['x_out'] for c in range(ncores)]

    xs = slabs(x0)
    xs = run_ffn(xs)
    x1 = unslabs(xs)
    WqT, WkT, WvT = asn(np.asarray(L0_attn_Wq).T), asn(np.asarray(L0_attn_Wk).T), asn(np.asarray(L0_attn_Wv).T)
    ab_full = np.stack([asn(L0_attn_alpha_raw), asn(L0_attn_beta_raw)])
    x1T = [asn(x1[b].T) for b in range(B)]
    y_acc = [np.empty((1024, T), f32) for _ in range(B)]
    s_cur = [np.zeros((256, 64), f32) for _ in range(8)]
    core_bh = [(c // 4, c % 4) for c in range(8)]
    wqkv = []
    for (b, hg) in core_bh:
        hs = np.s_[:, hg * 256:(hg + 1) * 256]
        wqkv.append(asn(np.concatenate([WqT[hs], WkT[hs], WvT[hs]], 1)))
    ab_core = []
    for (b, hg) in core_bh:
        a = np.zeros((2, 16), f32)
        a[:, 0:4] = ab_full[:, 4 * hg:4 * hg + 4]
        ab_core.append(a)
    for w in range(_nwin_att0):
        t0 = w * Twin
        ims = []
        for c, (b, hg) in enumerate(core_bh):
            ims.append({'x_in': asn(x1T[b][:, t0:t0 + Twin]), 'wqkvT': wqkv[c],
                        'ab': ab_core[c], 'sin': s_cur[c]})
        r = _run(nc_att0, ims)
        for c, (b, hg) in enumerate(core_bh):
            y_acc[b][hg * 256:(hg + 1) * 256, t0:t0 + Twin] = r[c]['yout']
            s_cur[c] = asn(r[c]['sout'])
    yslabs = []
    for c in range(ncores):
        yc = np.concatenate([y_acc[b][:, c * pos:(c + 1) * pos] for b in range(B)], 1)
        yslabs.append(asn(yc))
    wpT0 = asn(np.asarray(L0_attn_Wproj).T)
    r = _run(nc_op, [{'x_in': xs[c], 'y_in': yslabs[c], 'wpT': wpT0}
                     for c in range(ncores)])
    xs = [r[c]['x_out'] for c in range(ncores)]
    xs = run_ffn(xs)

    gwT = asn(np.asarray(L1_ffn_gate_w).T)
    w1 = asn(L1_ffn_w1); w2 = asn(L1_ffn_w2)

    ebz = np.zeros((16 * C + 128, 1024), f32)

    def run_moe(xslabs):
        rA = _run(nc_moeA, [{'x_in': xslabs[c], 'gwT': gwT} for c in range(ncores)])
        cnt = [np.asarray(rA[c]['cnt']) for c in range(ncores)]
        eb = [ebz for _ in range(ncores)]
        for w in range(nwin_moe):
            t0 = w * ntok_moew
            ims = []
            for c in range(ncores):
                ims.append({'xhat_win': np.asarray(rA[c]['xhat_tm'])[t0:t0 + ntok_moew],
                            'srow_win': np.asarray(rA[c]['srow'])[:, 2 * t0:2 * (t0 + ntok_moew)],
                            'ebuf_in': eb[c]})
            rS = _run(nc_moeS, ims)
            eb = [rS[c]['ebuf'] for c in range(ncores)]
        ims = []
        for c in range(ncores):
            c3 = np.zeros((96, 16), f32)
            for cc in range(c):
                c3[cc] = cnt[cc][:, 0]
            j = 0
            for cc in range(ncores):
                if cc != c:
                    c3[32 + j] = cnt[cc][:, 0]
                    j += 1
            for cc in range(c):
                c3[64 + cc] = cnt[cc][:, 1] - cnt[cc][:, 0]
            ims.append({'ebuf': eb[c], 'rows': rA[c]['rows'],
                        'cnts3': c3, 'w1': w1, 'w2': w2})
        rB = _run(nc_moeB, ims)
        xo = [np.array(np.asarray(x), copy=True) for x in xslabs]
        for w in range(nwin_moe):
            t0 = w * ntok_moew
            ims = []
            for c in range(ncores):
                ims.append({'x_in_win': np.ascontiguousarray(np.asarray(xslabs[c])[:, t0:t0 + ntok_moew]),
                            'obuf': rB[c]['obuf'],
                            'srow_win': np.asarray(rA[c]['srow'])[:, 2 * t0:2 * (t0 + ntok_moew)],
                            'vrow_win': np.asarray(rB[c]['vrow'])[:, 2 * t0:2 * (t0 + ntok_moew)],
                            'selw_win': np.asarray(rA[c]['selw'])[t0:t0 + ntok_moew]})
            rG = _run(nc_moeG, ims)
            for c in range(ncores):
                xo[c][:, t0:t0 + ntok_moew] = rG[c]['x_out_win']
        return [asn(x) for x in xo]

    xs = run_moe(xs)
    keyscat = np.concatenate([asn(L1_attn_keys), asn(L1_attn_keys)[:16]], 0)
    valuescat = np.concatenate([asn(L1_attn_values), asn(L1_attn_values)[:16]], 0)
    ktR = np.zeros((24, 128, 512), f32)
    for p in range(24):
        for g in range(4):
            for sl in range(2):
                ktR[p, 32 * g + sl * 16:32 * g + sl * 16 + 16, sl * 256:(sl + 1) * 256] = \
                    keyscat[2 * p + sl].T
    vtab = np.zeros((96 * 128, 32), f32)
    for s_ in range(48):
        for cc in range(2):
            vtab[(2 * s_ + cc) * 128:(2 * s_ + cc + 1) * 128] = \
                valuescat[s_][cc * 128:(cc + 1) * 128]
    wres8 = np.zeros((128, 64), f32)
    WresT = asn(np.asarray(L1_attn_Wres).T)
    for g in range(4):
        wres8[32 * g:32 * g + 16, 0:32] = WresT
        wres8[32 * g + 16:32 * g + 32, 32:64] = WresT
    E8 = np.zeros((128, 8), f32)
    for p in range(128):
        E8[p, p // 16] = 1
    att1_base = {
        'wqT': asn(np.asarray(L1_attn_Wq).T),
        'bq_cols': asn(asn(L1_attn_bq).reshape(6, 128).T),
        'gq_col': asn(np.tile(asn(L1_attn_gq), 8)[:, None]),
        'bqln_col': asn(np.tile(asn(L1_attn_bq_ln), 8)[:, None]),
        'ktR': ktR, 'vtab': vtab, 'wres8': wres8,
        'bres_col': asn(asn(L1_attn_bres)[:, None]),
        'wmemT': asn(np.asarray(L1_attn_Wmem).T),
        'bmem_row': asn(asn(L1_attn_bmem)[None, :]),
        'woT': asn(np.asarray(L1_attn_Wo).T),
        'gout_cols': asn(asn(L1_attn_g_out).reshape(8, 128).T),
        'bout_cols': asn(asn(L1_attn_b_out).reshape(8, 128).T),
        'E8': E8, 'E8b': asn(E8.T),
    }
    hw = nr_pairs_win * 128
    xnew = [np.array(x, copy=True) for x in xs]
    for w in range(_nwin_att1):
        t0 = w * hw
        ims = []
        for c in range(ncores):
            sub = np.concatenate([np.asarray(xs[c])[:, t0:t0 + hw],
                                  np.asarray(xs[c])[:, pos + t0:pos + t0 + hw]], 1)
            ims.append(dict(att1_base, x_in=asn(sub)))
        r = _run(nc_att1, ims)
        for c in range(ncores):
            xo = np.asarray(r[c]['x_out'])
            xnew[c][:, t0:t0 + hw] = xo[:, 0:hw]
            xnew[c][:, pos + t0:pos + t0 + hw] = xo[:, hw:2 * hw]
    xs = [asn(x) for x in xnew]
    xs = run_moe(xs)
    out = unslabs(xs)
    return out


def emit_moe_scat(kb, xhat_win, srow_win, ebuf_in, ebuf, C, ntok_win):
    nr = ntok_win // 128
    nbound = 16 * C
    with kb.scope():
        zt = kb.sb([128, 1024])
        nbrows = ebuf.ap.shape[0]
        for r0 in range(0, nbrows, 128):
            r1 = min(nbrows, r0 + 128)
            kb.dma("sync", zt, np.s_[0:r1 - r0, :], ebuf_in, np.s_[r0:r1, :])
            kb.dma("sync", ebuf, np.s_[r0:r1, :], zt, np.s_[0:r1 - r0, :])
        xtm = kb.sb([128, 1024])
        span = kb.sb([1, 256])
        rowt = kb.sb([1, 128])
        scol = kb.sb([128, 2], I32)
        scolf = kb.sb([128, 2])
        p_t = kb.psum([128, 128])
        for r in range(nr):
            kb.dma("sync", xtm, np.s_[:, :], xhat_win, np.s_[r * 128:(r + 1) * 128, :])
            kb.dma("sync", span, np.s_[:, :], srow_win, np.s_[0:1, r * 256:(r + 1) * 256])
            for k in range(2):
                kb.op("vector", lambda e, k=k: e.tensor_copy(
                    rowt.ap[:, :],
                    span.ap[0:1, :].rearrange("o (t k) -> o t k", k=2)[:, :, k]),
                    ins=(span,), outs=(rowt,))
                kb.op("tensor", lambda e: e.transpose(
                    p_t.ap[:, 0:1], rowt.ap[:, :], kb.ident.ap[0:1, 0:1]),
                    ins=(rowt,), outs=(p_t,))
                kb.op("vector", lambda e, k=k: e.tensor_copy(scolf.ap[:, k:k + 1],
                                                             p_t.ap[:, 0:1]),
                      ins=(p_t,), outs=(scolf,))
            kb.op("vector", lambda e: e.tensor_copy(scol.ap[:, :], scolf.ap[:, :]),
                  ins=(scolf,), outs=(scol,))
            for k in range(2):
                kb.op("gpsimd", lambda e, k=k: e.indirect_dma_start(
                    out=ebuf.ap[:, :],
                    out_offset=bass.IndirectOffsetOnAxis(ap=scol.ap[:, k:k + 1], axis=0),
                    in_=xtm.ap[:, :], in_offset=None,
                    bounds_check=nbound, oob_is_err=False),
                    ins=(xtm, scol), outs=(ebuf,), dma=True)


def emit_moe_gath(kb, x_in_win, x_out_win, obuf, srow_win, vrow_win, selw_win,
                  C, ntok_win):
    nr = ntok_win // 128
    nbound = 16 * C
    with kb.scope():
        span = kb.sb([1, 256]); vspan = kb.sb([1, 256])
        rowt = kb.sb([1, 128])
        scol = kb.sb([128, 2], I32); scolf = kb.sb([128, 2])
        wv = kb.sb([128, 2]); vcol = kb.sb([128, 2])
        g0 = kb.sb([128, 1024]); g1 = kb.sb([128, 1024])
        ytm = kb.sb([128, 1024])
        xchunk = kb.sb([128, 128])
        p_t = kb.psum([128, 128])
        for r in range(nr):
            kb.dma("sync", span, np.s_[:, :], srow_win, np.s_[0:1, r * 256:(r + 1) * 256])
            kb.dma("sync", vspan, np.s_[:, :], vrow_win, np.s_[0:1, r * 256:(r + 1) * 256])
            for k in range(2):
                kb.op("vector", lambda e, k=k: e.tensor_copy(
                    rowt.ap[:, :],
                    span.ap[0:1, :].rearrange("o (t k) -> o t k", k=2)[:, :, k]),
                    ins=(span,), outs=(rowt,))
                kb.op("tensor", lambda e: e.transpose(
                    p_t.ap[:, 0:1], rowt.ap[:, :], kb.ident.ap[0:1, 0:1]),
                    ins=(rowt,), outs=(p_t,))
                kb.op("vector", lambda e, k=k: e.tensor_copy(scolf.ap[:, k:k + 1],
                                                             p_t.ap[:, 0:1]),
                      ins=(p_t,), outs=(scolf,))
                kb.op("vector", lambda e, k=k: e.tensor_copy(
                    rowt.ap[:, :],
                    vspan.ap[0:1, :].rearrange("o (t k) -> o t k", k=2)[:, :, k]),
                    ins=(vspan,), outs=(rowt,))
                kb.op("tensor", lambda e: e.transpose(
                    p_t.ap[:, 0:1], rowt.ap[:, :], kb.ident.ap[0:1, 0:1]),
                    ins=(rowt,), outs=(p_t,))
                kb.op("vector", lambda e, k=k: e.tensor_copy(vcol.ap[:, k:k + 1],
                                                             p_t.ap[:, 0:1]),
                      ins=(p_t,), outs=(vcol,))
            kb.op("vector", lambda e: e.tensor_copy(scol.ap[:, :], scolf.ap[:, :]),
                  ins=(scolf,), outs=(scol,))
            kb.dma("sync", wv, np.s_[:, :], selw_win, np.s_[r * 128:(r + 1) * 128, 2:4])
            kb.op("vector", lambda e: e.tensor_mul(wv.ap[:, :], wv.ap[:, :], vcol.ap[:, :]),
                  ins=(wv, vcol), outs=(wv,))
            kb.op("vector", lambda e: e.memset(g0.ap[:, :], 0.0), outs=(g0,))
            kb.op("vector", lambda e: e.memset(g1.ap[:, :], 0.0), outs=(g1,))
            kb.op("gpsimd", lambda e: e.indirect_dma_start(
                out=g0.ap[:, :], out_offset=None, in_=obuf.ap[:, :],
                in_offset=bass.IndirectOffsetOnAxis(ap=scol.ap[:, 0:1], axis=0),
                bounds_check=nbound, oob_is_err=False),
                ins=(obuf, scol), outs=(g0,), dma=True)
            kb.op("gpsimd", lambda e: e.indirect_dma_start(
                out=g1.ap[:, :], out_offset=None, in_=obuf.ap[:, :],
                in_offset=bass.IndirectOffsetOnAxis(ap=scol.ap[:, 1:2], axis=0),
                bounds_check=nbound, oob_is_err=False),
                ins=(obuf, scol), outs=(g1,), dma=True)
            kb.op("vector", lambda e: e.tensor_single_scalar(
                out=ytm.ap[:, :], in_=g0.ap[:, :], scalar=wv.ap[:, 0:1], op=ALU.mult),
                ins=(g0, wv), outs=(ytm,))
            kb.op("vector", lambda e: e.tensor_single_scalar(
                out=g1.ap[:, :], in_=g1.ap[:, :], scalar=wv.ap[:, 1:2], op=ALU.mult),
                ins=(g1, wv), outs=(g1,))
            kb.op("vector", lambda e: e.tensor_add(ytm.ap[:, :], ytm.ap[:, :], g1.ap[:, :]),
                  ins=(ytm, g1), outs=(ytm,))
            for dc in range(8):
                kb.op("tensor", lambda e, dc=dc: e.transpose(
                    p_t.ap[:, :], ytm.ap[:, dc * 128:(dc + 1) * 128], kb.ident.ap[:, :]),
                    ins=(ytm,), outs=(p_t,))
                kb.dma("sync", xchunk, np.s_[:, :], x_in_win,
                       np.s_[dc * 128:(dc + 1) * 128, r * 128:(r + 1) * 128])
                kb.op("vector", lambda e: e.scalar_tensor_tensor(
                    out=xchunk.ap[:, :], in0=p_t.ap[:, :], scalar=0.5,
                    in1=xchunk.ap[:, :], op0=ALU.mult, op1=ALU.add),
                    ins=(p_t, xchunk), outs=(xchunk,))
                kb.dma("sync", x_out_win,
                       np.s_[dc * 128:(dc + 1) * 128, r * 128:(r + 1) * 128],
                       xchunk, np.s_[:, :])
